# version 09
# baseline (speedup 1.0000x reference)
import numpy as np
from contextlib import ExitStack

import concourse.bass as bass
import concourse.mybir as mybir
import concourse.bass_isa as bass_isa
from concourse.bass_utils import run_bass_kernel_spmd

F32 = mybir.dt.float32
BF16 = mybir.dt.bfloat16
AF = mybir.ActivationFunctionType
ALU = mybir.AluOpType
AXX = mybir.AxisListType.X
RADD = bass_isa.ReduceOp.add

B, T, C = 8, 2048, 1024
NC = 8
DH = 4096
NTT = T // 128

ENGS = ("t", "v", "s", "g", "y")
DMACH = {"g": "dG", "y": "dY", "s": "dS"}
NROT = 6


class Prog:

    def __init__(self, nc, sems):
        self.nc = nc
        self.sems = sems
        self.ops = {e: [] for e in ENGS}
        self.ncomp = {e: 0 for e in ENGS}
        self.need = {e: set() for e in ENGS}
        self.dmacnt = {}
        for q in DMACH.values():
            for i in range(NROT):
                self.dmacnt[q + str(i)] = 0
        self.dmarot = {q: 0 for q in DMACH.values()}
        self.dmagroup_open = {q: False for q in DMACH.values()}
        self.group_ends = {ch: [] for ch in self.dmacnt}
        self.pending = {e: [] for e in ENGS}
        self.bg = []

    def tok(self, e):
        return (e, self.ncomp[e])

    def _waits(self, e, after):
        w = []
        for t in list(after) + self.pending[e]:
            if t is None:
                continue
            kind, val = t
            if val <= 0:
                continue
            w.append((kind, val))
            if kind in ENGS:
                self.need[kind].add(val)
        self.pending[e] = []
        return w

    def op(self, e, fn, after=(), note=""):
        self.ops[e].append(dict(fn=fn, waits=self._waits(e, after), dma=None,
                                comp=self.ncomp[e], note=note))
        self.ncomp[e] += 1
        return (e, self.ncomp[e])

    def dma(self, e, out, in_, after=(), note="", new=False):
        q = DMACH[e]
        if new or not self.dmagroup_open[q]:
            cur = q + str(self.dmarot[q])
            if self.dmacnt[cur] > 0 and (not self.group_ends[cur]
                                         or self.group_ends[cur][-1] != self.dmacnt[cur]):
                self.group_ends[cur].append(self.dmacnt[cur])
            self.dmarot[q] = (self.dmarot[q] + 1) % NROT
            ch = q + str(self.dmarot[q])
            if self.dmacnt[ch] > 0:
                self.pending[e].append((ch, self.dmacnt[ch]))
            self.dmagroup_open[q] = True
        ch = q + str(self.dmarot[q])
        def fn(eng, out=out, in_=in_):
            return eng.dma_start(out=out, in_=in_)
        self.ops[e].append(dict(fn=fn, waits=self._waits(e, after), dma=ch,
                                comp=None, note=note))
        self.dmacnt[ch] += 16
        return (ch, self.dmacnt[ch])

    def dma_new(self, e):
        self.dmagroup_open[DMACH[e]] = False

    def barrier(self, engines=("t", "v", "s", "g", "y")):
        toks = {e: self.tok(e) for e in engines}
        for e in engines:
            for o in engines:
                if o != e:
                    self.pending[e].append(toks[o])

    def pump(self, n=1):
        for _ in range(n):
            if self.bg:
                self.bg.pop(0)()

    def pump_all(self):
        while self.bg:
            self.bg.pop(0)()

    def finalize(self, blk):
        import bisect
        for ch, tot in self.dmacnt.items():
            if tot > 0 and (not self.group_ends[ch] or self.group_ends[ch][-1] != tot):
                self.group_ends[ch].append(tot)
        for e in ENGS:
            for rec in self.ops[e]:
                w2 = []
                for kind, val in rec["waits"]:
                    if kind in self.group_ends:
                        ge = self.group_ends[kind]
                        val = ge[bisect.bisect_left(ge, val)]
                    w2.append((kind, val))
                rec["waits"] = w2
        incs = {e: {} for e in ENGS}
        for e in ENGS:
            need = sorted(self.need[e])
            if need and need[-1] > self.ncomp[e]:
                raise RuntimeError(f"wait beyond issued ops on {e}")
            full, prev = [], 0
            for p in need:
                while p - prev > 32:
                    prev += 32
                    full.append(prev)
                if p != prev:
                    full.append(p)
                    prev = p
            comp2idx = {}
            for i, rec in enumerate(self.ops[e]):
                if rec["comp"] is not None:
                    comp2idx[rec["comp"]] = i
            prev = 0
            for p in full:
                incs[e][comp2idx[p - 1]] = p - prev
                prev = p

        prog = self

        def make_body(e):
            def body(eng):
                seen = {}
                for i, rec in enumerate(prog.ops[e]):
                    for kind, val in rec["waits"]:
                        if seen.get(kind, 0) >= val:
                            continue
                        eng.wait_ge(prog.sems[kind], val)
                        seen[kind] = val
                    inst = rec["fn"](eng)
                    if rec["dma"] is not None:
                        inst.then_inc(prog.sems[rec["dma"]], 16)
                    amt = incs[e].get(i)
                    if amt:
                        inst.then_inc(prog.sems[e], amt)
            return body

        blk.tensor(make_body("t"))
        blk.vector(make_body("v"))
        blk.scalar(make_body("s"))
        blk.gpsimd(make_body("g"))
        blk.sync(make_body("y"))



def _colmajor(v):
    v = np.asarray(v, np.float32)
    n = v.size // 128
    return np.ascontiguousarray(v.reshape(n, 128).T)


def _bcast(v):
    v = np.asarray(v, np.float32)
    return np.ascontiguousarray(np.broadcast_to(v[None, :], (128, v.size)))


def _tr(w):
    return np.ascontiguousarray(np.asarray(w, np.float32).T)


def _jblk(w):
    wT = np.asarray(w, np.float32).T
    di, do = wT.shape
    return np.ascontiguousarray(
        wT.reshape(di // 128, 128, do // 128, 128).transpose(2, 1, 0, 3))


QW = [
    ("qwT", C, C), ("kwT", C, C), ("vwT", C, 2048), ("gwT", C, 2048),
    ("gk1wT", C, 16), ("gk2wT", 16, C), ("owT", 2048, C),
    ("f1T", C, DH), ("f2T", DH, C),
]
QWI = {nm: i for i, (nm, _, _) in enumerate(QW)}


def build(upto="all", tap=None):
    nc = bass.Bass("TRN2", target_bir_lowering=False)

    din = {}
    def dram_in(name, shape, dt=F32):
        din[name] = nc.dram_tensor(name, list(shape), dt, kind="ExternalInput").ap()
        return din[name]

    xT = dram_in("xT", [C, T])
    dram_in("idf", [128, 128]); dram_in("idb", [128, 128], BF16)
    dram_in("triu", [128, 128])
    dram_in("ones", [128, 128])
    for nm in ("tmk", "tmv", "tmr", "td", "tf", "s3", "bgk2c", "cb32"):
        dram_in(nm, [128, NC] if nm not in ("cb32",) else [128, 32])
    dram_in("cb", [128, 32])
    for nm in ("wkT", "wvT", "wrT", "woT"):
        dram_in(nm, [C, C])
    dram_in("w1R", [32, 128, 8, 128]); dram_in("w2R", [32, 128, 8, 128])
    dram_in("h1R", [32, 128, 32, 128]); dram_in("h2R", [32, 128, 32, 128])
    dram_in("w3R", [8, 128, 32, 128])
    for nm in ("s1", "s2", "sh1", "sh2"):
        dram_in(nm, [128, 32])
    for nm, di, do in QW:
        dram_in(nm, [di, do])
    for nm in ("nqB", "nkB", "nvB", "ngB", "ngk1B"):
        dram_in(nm, [128, C])
    dram_in("noB", [128, 2048])
    dram_in("ngk2c", [16, 1])
    dram_in("ngk2B", [128, 16])
    dram_in("cw", [128, 32, 3])

    out = nc.dram_tensor("out", [T, C], F32, kind="ExternalOutput").ap()
    tap_t = None
    if tap is not None:
        tap_t = nc.dram_tensor("tap", list(tap[0]), tap[1],
                               kind="ExternalOutput").ap()

    QSHAPE = {
        "qwT": [128, 1, 8, 1024], "kwT": [128, 1, 8, 1024],
        "vwT": [128, 2, 8, 1024], "gwT": [128, 2, 8, 1024],
        "owT": [128, 1, 16, 1024],
        "f1T": [32, 128, 8, 128], "f2T": [8, 128, 32, 128],
        "gk1wT": [C, 16], "gk2wT": [16, C],
    }
    qscr = {}
    for nm, di, do in QW:
        qscr[nm] = nc.dram_tensor("q_" + nm, QSHAPE[nm], BF16, kind="Internal").ap()

    with ExitStack() as es:
        sems = {}
        semnames = list(ENGS) + [q + str(i) for q in DMACH.values() for i in range(NROT)]
        for e in semnames:
            sems[e] = es.enter_context(nc.semaphore("sem_" + e))
        P = Prog(nc, sems)

        def sb(name, shape, dt=F32, stack=None):
            return (stack or es).enter_context(nc.sbuf_tensor(name, list(shape), dt))

        XA = sb("XA", [128, NC, T])
        IDF = sb("c_idf", [128, 128])
        IDB = sb("c_idb", [128, 128], BF16)
        TRIU = sb("c_triu", [128, 128])
        ONES = sb("c_ones", [128, 128])
        SVEC = sb("SVEC", [128, 64])
        SCL = sb("SCL", [128, 16])
        STAT = sb("STAT", [128, 64])
        PS = [es.enter_context(nc.psum_tensor(f"ps{b}", [128, 512], F32))
              for b in range(8)]
        PSB = [PS[b][:, :].bitcast(BF16) for b in range(8)]

        P.dma("y", IDF[:], din["idf"])
        d_idb = P.dma("y", IDB[:], din["idb"])
        d_triu = P.dma("y", TRIU[:], din["triu"])
        d_ones = P.dma("y", ONES[:], din["ones"])
        P.dma("y", SVEC[:, 0:8], din["tmk"])
        P.dma("y", SVEC[:, 8:16], din["tmv"])
        d_tm = P.dma("y", SVEC[:, 16:24], din["tmr"])
        d_td = P.dma("y", SVEC[:, 24:32], din["td"])
        d_tf = P.dma("y", SVEC[:, 32:40], din["tf"])
        P.dma("y", SVEC[:, 40:48], din["s3"])
        d_sv = P.dma("y", SVEC[:, 48:56], din["bgk2c"])
        d_x = None
        for c in range(NC):
            d_x = P.dma("y", XA[:, c, :], xT[c * 128:(c + 1) * 128, :],
                        new=(c == 0))
        s_ew1 = P.op("s", lambda s: s.activation(SVEC[:, 24:32], SVEC[:, 24:32], AF.Exp),
                     after=[d_td])
        s_ew = P.op("s", lambda s: s.activation(SVEC[:, 24:32], SVEC[:, 24:32], AF.Exp,
                                                scale=-1.0), after=[s_ew1])
        s_eu = P.op("s", lambda s: s.activation(SVEC[:, 32:40], SVEC[:, 32:40], AF.Exp),
                    after=[d_tf])

        pre_stack = ExitStack()
        pre = dict(
            ins=[sb(f"pre_in{i}", [128, 1024], F32, pre_stack) for i in range(3)],
            outs=[sb(f"pre_out{i}", [128, 1024], BF16, pre_stack) for i in range(2)],
            acc=sb("pre_acc", [128, 1], F32, pre_stack),
            reds=[sb(f"pre_red{i}", [128, 1], F32, pre_stack) for i in range(2)],
            rs=sb("pre_rs", [128, 1], F32, pre_stack),
            cmp=sb("pre_cmp", [128, 1024], F32, pre_stack))
        _emit_prepass(P, pre, din, qscr, SCL, ONES, PS, d_ones)

        ctx = dict(P=P, nc=nc, sb=sb, es=es, XA=XA, IDF=IDF, IDB=IDB, TRIU=TRIU,
                   SVEC=SVEC, SCL=SCL, STAT=STAT, PS=PS, PSB=PSB, din=din, qscr=qscr,
                   xready=d_x, svec_ready=(s_ew, s_eu, d_sv, d_tm))

        ctx["out"] = out
        order = ["l0attn", "ln1", "l0ffn", "ln2", "l1attn", "ln3", "l1ffn", "ln4"]
        stop = upto if upto in order or upto in ("l0ffnA", "l0ffnB") else "ln4"
        toks = stage_l0attn(ctx)
        done = upto == "l0attn"
        if not done:
            toks = stage_ln(ctx, 1, toks)
            P.pump_all()
            pre_stack.close()
            done = upto == "ln1"
        else:
            P.pump_all()
            pre_stack.close()
        if not done and upto in ("l0ffnA", "l0ffnB"):
            toksA = stage_l0ffn(ctx, toks, debug_stop=upto[-1])
            P.barrier()
            tk = [P.tok(e) for e in ("t", "v", "s", "g")]
            H0d = toksA["H0"]
            last = None
            for jj in range(32):
                last = P.dma("y", tap_t[jj * 128:(jj + 1) * 128, :],
                             H0d[:, jj * 1024:(jj + 1) * 1024], after=tk,
                             new=(jj == 0))
            P.op("y", lambda y: y.nop(), after=[last])
            done = True
        if not done:
            toks = stage_l0ffn(ctx, toks)
            done = upto == "l0ffn"
        if not done:
            toks = stage_ln(ctx, 2, toks)
            done = upto == "ln2"
        if not done:
            toks = stage_l1attn(ctx, toks)
            done = upto == "l1attn"
        if not done:
            toks = stage_ln(ctx, 3, toks)
            done = upto == "ln3"
        if not done:
            toks = stage_l1ffn(ctx, toks)
            done = upto == "l1ffn"
        if not done:
            toks = stage_ln(ctx, 4, toks)
            P.op("y", lambda y: y.nop(), after=[toks["out_dma"]])
            done = True
        elif tap_t is not None and upto not in ("l0ffnA", "l0ffnB"):
            _tap_xa(P, tap_t, XA)

        P.pump_all()
        blk = es.enter_context(nc.Block())
        P.finalize(blk)
    return nc


def _tap_xa(P, tap_t, XA):
    P.barrier()
    tk = [P.tok(e) for e in ("t", "v", "s", "g")]
    last = None
    for c in range(NC):
        last = P.dma("y", tap_t[c * 128:(c + 1) * 128, :], XA[:, c, :], after=tk,
                     new=(c == 0))
    P.op("y", lambda y: y.nop(), after=[last])


def _emit_prepass(P, pre, din, qscr, SCL, ONES, PS, d_ones):
    state = {"g": None, "v": None, "s": None, "tread": None,
             "ld": [None, None, None], "qd": [None, None, None],
             "redf": [None, None]}
    NR = 3
    for nm, di, do in QW:
        wi = QWI[nm]
        src, dst = din[nm], qscr[nm]
        tiles = []
        for r0 in range(0, di, 128):
            rr = min(128, di - r0)
            for c0 in range(0, do, 1024):
                tiles.append((r0, rr, c0, min(1024, do - c0)))
        inv_n = 1.0 / float(di * do)

        def t_zero():
            state["g"] = P.op("g", lambda g: g.memset(pre["acc"][:], 0.0),
                              after=[state["g"]])
        P.bg.append(t_zero)

        for ti, tile in enumerate(tiles):
            def t_red(tile=tile, ti=ti, src=src):
                r0, rr, c0, cc = tile
                buf = pre["ins"][ti % NR]
                red = pre["reds"][ti % 2]
                ld = P.dma("y", buf[0:rr, 0:cc], src[r0:r0 + rr, c0:c0 + cc],
                           after=[state["ld"][ti % NR]], new=True)
                s1 = P.op("s", lambda s: s.activation(
                    buf[0:rr, 0:cc], buf[0:rr, 0:cc], AF.Abs,
                    accum_out=red[0:rr, :]), after=[ld, state["s"],
                                                    state["redf"][ti % 2]])
                state["s"] = s1
                state["ld"][ti % NR] = s1
                state["g"] = P.op("g", lambda g: g.tensor_add(
                    pre["acc"][0:rr, :], pre["acc"][0:rr, :], red[0:rr, :]),
                    after=[s1, state["g"]])
                state["redf"][ti % 2] = state["g"]
            P.bg.append(t_red)

        def t_scale(wi=wi, inv_n=inv_n):
            mm = P.op("t", lambda t: t.matmul(
                PS[7][:, 0:1], ONES[:], pre["acc"][:, :], start=True, stop=True),
                after=[state["g"], d_ones, state["tread"]])
            state["tread"] = P.op("v", lambda v: v.tensor_scalar(
                SCL[:, wi:wi + 1], PS[7][:, 0:1], inv_n, 1e-5,
                ALU.mult, ALU.max), after=[mm, state["v"]])
            state["v"] = state["tread"]
            state["g"] = state["tread"]

        P.bg.append(t_scale)

        def t_recip(wi=wi):
            state["v"] = P.op("v", lambda v: v.reciprocal(
                pre["rs"][:], SCL[:, wi:wi + 1]), after=[state["g"], state["v"]])
        P.bg.append(t_recip)

        for ti, tile in enumerate(tiles):
            def t_quant(tile=tile, ti=ti, src=src, dst=dst, nm=nm):
                r0, rr, c0, cc = tile
                buf = pre["ins"][ti % NR]
                obuf = pre["outs"][ti % 2]
                ld = P.dma("y", buf[0:rr, 0:cc], src[r0:r0 + rr, c0:c0 + cc],
                           after=[state["ld"][ti % NR], state["v"]], new=True)
                v1 = P.op("v", lambda v: v.tensor_single_scalar(
                    buf[0:rr, 0:cc], buf[0:rr, 0:cc], pre["rs"][0:rr, :], ALU.mult),
                    after=[ld, state["v"]])
                v2 = P.op("v", lambda v: v.tensor_scalar(
                    pre["cmp"][0:rr, 0:cc], buf[0:rr, 0:cc], -0.5, -1.0,
                    ALU.is_le, ALU.mult), after=[v1])
                v4 = P.op("v", lambda v: v.scalar_tensor_tensor(
                    obuf[0:rr, 0:cc], buf[0:rr, 0:cc], 0.5, pre["cmp"][0:rr, 0:cc],
                    ALU.is_ge, ALU.add), after=[v2, state["qd"][ti % 2]])
                state["v"] = v4
                state["ld"][ti % NR] = v4
                if nm in ("f1T", "f2T"):
                    j0 = c0 // 128
                    st = None
                    for jj in range(cc // 128):
                        st = P.dma("g", dst[j0 + jj][:, r0 // 128, :],
                                   obuf[0:rr, jj * 128:(jj + 1) * 128],
                                   after=[v4], new=(jj == 0))
                    state["qd"][ti % 2] = st
                elif nm in ("gk1wT", "gk2wT"):
                    state["qd"][ti % 2] = P.dma("g", dst[r0:r0 + rr, c0:c0 + cc],
                                                obuf[0:rr, 0:cc], after=[v4], new=True)
                else:
                    state["qd"][ti % 2] = P.dma(
                        "g", dst[:, c0 // 1024, r0 // 128, 0:cc],
                        obuf[0:rr, 0:cc], after=[v4], new=True)
            P.bg.append(t_quant)


def stage_l0attn(ctx):
    P, sb, es = ctx["P"], ctx["sb"], ctx["es"]
    XA, SVEC, PS, din = ctx["XA"], ctx["SVEC"], ctx["PS"], ctx["din"]
    s_ew, s_eu, d_sv, d_tm = ctx["svec_ready"]
    xready = ctx["xready"]

    with ExitStack() as st_:
        WK = sb("l0_wk", [128, NC, C], BF16, st_)
        WV = sb("l0_wv", [128, NC, C], BF16, st_)
        WR = sb("l0_wr", [128, NC, C], BF16, st_)
        WO = sb("l0_wo", [128, NC, C], BF16, st_)
        XK = sb("l0_xk", [128, NC, 512], BF16, st_)
        XV = sb("l0_xv", [128, NC, 512], BF16, st_)
        XR = sb("l0_xr", [128, NC, 512], BF16, st_)
        RW = sb("l0_rw", [128, NC, 512], BF16, st_)
        TMPD = sb("l0_tmpd", [128, 512], F32, st_)
        XPREV = sb("l0_xprev", [128, NC, 1], F32, st_)
        NS = 2
        SL_EK = [sb(f"l0_ek{i}", [128, 512], F32, st_) for i in range(NS)]
        SL_EKV = [sb(f"l0_ekv{i}", [128, 512], F32, st_) for i in range(NS)]
        SL_A = [sb(f"l0_A{i}", [128, 513], F32, st_) for i in range(NS)]
        SL_B = [sb(f"l0_B{i}", [128, 513], F32, st_) for i in range(NS)]
        SL_SR = [sb(f"l0_sr{i}", [128, 512], BF16, st_) for i in range(NS)]
        CARA = sb("l0_carA", [128, NC, 1], F32, st_)
        CARB = sb("l0_carB", [128, NC, 1], F32, st_)

        wtok = None
        first = True
        for Wb, nm in ((WK, "wkT"), (WV, "wvT"), (WR, "wrT"), (WO, "woT")):
            for c in range(NC):
                wtok = P.dma("g", Wb[:, c, :], din[nm][c * 128:(c + 1) * 128, :],
                             new=first)
                first = False

        v0 = P.op("v", lambda v: v.memset(CARA[:], 0.0))
        v0 = P.op("v", lambda v: v.memset(CARB[:], 0.0), after=[v0])
        vprev = P.op("v", lambda v: v.memset(XPREV[:], 0.0), after=[v0])

        tmk, tmv, tmr = SVEC[:, 0:8], SVEC[:, 8:16], SVEC[:, 16:24]
        ew, eu = SVEC[:, 24:32], SVEC[:, 32:40]

        slot_free = [None] * NS
        bank_free = [None] * 8
        vt = vprev
        st = None

        for s in range(4):
            s0 = s * 512
            mix_start = [xready, d_tm, vprev, vt]
            for c in range(NC):
                P.pump(1)
                v1 = P.op("v", lambda v, c=c, s0=s0: v.tensor_sub(
                    TMPD[:, 0:1], XA[:, c, s0:s0 + 1], XPREV[:, c, :]),
                    after=mix_start if c == 0 else [vt])
                v2 = P.op("v", lambda v, c=c, s0=s0: v.tensor_sub(
                    TMPD[:, 1:512], XA[:, c, s0 + 1:s0 + 512], XA[:, c, s0:s0 + 511]),
                    after=[v1])
                vt = v2
                for XBUF, tm in ((XK, tmk), (XV, tmv), (XR, tmr)):
                    m1 = P.op("v", lambda v, XBUF=XBUF, tm=tm, c=c: v.scalar_tensor_tensor(
                        XBUF[:, c, 0:1], TMPD[:, 0:1], tm[:, c:c + 1], XPREV[:, c, :],
                        ALU.mult, ALU.add), after=[vt])
                    vt = P.op("v", lambda v, XBUF=XBUF, tm=tm, c=c, s0=s0: v.scalar_tensor_tensor(
                        XBUF[:, c, 1:512], TMPD[:, 1:512], tm[:, c:c + 1],
                        XA[:, c, s0:s0 + 511], ALU.mult, ALU.add), after=[m1])
            mix_all = P.tok("v")

            for c in range(NC):
                P.pump(1)
                sl = c % NS
                ek, ekv, A, Bt, sr = SL_EK[sl], SL_EKV[sl], SL_A[sl], SL_B[sl], SL_SR[sl]
                bk = c % 2
                bv = bk ^ 1

                mmK = None
                for i in range(NC):
                    mmK = P.op("t", lambda t, c=c, i=i, bk=bk: t.matmul(
                        PS[bk][:, :], WK[:, i, c * 128:(c + 1) * 128], XK[:, i, :],
                        start=(i == 0), stop=(i == NC - 1)),
                        after=[wtok, mix_all, bank_free[bk]] if i == 0 else [])
                e1 = P.op("s", lambda s_, ek=ek, bk=bk: s_.activation(
                    ek[:], PS[bk][:, :], AF.Exp), after=[mmK, slot_free[sl], st])
                st = e1
                mmV = None
                for i in range(NC):
                    mmV = P.op("t", lambda t, c=c, i=i, bv=bv: t.matmul(
                        PS[bv][:, :], WV[:, i, c * 128:(c + 1) * 128], XV[:, i, :],
                        start=(i == 0), stop=(i == NC - 1)),
                        after=[bank_free[bv]] if i == 0 else [])
                e2 = P.op("v", lambda v, ek=ek, ekv=ekv, bv=bv: v.tensor_mul(
                    ekv[:], ek[:], PS[bv][:, :]), after=[mmV, e1, vt, slot_free[sl]])
                vt = e2
                bank_free[bk] = e1
                bank_free[bv] = e2
                mmR = None
                for i in range(NC):
                    mmR = P.op("t", lambda t, c=c, i=i: t.matmul(
                        PS[2][:, :], WR[:, i, c * 128:(c + 1) * 128], XR[:, i, :],
                        start=(i == 0), stop=(i == NC - 1)),
                        after=[bank_free[2]] if i == 0 else [])
                e3 = P.op("s", lambda s_, sr=sr: s_.activation(
                    sr[:], PS[2][:, :], AF.Sigmoid), after=[mmR, st, slot_free[sl]])
                st = e3
                bank_free[2] = e3

                c1 = P.op("v", lambda v, A=A, c=c: v.tensor_copy(
                    A[:, 0:1], CARA[:, c, :]), after=[e2, vt])
                c2 = P.op("v", lambda v, Bt=Bt, c=c: v.tensor_copy(
                    Bt[:, 0:1], CARB[:, c, :]), after=[c1])
                sc1 = P.op("v", lambda v, A=A, ekv=ekv, c=c: v.tensor_tensor_scan(
                    A[:, 1:513], ew[:, c:c + 1].broadcast_to([128, 512]), ekv[:],
                    A[:, 0:1], ALU.mult, ALU.add), after=[c2, s_ew])
                sc2 = P.op("v", lambda v, Bt=Bt, ek=ek, c=c: v.tensor_tensor_scan(
                    Bt[:, 1:513], ew[:, c:c + 1].broadcast_to([128, 512]), ek[:],
                    Bt[:, 0:1], ALU.mult, ALU.add), after=[sc1, e1])
                c3 = P.op("v", lambda v, A=A, c=c: v.tensor_copy(
                    CARA[:, c, :], A[:, 512:513]), after=[sc1])
                c4 = P.op("v", lambda v, Bt=Bt, c=c: v.tensor_copy(
                    CARB[:, c, :], Bt[:, 512:513]), after=[sc2])
                n1 = P.op("v", lambda v, ekv=ekv, A=A, c=c: v.scalar_tensor_tensor(
                    ekv[:], ekv[:], eu[:, c:c + 1], A[:, 0:512], ALU.mult, ALU.add),
                    after=[c3, s_eu])
                n2 = P.op("v", lambda v, ek=ek, Bt=Bt, c=c: v.scalar_tensor_tensor(
                    ek[:], ek[:], eu[:, c:c + 1], Bt[:, 0:512], ALU.mult, ALU.add),
                    after=[c4, n1])
                n3 = P.op("v", lambda v, ek=ek: v.reciprocal(
                    ek[:], ek[:]), after=[n2])
                n3b = P.op("v", lambda v, ekv=ekv, ek=ek: v.tensor_mul(
                    ekv[:], ekv[:], ek[:]), after=[n3])
                n4 = P.op("v", lambda v, ekv=ekv, sr=sr, c=c: v.tensor_mul(
                    RW[:, c, :], ekv[:], sr[:]), after=[n3b, e3])
                vt = n4
                slot_free[sl] = n4

            vsave = P.op("v", lambda v, s0=s0: v.tensor_copy(
                XPREV[:, :, :], XA[:, :, s0 + 511:s0 + 512]), after=[vt])
            vprev = vsave
            vt = vsave
            rw_all = P.tok("v")
            for c in range(NC):
                P.pump(1)
                bo = 3 + (c % 2)
                mmO = None
                for i in range(NC):
                    mmO = P.op("t", lambda t, c=c, i=i, bo=bo: t.matmul(
                        PS[bo][:, :], WO[:, i, c * 128:(c + 1) * 128], RW[:, i, :],
                        start=(i == 0), stop=(i == NC - 1)),
                        after=[rw_all, bank_free[bo]] if i == 0 else [])
                ev = P.op("v", lambda v, c=c, s0=s0, bo=bo: v.tensor_add(
                    XA[:, c, s0:s0 + 512], PS[bo][:, :], XA[:, c, s0:s0 + 512]),
                    after=[mmO, vt])
                vt = ev
                bank_free[bo] = ev



def stage_ln(ctx, which, toks):
    P, sb, es = ctx["P"], ctx["sb"], ctx["es"]
    XA, IDF, PS, STAT = ctx["XA"], ctx["IDF"], ctx["PS"], ctx["STAT"]
    out = ctx["out"]
    final = which == 4

    with ExitStack() as st_:
        TM = [sb(f"ln{which}_tm{i}", [128, 1024], F32, st_) for i in range(2)]
        SQJ = sb(f"ln{which}_sqj", [128, 1024], F32, st_)
        COLS = sb(f"ln{which}_cols", [128, 8], F32, st_)

        P.barrier()
        vt = None
        stt = None
        ttok = None
        tm_free = [None, None]
        fwd_bank_free = [None, None]
        bwd_bank_free = [None, None]
        dma_last = None
        for tt in range(NTT):
            P.pump(1)
            sl = tt % 2
            tmb = TM[sl]
            tps = []
            for c in range(NC):
                bank = 5 + c // 4
                cc = c % 4
                a = [tm_free[sl], fwd_bank_free[c // 4]] if c in (0, 4) else []
                tps.append(P.op("t", lambda t, c=c, tt=tt, bank=bank, cc=cc: t.transpose(
                    PS[bank][:, cc * 128:(cc + 1) * 128],
                    XA[:, c, tt * 128:(tt + 1) * 128], IDF[:]), after=a))
            ttok = tps[-1]
            e1 = P.op("s", lambda s, tmb=tmb: s.activation(
                tmb[:, 0:512], PS[5][:, :], AF.Identity), after=[tps[3], stt, tm_free[sl]])
            e2 = P.op("s", lambda s, tmb=tmb: s.activation(
                tmb[:, 512:1024], PS[6][:, :], AF.Identity), after=[tps[7], e1])
            stt = e2
            fwd_bank_free[0] = e1
            fwd_bank_free[1] = e2
            v1 = P.op("v", lambda v, tmb=tmb: v.reduce_sum(
                COLS[:, 0:1], tmb[:], axis=AXX), after=[e2, vt])
            v2 = P.op("s", lambda s, tmb=tmb: s.activation(
                SQJ[:], tmb[:], AF.Square, accum_out=COLS[:, 1:2]),
                after=[v1, stt])
            stt = v2
            v3 = P.op("v", lambda v: v.tensor_scalar(
                COLS[:, 2:3], COLS[:, 0:1], 1.0 / 1024, None, ALU.mult), after=[v2])
            v4 = P.op("v", lambda v: v.tensor_mul(
                COLS[:, 3:4], COLS[:, 2:3], COLS[:, 2:3]), after=[v3])
            v5 = P.op("v", lambda v: v.scalar_tensor_tensor(
                COLS[:, 4:5], COLS[:, 1:2], 1.0 / 1024, COLS[:, 3:4],
                ALU.mult, ALU.subtract), after=[v4])
            v6 = P.op("v", lambda v: v.tensor_scalar(
                COLS[:, 4:5], COLS[:, 4:5], 1e-5, None, ALU.add), after=[v5])
            v7 = P.op("v", lambda v: v.reciprocal(COLS[:, 5:6], COLS[:, 4:5]),
                      after=[v6])
            s1 = P.op("s", lambda s: s.activation(
                COLS[:, 6:7], COLS[:, 5:6], AF.Sqrt), after=[v7, stt])
            stt = s1
            v8 = P.op("v", lambda v, tmb=tmb: v.tensor_scalar(
                tmb[:], tmb[:], COLS[:, 2:3], COLS[:, 6:7],
                ALU.subtract, ALU.mult), after=[s1, vt])
            vt = v8
            if which == 2:
                stt = P.op("s", lambda s, tmb=tmb, tt=tt: s.activation(
                    SQJ[:], tmb[:], AF.Square,
                    accum_out=STAT[:, 16 + tt:17 + tt]), after=[v8, stt])
                vt = v8
            if final:
                dma_last = P.dma("y", out[tt * 128:(tt + 1) * 128, :], tmb[:],
                                 after=[vt], new=True)
                tm_free[sl] = dma_last
            else:
                bts = []
                for c in range(NC):
                    bank = 3 + c // 4
                    cc = c % 4
                    a = [vt, bwd_bank_free[c // 4]] if c in (0, 4) else []
                    bts.append(P.op("t", lambda t, c=c, bank=bank, cc=cc, tmb=tmb: t.transpose(
                        PS[bank][:, cc * 128:(cc + 1) * 128],
                        tmb[:, c * 128:(c + 1) * 128], IDF[:]), after=a))
                ttok = bts[-1]
                ev = []
                for half in range(2):
                    bank = 3 + half
                    ev.append(P.op("s", lambda s, tt=tt, bank=bank, half=half: s.activation(
                        _xa_cols(XA, tt, half), PS[bank][:, :], AF.Identity),
                        after=[bts[3 + 4 * half], stt]))
                stt = ev[-1]
                bwd_bank_free[0] = ev[0]
                bwd_bank_free[1] = ev[1]
                tm_free[sl] = ev[-1]
        return dict(v=P.tok("v"), t=P.tok("t"), s=P.tok("s"),
                    out_dma=dma_last)


def _xa_cols(XA, tt, half):
    return XA[:, half * 4:(half + 1) * 4, tt * 128:(tt + 1) * 128]


def stage_l0ffn(ctx, toks, debug_stop=None):
    P, sb = ctx["P"], ctx["sb"]
    XA, SVEC, PS, din = ctx["XA"], ctx["SVEC"], ctx["PS"], ctx["din"]

    with ExitStack() as st_:
        H0 = sb("f_h0", [128, 32768], BF16, st_)
        H1X = sb("f_h1", [128, 32768], BF16, st_)
        WJ = [sb(f"f_wj{i}", [128, 16, 128], BF16, st_) for i in range(2)]
        G1 = sb("f_g1", [128, 1024], BF16, st_)
        SC = sb("f_sc", [128, 128], F32, st_)

        P.barrier()
        d_sc = P.dma("y", SC[:, 0:32], din["s1"], new=True)
        P.dma("y", SC[:, 32:64], din["s2"])
        P.dma("y", SC[:, 64:96], din["sh1"])
        d_sc2 = P.dma("y", SC[:, 96:128], din["sh2"])
        vt = None
        stt = None
        wj_free = [None, None]
        bank_free = [None] * 8
        s3 = SVEC[:, 40:48]
        wi = 0

        def wload(nm, j, half, after=()):
            nonlocal wi
            slot = wi % 2
            wb = WJ[slot]
            srcap = din[nm][j]
            if half is None:
                dst, s_ = wb[:, 0:8, :], srcap
            else:
                dst, s_ = wb[:, 0:16, :], srcap[:, half * 16:half * 16 + 16, :]
            tok = P.dma("g", dst, s_, after=[wj_free[slot]] + list(after), new=True)
            wi += 1
            return slot, tok

        for hb in range(2):
            t0 = hb * 1024
            XB = H1X[:, 0:8192].rearrange("p (c n) -> p c n", c=8)
            cast = None
            for c in range(NC):
                cast = P.op("v", lambda v, c=c, t0=t0: v.tensor_copy(
                    XB[:, c, :], XA[:, c, t0:t0 + 1024]),
                    after=[vt, toks.get("v"), toks.get("s")] if c == 0 else [])
            vt = cast
            xb_tok = cast

            for j in range(32):
                P.pump(1)
                slot1, dtok1 = wload("w1R", j, None)
                slot2, dtok2 = wload("w2R", j, None)
                wb1, wb2 = WJ[slot1], WJ[slot2]
                mmlast = None
                for wsel in range(2):
                    wb = (wb1, wb2)[wsel]
                    dtok = (dtok1, dtok2)[wsel]
                    for i in range(NC):
                        for s2 in range(2):
                            bank = wsel * 4 + (j % 2) * 2 + s2
                            mmlast = P.op("t", lambda t, wb=wb, i=i, bank=bank, s2=s2:
                                          t.matmul(
                                PS[bank][:, :], wb[:, i, :],
                                XB[:, i, s2 * 512:(s2 + 1) * 512],
                                start=(i == 0), stop=(i == NC - 1)),
                                after=[dtok, xb_tok, bank_free[bank]]
                                if i == 0 and s2 == 0 else
                                ([bank_free[bank]] if i == 0 else []))
                for s2 in range(2):
                    scj = j
                    bank = (j % 2) * 2 + s2
                    evs = P.op("s", lambda s, scj=scj, bank=bank, s2=s2: s.activation(
                        G1[:, s2 * 512:(s2 + 1) * 512], PS[bank][:, :], AF.Silu,
                        scale=SC[:, scj:scj + 1]), after=[mmlast, stt, vt, d_sc])
                    stt = evs
                    bank_free[bank] = evs
                for s2 in range(2):
                    scj = 32 + j
                    bank = 4 + (j % 2) * 2 + s2
                    off = s2 * 512
                    evv = P.op("v", lambda v, scj=scj, j=j, off=off, bank=bank:
                               v.scalar_tensor_tensor(
                        H0[:, j * 1024 + off:j * 1024 + off + 512],
                        PS[bank][:, :], SC[:, scj:scj + 1],
                        G1[:, off:off + 512], ALU.mult, ALU.mult),
                        after=[mmlast, stt, vt])
                    vt = evv
                    bank_free[bank] = evv
                wj_free[slot1] = mmlast
                wj_free[slot2] = mmlast
            h0_done_v = vt

            for phase, nm, scoff, SRC, DST in (
                    ("B", "h1R", 64, H0, H1X), ("C", "h2R", 96, H1X, H0)):
                for j in range(32):
                    P.pump(1)
                    stok = [None, None]
                    wbs = [None, None]
                    for half in range(2):
                        wbs[half], stok[half] = wload(nm, j, half)
                    mm = None
                    for ihalf in range(2):
                        wb = WJ[wbs[ihalf]]
                        for i2 in range(16):
                            i = ihalf * 16 + i2
                            for s2 in range(2):
                                off = s2 * 512
                                bank = 4 + (j % 2) * 2 + s2
                                mm = P.op("t", lambda t, wb=wb, i2=i2, i=i, bank=bank,
                                          off=off, SRC=SRC: t.matmul(
                                    PS[bank][:, :], wb[:, i2, :],
                                    SRC[:, i * 1024 + off:i * 1024 + off + 512],
                                    start=(i == 0), stop=(i == 31)),
                                    after=[stok[0], stok[1], bank_free[bank], h0_done_v]
                                    if i == 0 and s2 == 0 else
                                    ([bank_free[bank]] if i == 0 else []))
                    for s2 in range(2):
                        off = s2 * 512
                        bank = 4 + (j % 2) * 2 + s2
                        ev = P.op("s", lambda s, j=j, off=off, bank=bank, scoff=scoff,
                                  DST=DST: s.activation(
                            DST[:, j * 1024 + off:j * 1024 + off + 512],
                            PS[bank][:, :], AF.Gelu, scale=SC[:, scoff + j:scoff + j + 1]),
                            after=[mm, stt, d_sc2])
                        stt = ev
                        bank_free[bank] = ev
                    wj_free[wbs[0]] = mm
                    wj_free[wbs[1]] = mm
                h0_done_v = stt
            hc_done = stt

            for j in range(NC):
                P.pump(1)
                stok = [None, None]
                wbs = [None, None]
                for half in range(2):
                    wbs[half], stok[half] = wload("w3R", j, half)
                mm = None
                for ihalf in range(2):
                    wb = WJ[wbs[ihalf]]
                    for i2 in range(16):
                        i = ihalf * 16 + i2
                        for s2 in range(2):
                            off = s2 * 512
                            bank = (j % 2) * 2 + s2
                            mm = P.op("t", lambda t, wb=wb, i2=i2, i=i, bank=bank,
                                      off=off: t.matmul(
                                PS[bank][:, :], wb[:, i2, :],
                                H0[:, i * 1024 + off:i * 1024 + off + 512],
                                start=(i == 0), stop=(i == 31)),
                                after=[stok[0], stok[1], bank_free[bank], hc_done]
                                if i == 0 and s2 == 0 else
                                ([bank_free[bank]] if i == 0 else []))
                for s2 in range(2):
                    off = s2 * 512
                    bank = (j % 2) * 2 + s2
                    ev = P.op("v", lambda v, j=j, off=off, t0=t0, bank=bank:
                              v.scalar_tensor_tensor(
                        XA[:, j, t0 + off:t0 + off + 512], PS[bank][:, :],
                        s3[:, j:j + 1], XA[:, j, t0 + off:t0 + off + 512],
                        ALU.mult, ALU.add), after=[mm, vt])
                    vt = ev
                    bank_free[bank] = ev
                wj_free[wbs[0]] = mm
                wj_free[wbs[1]] = mm
        return dict(v=P.tok("v"), t=P.tok("t"), s=P.tok("s"))



def stage_l1attn(ctx, toks):
    P, sb = ctx["P"], ctx["sb"]
    XA, IDF, TRIU, PS, STAT, SCL = (ctx["XA"], ctx["IDF"], ctx["TRIU"],
                                    ctx["PS"], ctx["STAT"], ctx["SCL"])
    din, qscr, SVEC, IDB = ctx["din"], ctx["qscr"], ctx["SVEC"], ctx["IDB"]
    PSB = ctx["PSB"]

    with ExitStack() as st_:
        S = sb("a_S", [128, 8, 512], F32, st_)
        X2 = sb("a_x2", [128, 2, 1024], BF16, st_)
        QSz = sb("a_qz", [128, 1024], BF16, st_)
        QSi = sb("a_qi", [128, 1024], mybir.dt.int32, st_)
        QSt = sb("a_qt", [128, 1024], BF16, st_)
        QSd = sb("a_qd", [128, 1024], BF16, st_)
        Q2z = sb("a_q2z", [128, 1024], BF16, st_)
        Q2i = sb("a_q2i", [128, 1024], mybir.dt.int32, st_)
        Q2t = sb("a_q2t", [128, 1024], BF16, st_)
        Q2d = sb("a_q2d", [128, 1024], BF16, st_)
        CL2 = sb("a_cl2", [128, 16], F32, st_)
        XQF = sb("a_xqf", [128, 8, 256], BF16, st_)
        Q = sb("a_q", [128, 8, 256], F32, st_)
        K = sb("a_k", [128, 8, 256], BF16, st_)
        GK = sb("a_gk", [128, 8, 256], BF16, st_)
        WF = sb("a_wf", [128, 8, 256], F32, st_)
        M0 = sb("a_m0", [128, 8, 256], BF16, st_)
        V = sb("a_v", [128, 2, 2048], F32, st_)
        SG = sb("a_sg", [128, 2, 2048], BF16, st_)
        GT = sb("a_gt", [128, 8, 129], F32, st_)
        DC = sb("a_dc", [128, 8], F32, st_)
        QG = sb("a_qg", [128, 2, 128], F32, st_)
        ET = sb("a_et", [128, 2, 128], F32, st_)
        KP = sb("a_kp", [128, 2, 128], F32, st_)
        KG = sb("a_kg", [128, 2, 128], F32, st_)
        KGt = sb("a_kgt", [128, 2, 128], F32, st_)
        A2 = sb("a_a2", [128, 128], F32, st_)
        OG = sb("a_og", [128, 2048], BF16, st_)
        OQF = sb("a_oqf", [128, 16, 128], BF16, st_)
        NB = sb("a_nb", [128, 1024], BF16, st_)
        NOB = sb("a_nob", [128, 2048], BF16, st_)
        NGK2 = sb("a_ngk2", [128, 16], BF16, st_)
        G1T = sb("a_g1t", [128, 2, 16], F32, st_)
        X2Q = sb("a_x2q", [16, 256], BF16, st_)
        WGK1 = sb("a_wgk1", [128, 8, 16], BF16, st_)
        WGK2 = sb("a_wgk2", [16, 1024], BF16, st_)
        WL = sb("a_wl", [128, 8192], BF16, st_)
        CL = sb("a_cl", [128, 16], F32, st_)

        P.barrier()
        d_ngk2 = P.dma("g", NGK2[:], din["ngk2B"], new=True)
        d_nob = P.dma("g", NOB[:], din["noB"], new=True)
        d_wgk1 = P.dma("g", WGK1[:, :, :],
                       qscr["gk1wT"].rearrange("(c p) n -> p c n", p=128), new=True)
        d_wgk2 = P.dma("g", WGK2[:, :], qscr["gk2wT"], new=True)

        vz = P.op("v", lambda v: v.memset(S[:], 0.0))
        vz2 = P.op("v", lambda v: v.memset(STAT[:, 62:63], 0.0), after=[vz])
        vt = vz2
        gt = None
        nb_free2 = [None, None]
        zb_free = [None, None]
        stt = None
        bank_free = [None] * 8
        wl_free = None
        x2_free = None
        sgate = SVEC[:, 48:56]

        NPROJ = [("q", "qwT", "nqB", 1024), ("k", "kwT", "nkB", 1024),
                 ("v", "vwT", "nvB", 2048), ("g", "gwT", "ngB", 2048),
                 ("gk1", "gk1wT", "ngk1B", 16)]

        for blk_i in range(8):
            t0 = blk_i * 256
            for tt in range(2):
                tps = []
                for c in range(NC):
                    bank = 5 + c // 4
                    cc = c % 4
                    a = [x2_free, bank_free[bank]] if (c % 4 == 0) else []
                    tps.append(P.op("t", lambda t, tt=tt, c=c, t0=t0, bank=bank, cc=cc:
                                    t.transpose(PS[bank][:, cc * 128:(cc + 1) * 128],
                                    XA[:, c, t0 + tt * 128:t0 + (tt + 1) * 128], IDF[:]),
                                    after=a))
                for half in range(2):
                    bank = 5 + half
                    ev = P.op("v", lambda v, tt=tt, half=half, bank=bank: v.tensor_copy(
                        X2[:, tt, half * 512:(half + 1) * 512], PS[bank][:, :]),
                        after=[tps[half * 4 + 3], vt])
                    vt = ev
                    bank_free[bank] = ev
            x2tok = vt
            c1 = P.op("v", lambda v, blk_i=blk_i: v.tensor_scalar(
                CL[:, 0:2], STAT[:, 16 + 2 * blk_i:18 + 2 * blk_i],
                1.0 / 1024, 1e-8, ALU.mult, ALU.add), after=[vt])
            c2 = P.op("v", lambda v: v.reciprocal(CL[:, 2:4], CL[:, 0:2]), after=[c1])
            c3 = P.op("s", lambda s: s.activation(CL[:, 0:2], CL[:, 2:4], AF.Sqrt),
                      after=[c2, stt])
            stt = c3

            for pi, (pnm, wnm, nnm, dout) in enumerate(NPROJ):
                d_nb = P.dma("g", NB[:, 0:1024], din[nnm], new=True,
                             after=[nb_free2[0], nb_free2[1]])
                qtoks = []
                for tt in range(2):
                    en = "v"
                    Zb, Ib, Tb, Db = (QSz, QSi, QSt, QSd) if tt == 0 else (Q2z, Q2i, Q2t, Q2d)
                    CLx = CL if tt == 0 else CL2
                    ch = vt
                    p1 = P.op(en, lambda v, tt=tt, Zb=Zb: v.tensor_mul(
                        Zb[:], X2[:, tt, :], NB[:, 0:1024]),
                        after=[d_nb, x2tok, ch, zb_free[tt]])
                    nb_free2[tt] = p1
                    p2 = P.op("v", lambda v, Zb=Zb, CLx=CLx: v.reduce_max(
                        CLx[:, 4:5], Zb[:], axis=AXX, apply_absolute_value=True),
                        after=[p1, vt])
                    p3 = P.op("v", lambda v, tt=tt, CLx=CLx: v.tensor_scalar(
                        CLx[:, 5:6], CLx[:, 4:5], CL[:, tt:tt + 1], 1e-5,
                        ALU.mult, ALU.max), after=[p2, c3])
                    p4 = P.op("v", lambda v, CLx=CLx: v.reciprocal(
                        CLx[:, 6:7], CLx[:, 5:6]), after=[p3])
                    p5 = P.op("v", lambda v, tt=tt, CLx=CLx: v.scalar_tensor_tensor(
                        CLx[:, 7:8], CLx[:, 6:7], 127.0, CL[:, tt:tt + 1],
                        ALU.mult, ALU.mult), after=[p4])
                    p6 = P.op("v", lambda v, CLx=CLx: v.tensor_scalar(
                        CLx[:, 8:9], CLx[:, 5:6], 1.0 / 127, None, ALU.mult),
                        after=[p5])
                    vt = p6
                    q1 = P.op("s", lambda s, Zb=Zb, CLx=CLx: s.activation(
                        Zb[:], Zb[:], AF.Identity, scale=CLx[:, 7:8]),
                        after=[p6, ch, stt])
                    stt = q1
                    q6, s6 = _round_fix(P, en, Zb, Ib, Tb, Db, Zb, q1, stt)
                    stt = s6
                    q7 = P.op(en, lambda v, Zb=Zb, Tb=Tb, CLx=CLx: v.tensor_scalar(
                        Zb[:], Tb[:], 1.0, CLx[:, 8:9], ALU.subtract, ALU.mult),
                        after=[q6])
                    vt = q7
                    xtp = []
                    for c in range(NC):
                        bank = 5 + c // 4
                        cc = c % 4
                        a = [q7, bank_free[bank]] if (c % 4 == 0) else []
                        xtp.append(P.op("t", lambda t, c=c, bank=bank, cc=cc, Zb=Zb:
                                        t.transpose(
                            PSB[bank][:, cc * 128:(cc + 1) * 128],
                            Zb[:, c * 128:(c + 1) * 128], IDB[:]), after=a))
                    zb_free[tt] = xtp[-1]
                    for half in range(2):
                        bank = 5 + half
                        ev = P.op("v", lambda v, tt=tt, half=half, bank=bank: v.tensor_copy(
                            XQF[:, half * 4:(half + 1) * 4, tt * 128:(tt + 1) * 128],
                            PSB[bank][:, 0:512].rearrange("p (c n) -> p c n", c=4)),
                            after=[xtp[half * 4 + 3], vt])
                        vt = ev
                        bank_free[bank] = ev
                    qtoks.append(vt)
                xq_tok = vt

                if pnm == "gk1":
                    mms = []
                    for tt in range(2):
                        bank = 2
                        mm = None
                        for i in range(NC):
                            mm = P.op("t", lambda t, i=i, tt=tt, bank=bank: t.matmul(
                                PS[bank][:, 0:16], XQF[:, i, tt * 128:(tt + 1) * 128],
                                WGK1[:, i, :], start=(i == 0), stop=(i == NC - 1)),
                                after=[d_wgk1, xq_tok, bank_free[bank]] if i == 0 else [])
                        ev = P.op("v", lambda v, tt=tt, bank=bank: v.tensor_scalar(
                            G1T[:, tt, :], PS[bank][:, 0:16],
                            SCL[:, QWI["gk1wT"]:QWI["gk1wT"] + 1], None, ALU.mult),
                            after=[mm, vt])
                        vt = ev
                        bank_free[bank] = ev
                    continue
                wq = qscr[wnm]
                scl_c = SCL[:, QWI[wnm]:QWI[wnm] + 1]
                for half in range(dout // 1024):
                    WLv = WL[:, 0:8192].rearrange("p (c n) -> p c n", c=8)
                    d_w = P.dma("g", WLv[:, :, :], wq[:, half, :, :],
                                after=[wl_free], new=True)
                    if pnm in ("q", "k"):
                        DSTB = Q if pnm == "q" else K
                        for j in range(NC):
                            bank = j % 2
                            mm = None
                            for i in range(NC):
                                mm = P.op("t", lambda t, i=i, j=j, bank=bank: t.matmul(
                                    PS[bank][:, 0:256], WLv[:, i, j * 128:(j + 1) * 128],
                                    XQF[:, i, :],
                                    start=(i == 0), stop=(i == NC - 1)),
                                    after=[d_w, xq_tok, bank_free[bank]] if i == 0 else [])
                            ev = P.op("v", lambda v, j=j, bank=bank, DSTB=DSTB,
                                      scl_c=scl_c: v.tensor_scalar(
                                DSTB[:, j, :], PS[bank][:, 0:256], scl_c, None,
                                ALU.mult), after=[mm, vt])
                            vt = ev
                            bank_free[bank] = ev
                            wl_free = mm
                    else:
                        for tt in range(2):
                            for ds in range(2):
                                bank = ds % 2
                                mm = None
                                for i in range(NC):
                                    mm = P.op("t", lambda t, i=i, tt=tt, ds=ds, bank=bank:
                                              t.matmul(
                                        PS[bank][:, :],
                                        XQF[:, i, tt * 128:(tt + 1) * 128],
                                        WLv[:, i, ds * 512:(ds + 1) * 512],
                                        start=(i == 0), stop=(i == NC - 1)),
                                        after=[d_w, xq_tok, bank_free[bank]] if i == 0 else [])
                                off = half * 1024 + ds * 512
                                if pnm == "v":
                                    ev = P.op("v", lambda v, tt=tt, off=off, bank=bank,
                                              scl_c=scl_c: v.tensor_scalar(
                                        V[:, tt, off:off + 512], PS[bank][:, :],
                                        scl_c, None, ALU.mult), after=[mm, vt])
                                    vt = ev
                                else:
                                    ev = P.op("s", lambda s, tt=tt, off=off, bank=bank,
                                              scl_c=scl_c: s.activation(
                                        SG[:, tt, off:off + 512], PS[bank][:, :],
                                        AF.Silu, scale=scl_c), after=[mm, stt])
                                    stt = ev
                                bank_free[bank] = ev
                                wl_free = mm

            for tt in range(2):
                h1 = P.op("s", lambda s, tt=tt: s.activation(
                    WF[:, 0, 0:16], G1T[:, tt, :], AF.Square,
                    accum_out=CL[:, 9:10]), after=[vt, stt])
                stt = h1
                h2 = P.op("v", lambda v: v.tensor_scalar(
                    CL[:, 10:11], CL[:, 9:10], 1.0 / 16, 1e-8, ALU.mult, ALU.add),
                    after=[h1])
                h3 = P.op("v", lambda v: v.reciprocal(CL[:, 11:12], CL[:, 10:11]),
                          after=[h2])
                h4 = P.op("s", lambda s: s.activation(CL[:, 10:11], CL[:, 11:12],
                                                      AF.Sqrt), after=[h3, stt])
                stt = h4
                r1 = P.op("v", lambda v, tt=tt: v.tensor_mul(
                    QSz[:, 0:16], G1T[:, tt, :], NGK2[:]),
                    after=[h4, d_ngk2, vt, zb_free[0]])
                r2 = P.op("v", lambda v: v.reduce_max(
                    CL[:, 4:5], QSz[:, 0:16], axis=AXX, apply_absolute_value=True),
                    after=[r1])
                r3 = P.op("v", lambda v: v.tensor_scalar(
                    CL[:, 5:6], CL[:, 4:5], CL[:, 10:11], 1e-5, ALU.mult, ALU.max),
                    after=[r2])
                r4 = P.op("v", lambda v: v.reciprocal(CL[:, 6:7], CL[:, 5:6]),
                          after=[r3])
                r5 = P.op("v", lambda v: v.scalar_tensor_tensor(
                    CL[:, 7:8], CL[:, 6:7], 127.0, CL[:, 10:11], ALU.mult, ALU.mult),
                    after=[r4])
                r6 = P.op("v", lambda v: v.tensor_scalar(
                    CL[:, 8:9], CL[:, 5:6], 1.0 / 127, None, ALU.mult), after=[r5])
                u1 = P.op("v", lambda v: v.tensor_single_scalar(
                    QSz[:, 0:16], QSz[:, 0:16], CL[:, 7:8], ALU.mult), after=[r6])
                u2 = P.op("v", lambda v: v.tensor_copy(QSi[:, 0:16], QSz[:, 0:16]),
                          after=[u1])
                u3 = P.op("v", lambda v: v.tensor_copy(QSt[:, 0:16], QSi[:, 0:16]),
                          after=[u2])
                u4 = P.op("v", lambda v: v.tensor_sub(
                    QSd[:, 0:16], QSz[:, 0:16], QSt[:, 0:16]), after=[u3])
                u5 = P.op("v", lambda v: v.scalar_tensor_tensor(
                    QSt[:, 0:16], QSd[:, 0:16], 0.5, QSt[:, 0:16], ALU.is_ge,
                    ALU.add), after=[u4])
                u6 = P.op("v", lambda v: v.scalar_tensor_tensor(
                    QSt[:, 0:16], QSd[:, 0:16], -0.5, QSt[:, 0:16], ALU.is_gt,
                    ALU.add), after=[u5])
                u7 = P.op("v", lambda v: v.tensor_scalar(
                    QSz[:, 0:16], QSt[:, 0:16], 1.0, CL[:, 8:9], ALU.subtract,
                    ALU.mult), after=[u6])
                vt = u7
                tp = P.op("t", lambda t, tt=tt: t.transpose(
                    PSB[2][0:16, 0:128], QSz[:, 0:16], IDB[:]), after=[u7, bank_free[2]])
                zb_free[0] = tp
                ev = P.op("v", lambda v, tt=tt: v.tensor_copy(
                    X2Q[:, tt * 128:(tt + 1) * 128], PSB[2][0:16, 0:128]),
                    after=[tp, vt])
                vt = ev
                bank_free[2] = ev
            for j in range(NC):
                bank = 2 + (j % 2)
                mm = P.op("t", lambda t, j=j, bank=bank: t.matmul(
                    PS[bank][:, 0:256], WGK2[:, j * 128:(j + 1) * 128],
                    X2Q[:], start=True, stop=True),
                    after=[d_wgk2, vt, bank_free[bank]])
                ev = P.op("s", lambda s, j=j, bank=bank: s.activation(
                    WF[:, j, :], PS[bank][:, 0:256], AF.Identity,
                    scale=SCL[:, QWI["gk2wT"]:QWI["gk2wT"] + 1],
                    bias=sgate[:, j:j + 1]), after=[mm, stt])
                stt = ev
                bank_free[bank] = ev
            l1 = P.op("v", lambda v: v.tensor_scalar(
                M0[:], WF[:], 0.0, 1.0 / 16, ALU.min, ALU.mult), after=[stt, vt])
            l2 = P.op("s", lambda s: s.activation(WF[:], WF[:], AF.Abs),
                      after=[l1, stt])
            l3 = P.op("s", lambda s: s.activation(WF[:], WF[:], AF.Exp, scale=-1.0),
                      after=[l2])
            l4 = P.op("s", lambda s: s.activation(WF[:], WF[:], AF.Ln, bias=1.0),
                      after=[l3])
            l5 = P.op("v", lambda v: v.scalar_tensor_tensor(
                GK[:], WF[:], -1.0 / 16, M0[:], ALU.mult, ALU.add), after=[l4])
            vt = l5

            for cn in range(2):
                z0 = P.op("v", lambda v: v.memset(GT[:, :, 0:1], 0.0), after=[vt])
                sc = []
                for c2 in range(NC):
                    sc.append(P.op("v", lambda v, c2=c2, cn=cn: v.tensor_tensor_scan(
                        GT[:, c2, 1:129], GK[:, c2, cn * 128:(cn + 1) * 128],
                        STAT[:, 62:63].broadcast_to([128, 128]), 0.0,
                        ALU.add, ALU.add), after=[z0 if c2 == 0 else sc[-1], l5]))
                vt = sc[-1]
                dc1 = P.op("s", lambda s: s.activation(
                    DC[:, :], GT[:, :, 128:129].rearrange("p c n -> p (c n)"),
                    AF.Exp), after=[vt, stt])
                stt = dc1
                for h in range(4):
                    f1 = P.op("s", lambda s, h=h: s.activation(
                        QG[:, :, :], GT[:, 2 * h:2 * h + 2, 1:129], AF.Exp),
                        after=[vt, stt, bank_free[3]])
                    f2 = P.op("s", lambda s, h=h: s.activation(
                        ET[:, :, :], GT[:, 2 * h:2 * h + 2, 1:129], AF.Exp,
                        scale=-1.0), after=[f1])
                    stt = f2
                    f3 = P.op("v", lambda v, h=h, cn=cn: v.scalar_tensor_tensor(
                        QG[:, :, :], QG[:, :, :], 1.0 / 16,
                        Q[:, 2 * h:2 * h + 2, cn * 128:(cn + 1) * 128],
                        ALU.mult, ALU.mult), after=[f1, vt])
                    f4 = P.op("v", lambda v, h=h, cn=cn: v.tensor_mul(
                        KP[:, :, :], ET[:, :, :],
                        K[:, 2 * h:2 * h + 2, cn * 128:(cn + 1) * 128]),
                        after=[f2, f3])
                    f5 = []
                    for j in range(2):
                        f5.append(P.op("v", lambda v, h=h, j=j: v.tensor_scalar(
                            ET[:, j, :], GT[:, 2 * h + j, 1:129],
                            GT[:, 2 * h + j, 128:129], None, ALU.subtract),
                            after=[f4]))
                    f6 = P.op("s", lambda s: s.activation(
                        ET[:, :, :], ET[:, :, :], AF.Exp, scale=-1.0),
                        after=[f5[0], f5[1], stt])
                    stt = f6
                    f7 = P.op("v", lambda v, h=h, cn=cn: v.tensor_mul(
                        KG[:, :, :], ET[:, :, :],
                        K[:, 2 * h:2 * h + 2, cn * 128:(cn + 1) * 128]),
                        after=[f6])
                    vt = f7
                    amm = None
                    for j in range(2):
                        amm = P.op("t", lambda t, j=j: t.matmul(
                            PS[2][:, 0:128], KP[:, j, :], QG[:, j, :],
                            start=(j == 0), stop=(j == 1)),
                            after=[f7, bank_free[2]] if j == 0 else [])
                    aev = P.op("v", lambda v: v.tensor_mul(
                        A2[:], PS[2][:, 0:128], TRIU[:]), after=[amm, vt])
                    vt = aev
                    bank_free[2] = aev
                    ktp = []
                    for j in range(2):
                        ktp.append(P.op("t", lambda t, j=j: t.transpose(
                            PS[4][:, j * 128:(j + 1) * 128], KG[:, j, :], IDF[:]),
                            after=[f7, bank_free[4]] if j == 0 else []))
                    kev = P.op("v", lambda v: v.tensor_copy(
                        KGt[:, :, :], PS[4][:, 0:256].rearrange("p (c n) -> p c n", c=2)),
                        after=[ktp[1], vt])
                    vt = kev
                    bank_free[4] = kev
                    omm = None
                    for j in range(2):
                        omm = P.op("t", lambda t, h=h, j=j: t.matmul(
                            PS[3][:, :], QG[:, j, :], S[:, 2 * h + j, :],
                            start=(j == 0), stop=False),
                            after=[f3, bank_free[3], vz] if j == 0 else [])
                    omm = P.op("t", lambda t, h=h, cn=cn: t.matmul(
                        PS[3][:, :], A2[:], V[:, cn, h * 512:(h + 1) * 512],
                        start=False, stop=True), after=[aev])
                    dmm = []
                    for j in range(2):
                        dmm.append(P.op("t", lambda t, h=h, cn=cn, j=j: t.matmul(
                            PS[6 + j][:, :], KGt[:, j, :],
                            V[:, cn, h * 512:(h + 1) * 512],
                            start=True, stop=True),
                            after=[kev, bank_free[6 + j]]))
                    for j in range(2):
                        su = P.op("v", lambda v, h=h, j=j: v.scalar_tensor_tensor(
                            S[:, 2 * h + j, :], S[:, 2 * h + j, :],
                            DC[:, 2 * h + j:2 * h + j + 1], PS[6 + j][:, :],
                            ALU.mult, ALU.add), after=[dmm[j], omm, dc1, vt])
                        vt = su
                        bank_free[6 + j] = su
                    oev = P.op("v", lambda v, h=h, cn=cn: v.tensor_mul(
                        OG[:, h * 512:(h + 1) * 512], PS[3][:, :],
                        SG[:, cn, h * 512:(h + 1) * 512]), after=[omm, vt])
                    vt = oev
                    bank_free[3] = oev
                og_tok = vt

                WFf = WF[:, :, :].rearrange("p c n -> p (c n)")
                m1 = P.op("s", lambda s, WFf=WFf: s.activation(
                    WFf[:, 0:1024], OG[:, 0:1024], AF.Square,
                    accum_out=CL[:, 9:10]), after=[og_tok, stt])
                m2 = P.op("s", lambda s, WFf=WFf: s.activation(
                    WFf[:, 1024:2048], OG[:, 1024:2048], AF.Square,
                    accum_out=CL[:, 10:11]), after=[m1])
                stt = m2
                m3 = P.op("v", lambda v: v.tensor_add(
                    CL[:, 9:10], CL[:, 9:10], CL[:, 10:11]), after=[m2, vt])
                m4 = P.op("v", lambda v: v.tensor_scalar(
                    CL[:, 10:11], CL[:, 9:10], 1.0 / 2048, 1e-8, ALU.mult, ALU.add),
                    after=[m3])
                m5 = P.op("v", lambda v: v.reciprocal(CL[:, 11:12], CL[:, 10:11]),
                          after=[m4])
                m6 = P.op("s", lambda s: s.activation(CL[:, 10:11], CL[:, 11:12],
                                                      AF.Sqrt), after=[m5, stt])
                stt = m6
                hv = []
                for half in range(2):
                    w1 = P.op("v", lambda v, half=half: v.tensor_mul(
                        QSz[:], OG[:, half * 1024:(half + 1) * 1024],
                        NOB[:, half * 1024:(half + 1) * 1024]), after=[m6, vt, d_nob])
                    w2 = P.op("v", lambda v, half=half: v.reduce_max(
                        CL[:, 4 + half:5 + half], QSz[:], axis=AXX,
                        apply_absolute_value=True), after=[w1])
                    hv.append(w2)
                    vt = w2
                w3 = P.op("v", lambda v: v.tensor_max(
                    CL[:, 4:5], CL[:, 4:5], CL[:, 5:6]), after=[hv[0], hv[1]])
                w4 = P.op("v", lambda v: v.tensor_scalar(
                    CL[:, 5:6], CL[:, 4:5], CL[:, 10:11], 1e-5, ALU.mult, ALU.max),
                    after=[w3])
                w5 = P.op("v", lambda v: v.reciprocal(CL[:, 6:7], CL[:, 5:6]),
                          after=[w4])
                w6 = P.op("v", lambda v: v.scalar_tensor_tensor(
                    CL[:, 7:8], CL[:, 6:7], 127.0, CL[:, 10:11], ALU.mult, ALU.mult),
                    after=[w5])
                w7 = P.op("v", lambda v: v.tensor_scalar(
                    CL[:, 8:9], CL[:, 5:6], 1.0 / 127, None, ALU.mult), after=[w6])
                vt = w7
                ytoks = [None, None]
                for half in range(2):
                    en = "v"
                    Zb, Ib, Tb, Db = (QSz, QSi, QSt, QSd) if half == 0 else (Q2z, Q2i, Q2t, Q2d)
                    ch = vt
                    y1 = P.op(en, lambda v, half=half, Zb=Zb: v.tensor_mul(
                        Zb[:], OG[:, half * 1024:(half + 1) * 1024],
                        NOB[:, half * 1024:(half + 1) * 1024]),
                        after=[ch, w7, zb_free[half]])
                    y2 = P.op("s", lambda s, Zb=Zb: s.activation(
                        Zb[:], Zb[:], AF.Identity, scale=CL[:, 7:8]),
                        after=[y1, stt])
                    stt = y2
                    y7_, sy = _round_fix(P, en, Zb, Ib, Tb, Db, Zb, y2, stt)
                    stt = sy
                    y8 = P.op(en, lambda v, Zb=Zb, Tb=Tb: v.tensor_scalar(
                        Zb[:], Tb[:], 1.0, CL[:, 8:9], ALU.subtract, ALU.mult),
                        after=[y7_])
                    ytoks[half] = y8
                    vt = y8
                    otp = []
                    Zb2 = QSz if half == 0 else Q2z
                    for c in range(NC):
                        bank = 5 + c // 4
                        cc = c % 4
                        a = [ytoks[half], bank_free[bank]] if (c % 4 == 0) else []
                        otp.append(P.op("t", lambda t, c=c, bank=bank, cc=cc, Zb2=Zb2:
                                        t.transpose(
                            PSB[bank][:, cc * 128:(cc + 1) * 128],
                            Zb2[:, c * 128:(c + 1) * 128], IDB[:]), after=a))
                    zb_free[half] = otp[-1]
                    for hh in range(2):
                        bank = 5 + hh
                        ev = P.op("v", lambda v, half=half, hh=hh, bank=bank: v.tensor_copy(
                            OQF[:, half * 8 + hh * 4:half * 8 + (hh + 1) * 4, :],
                            PSB[bank][:, 0:512].rearrange("p (c n) -> p c n", c=4)),
                            after=[otp[hh * 4 + 3], vt])
                        vt = ev
                        bank_free[bank] = ev
                oq_tok = vt

                WLv = WL[:, 0:8192].rearrange("p (c n) -> p c n", c=8)
                mm = None
                for rnd in range(2):
                    d_wo = P.dma("g", WLv[:, :, :],
                                 qscr["owT"][:, 0, rnd * 8:(rnd + 1) * 8, :],
                                 after=[wl_free, mm], new=True)
                    for jj in range(NC):
                        bank = jj // 4
                        cc = jj % 4
                        for i in range(NC):
                            mm = P.op("t", lambda t, i=i, jj=jj, rnd=rnd, bank=bank,
                                      cc=cc: t.matmul(
                                PS[bank][:, cc * 128:(cc + 1) * 128],
                                WLv[:, i, jj * 128:(jj + 1) * 128],
                                OQF[:, rnd * 8 + i, :],
                                start=(rnd == 0 and i == 0 and jj % 4 == 0),
                                stop=(rnd == 1 and i == 7 and jj % 4 == 3)),
                                after=[d_wo, oq_tok, bank_free[bank]]
                                if (jj % 4 == 0 and i == 0) else [])
                    wl_free = mm
                for jj in range(NC):
                    bank = jj // 4
                    cc = jj % 4
                    ev = P.op("v", lambda v, jj=jj, cn=cn, t0=t0, bank=bank, cc=cc:
                              v.scalar_tensor_tensor(
                        XA[:, jj, t0 + cn * 128:t0 + (cn + 1) * 128],
                        PS[bank][:, cc * 128:(cc + 1) * 128],
                        SCL[:, QWI["owT"]:QWI["owT"] + 1],
                        XA[:, jj, t0 + cn * 128:t0 + (cn + 1) * 128],
                        ALU.mult, ALU.add), after=[mm, vt])
                    vt = ev
                bank_free[0] = vt
                bank_free[1] = vt
            x2_free = vt
        return dict(v=P.tok("v"), t=P.tok("t"), s=P.tok("s"))



def _round_fix(P, en, Zb, Ib, Tb, Db, M, prev, sprev=None):
    o1 = P.op("s", lambda s, Zb=Zb, Ib=Ib: s.activation(Ib[:], Zb[:], AF.Identity),
              after=[prev, sprev])
    o2 = P.op("s", lambda s, Ib=Ib, Tb=Tb: s.activation(Tb[:], Ib[:], AF.Identity),
              after=[o1])
    o3 = P.op("v", lambda v, Zb=Zb, Tb=Tb, Db=Db: v.tensor_sub(Db[:], Zb[:], Tb[:]),
              after=[o2, prev])
    o4 = P.op("v", lambda v, Tb=Tb, Db=Db: v.scalar_tensor_tensor(
        Tb[:], Db[:], 0.5, Tb[:], ALU.is_ge, ALU.add), after=[o3])
    o5 = P.op("v", lambda v, Tb=Tb, Db=Db: v.scalar_tensor_tensor(
        Tb[:], Db[:], -0.5, Tb[:], ALU.is_gt, ALU.add), after=[o4])
    return o5, o2


def _act_quant_fm(P, flat, UB, WB, QSV, QSG, vt, gt, ready=None, stt=None):
    st = [stt]
    """Feature-major act-quant of flat [128, 32*512] in place; pieces alternate
    vector/gpsimd. QSV/QSG = (QI, QT, QD) scratch per engine. Returns (vt, gt)."""
    piece = 1024
    for pc in range(16384 // piece):
        o0 = pc * piece
        en = "v"
        QI, QT, QD, QM = QSV if pc % 2 == 0 else QSG
        ch = vt
        q1 = P.op(en, lambda v, o0=o0: v.tensor_mul(
            flat[:, o0:o0 + piece], flat[:, o0:o0 + piece], UB[:, 0:piece]),
            after=[ch, ready if pc < 2 else None])
        q6, s6 = _round_fix(P, en, flat[:, o0:o0 + piece], QI[:, 0:piece],
                            QT[:, 0:piece], QD[:, 0:piece], QM[:, 0:piece], q1, st[0])
        st[0] = s6
        q7 = P.op(en, lambda v, QT=QT: v.tensor_scalar(
            QT[:, 0:piece], QT[:, 0:piece], 1.0, None, ALU.subtract), after=[q6])
        q8 = P.op(en, lambda v, o0=o0, QT=QT: v.tensor_mul(
            flat[:, o0:o0 + piece], QT[:, 0:piece], WB[:, 0:piece]), after=[q7])
        vt = q8
    return vt, gt, st[0]


def stage_l1ffn(ctx, toks):
    P, sb = ctx["P"], ctx["sb"]
    XA, IDF, IDB, PS, STAT, SCL = (ctx["XA"], ctx["IDF"], ctx["IDB"],
                                   ctx["PS"], ctx["STAT"], ctx["SCL"])
    din, qscr, PSB = ctx["din"], ctx["qscr"], ctx["PSB"]

    with ExitStack() as st_:
        XQ3 = sb("b_xq3", [128, 8, 512], BF16, st_)
        H0 = sb("b_h0", [128, 32, 514], BF16, st_)
        HC = sb("b_hc", [128, 32, 512], BF16, st_)
        TL = sb("b_tl", [128, 32, 2], BF16, st_)
        FWJ = [sb(f"b_wj{i}", [128, 16, 128], BF16, st_) for i in range(3)]
        QZ = sb("b_qz", [128, 1024], BF16, st_)
        QI = sb("b_qi", [128, 1024], mybir.dt.int32, st_)
        QT = sb("b_qt", [128, 1024], BF16, st_)
        QD = sb("b_qd", [128, 1024], BF16, st_)
        Q2z = sb("b_q2z", [128, 1024], BF16, st_)
        Q2i = sb("b_q2i", [128, 1024], mybir.dt.int32, st_)
        Q2t = sb("b_q2t", [128, 1024], BF16, st_)
        Q2d = sb("b_q2d", [128, 1024], BF16, st_)
        CL2 = sb("b_cl2", [128, 16], F32, st_)
        MSKF = [sb(f"b_msk{i}", [128, 1024], BF16, st_) for i in range(2)]
        UB = sb("b_ub", [128, 1024], BF16, st_)
        WB = sb("b_wb", [128, 1024], BF16, st_)
        CWC = sb("b_cw", [128, 32, 3], F32, st_)
        CBC = sb("b_cb", [128, 32], F32, st_)
        CL = sb("b_cl", [128, 16], F32, st_)
        MX = sb("b_mx", [128, 8], F32, st_)
        UR = sb("b_ur", [1, 512], F32, st_)
        WR = sb("b_wr", [1, 512], F32, st_)
        ONE1 = sb("b_one", [1, 128], F32, st_)

        P.barrier()
        d_cw = P.dma("y", CWC[:], din["cw"], new=True)
        d_cb = P.dma("y", CBC[:], din["cb"])
        o1 = P.op("v", lambda v: v.memset(ONE1[:], 1.0))
        vt = o1
        gt = None
        stt = None
        zb_free = [None, None]
        wj_free = [None, None, None]
        bank_free = [None] * 8
        wi = 0
        sc1 = SCL[:, QWI["f1T"]:QWI["f1T"] + 1]
        sc2 = SCL[:, QWI["f2T"]:QWI["f2T"] + 1]

        def fwload(nm, j, half, after=()):
            nonlocal wi
            slot = wi % 3
            if half is None:
                dst, s_ = FWJ[slot][:, 0:8, :], qscr[nm][j]
            else:
                dst = FWJ[slot][:, 0:16, :]
                s_ = qscr[nm][j][:, half * 16:half * 16 + 16, :]
            tok = P.dma("g", dst, s_, after=[wj_free[slot]] + list(after), new=True)
            wi += 1
            return slot, tok

        for q in range(4):
            t0 = q * 512
            if q > 0:
                tl1 = P.op("v", lambda v: v.tensor_copy(TL[:], H0[:, :, 512:514]),
                           after=[vt])
                vt = tl1
            else:
                tl1 = None
            for tt in range(4):
                en = "v"
                Zb, Ib, Tb, Db = (QZ, QI, QT, QD) if tt % 2 == 0 else (Q2z, Q2i, Q2t, Q2d)
                CLx = CL if tt % 2 == 0 else CL2
                ch = vt
                tps = []
                for c in range(NC):
                    bank = 5 + c // 4
                    cc = c % 4
                    a = [bank_free[bank]] if (c % 4 == 0) else []
                    tps.append(P.op("t", lambda t, tt=tt, c=c, t0=t0, bank=bank, cc=cc:
                                    t.transpose(PS[bank][:, cc * 128:(cc + 1) * 128],
                                    XA[:, c, t0 + tt * 128:t0 + (tt + 1) * 128],
                                    IDF[:]), after=a + ([toks.get("v"), toks.get("s")]
                                    if q == 0 and tt == 0 and c == 0 else [])))
                e1 = P.op("v", lambda v, Zb=Zb: v.tensor_copy(
                    Zb[:, 0:512], PS[5][:, :]), after=[tps[3], vt, zb_free[tt % 2]])
                e2 = P.op("v", lambda v, Zb=Zb: v.tensor_copy(
                    Zb[:, 512:1024], PS[6][:, :]), after=[tps[7], e1])
                vt = e2
                bank_free[5] = e1
                bank_free[6] = e2
                a1 = P.op("v", lambda v, Zb=Zb, CLx=CLx: v.reduce_max(
                    CLx[:, 0:1], Zb[:], axis=AXX, apply_absolute_value=True),
                    after=[e2])
                a2 = P.op("v", lambda v, CLx=CLx: v.tensor_scalar(
                    CLx[:, 1:2], CLx[:, 0:1], 1e-5, None, ALU.max), after=[a1])
                a3 = P.op("v", lambda v, CLx=CLx: v.reciprocal(CLx[:, 2:3], CLx[:, 1:2]),
                          after=[a2])
                a4 = P.op("v", lambda v, CLx=CLx: v.tensor_scalar(
                    CLx[:, 3:4], CLx[:, 2:3], 127.0, None, ALU.mult), after=[a3])
                a5 = P.op("v", lambda v, CLx=CLx: v.tensor_scalar(
                    CLx[:, 4:5], CLx[:, 1:2], 1.0 / 127, None, ALU.mult), after=[a4])
                vt = a5
                b1 = P.op("s", lambda s, Zb=Zb, CLx=CLx: s.activation(
                    Zb[:], Zb[:], AF.Identity, scale=CLx[:, 3:4]),
                    after=[a5, ch, stt])
                stt = b1
                b6, sb6 = _round_fix(P, en, Zb, Ib, Tb, Db, MSKF[tt % 2], b1, stt)
                stt = sb6
                b7 = P.op(en, lambda v, Zb=Zb, Tb=Tb, CLx=CLx: v.tensor_scalar(
                    Zb[:], Tb[:], 1.0, CLx[:, 4:5], ALU.subtract, ALU.mult), after=[b6])
                vt = b7
                xtp = []
                for c in range(NC):
                    bank = 5 + c // 4
                    cc = c % 4
                    a = [b7, bank_free[bank]] if (c % 4 == 0) else []
                    xtp.append(P.op("t", lambda t, c=c, bank=bank, cc=cc, Zb=Zb:
                                    t.transpose(
                        PSB[bank][:, cc * 128:(cc + 1) * 128],
                        Zb[:, c * 128:(c + 1) * 128], IDB[:]), after=a))
                zb_free[tt % 2] = xtp[-1]
                for half in range(2):
                    bank = 5 + half
                    ev = P.op("v", lambda v, tt=tt, half=half, bank=bank: v.tensor_copy(
                        XQ3[:, half * 4:(half + 1) * 4, tt * 128:(tt + 1) * 128],
                        PSB[bank][:, 0:512].rearrange("p (c n) -> p c n", c=4)),
                        after=[xtp[half * 4 + 3], vt])
                    vt = ev
                    bank_free[bank] = ev
            xq_tok = vt
            for j in range(32):
                P.pump(1)
                slot, dtok = fwload("f1T", j, None)
                bank = j % 2
                mm = None
                for i in range(NC):
                    mm = P.op("t", lambda t, slot=slot, i=i, j=j, bank=bank: t.matmul(
                        PS[bank][:, :], FWJ[slot][:, i, :], XQ3[:, i, :],
                        start=(i == 0), stop=(i == NC - 1)),
                        after=[dtok, xq_tok, bank_free[bank]] if i == 0 else [])
                ev = P.op("s", lambda s, j=j, bank=bank: s.activation(
                    H0[:, j, 2:514], PS[bank][:, :], AF.Identity, scale=sc1),
                    after=[mm, stt, tl1])
                stt = ev
                bank_free[bank] = ev
                wj_free[slot] = mm
            h0_tok = stt
            if q == 0:
                tz = P.op("v", lambda v: v.memset(H0[:, :, 0:2], 0.0), after=[vt, h0_tok])
            else:
                tz = P.op("v", lambda v: v.tensor_copy(H0[:, :, 0:2], TL[:]),
                          after=[vt, h0_tok, tl1])
            vt = tz

            for j in range(32):
                c1 = P.op("v", lambda v, j=j: v.tensor_single_scalar(
                    QZ[:, 0:512], H0[:, j, 0:512], CWC[:, j, 0:1], ALU.mult),
                    after=[tz, vt, stt, d_cw, h0_tok, zb_free[0]])
                c2 = P.op("v", lambda v, j=j: v.scalar_tensor_tensor(
                    QZ[:, 0:512], H0[:, j, 1:513], CWC[:, j, 1:2], QZ[:, 0:512],
                    ALU.mult, ALU.add), after=[c1])
                c3 = P.op("v", lambda v, j=j: v.scalar_tensor_tensor(
                    QZ[:, 0:512], H0[:, j, 2:514], CWC[:, j, 2:3], QZ[:, 0:512],
                    ALU.mult, ALU.add), after=[c2])
                c4 = P.op("s", lambda s, j=j: s.activation(
                    HC[:, j, :], QZ[:, 0:512], AF.Silu, bias=CBC[:, j:j + 1]),
                    after=[c3, stt, d_cb])
                stt = c4
                vt = c3
            hc_tok = stt

            wlo = 1 if q == 0 else 0
            mzero = P.op("v", lambda v: v.memset(MX[:, 0:4], 0.0), after=[vt])
            vt = mzero
            for tt in range(4):
                for grp in range(8):
                    tps = []
                    for cc in range(4):
                        c = grp * 4 + cc
                        bank = 5
                        a = [hc_tok, bank_free[5]] if cc == 0 else []
                        tps.append(P.op("t", lambda t, c=c, tt=tt, cc=cc: t.transpose(
                            PSB[5][:, cc * 128:(cc + 1) * 128], HC[:, c, tt * 128:(tt + 1) * 128],
                            IDB[:]), after=a))
                    r1 = P.op("v", lambda v: v.reduce_max(
                        CL[:, 0:1], PSB[5][:, 0:512], axis=AXX, apply_absolute_value=True),
                        after=[tps[3], vt])
                    r2 = P.op("v", lambda v, tt=tt: v.tensor_max(
                        MX[:, tt:tt + 1], MX[:, tt:tt + 1], CL[:, 0:1]), after=[r1])
                    vt = r2
                    bank_free[5] = r1
            s1_ = P.op("v", lambda v: v.tensor_scalar(
                MX[:, 4:8], MX[:, 0:4], 1e-5, None, ALU.max), after=[vt])
            s2_ = P.op("v", lambda v: v.reciprocal(MX[:, 0:4], MX[:, 4:8]), after=[s1_])
            s3_ = P.op("v", lambda v: v.tensor_scalar(
                MX[:, 0:4], MX[:, 0:4], 127.0, None, ALU.mult), after=[s2_])
            s4_ = P.op("v", lambda v: v.tensor_scalar(
                MX[:, 4:8], MX[:, 4:8], 1.0 / 127, None, ALU.mult), after=[s3_])
            vt = s4_
            for which, COLS, ROW in ((0, MX[:, 0:4], UR), (1, MX[:, 4:8], WR)):
                for tt in range(4):
                    tpu = P.op("t", lambda t, COLS=COLS, tt=tt: t.transpose(
                        PS[6][0:1, 0:128], COLS[:, tt:tt + 1], IDF[:]),
                        after=[vt, s4_, bank_free[6]])
                    evu = P.op("v", lambda v, ROW=ROW, tt=tt: v.tensor_copy(
                        ROW[:, tt * 128:(tt + 1) * 128], PS[6][0:1, 0:128]),
                        after=[tpu, vt])
                    vt = evu
                    bank_free[6] = evu
                bmm = P.op("t", lambda t, ROW=ROW: t.matmul(
                    PS[6][:, :], ONE1[:], ROW[:, 0:512],
                    start=True, stop=True), after=[vt, bank_free[6], o1])
                bev1 = P.op("v", lambda v, which=which: v.tensor_copy(
                    (UB if which == 0 else WB)[:, 0:512], PS[6][:, :]),
                    after=[bmm, vt])
                bev = P.op("v", lambda v, which=which: v.tensor_copy(
                    (UB if which == 0 else WB)[:, 512:1024], PS[6][:, :]),
                    after=[bev1])
                vt = bev
                bank_free[6] = bev
            vt, gt, stt = _act_quant_fm(P, HC[:, :, :].rearrange("p c n -> p (c n)"),
                                        UB, WB, (QI, QT, QD, MSKF[0]),
                                        (Q2i, Q2t, Q2d, MSKF[1]), vt, gt,
                                        ready=vt, stt=stt)
            hq_tok = (vt, gt)

            for j in range(NC):
                P.pump(1)
                stoks = [None, None]
                slots = [None, None]
                for half in range(2):
                    slots[half], stoks[half] = fwload("f2T", j, half)
                bank = 2 + (j % 2)
                mm = None
                for ihalf in range(2):
                    for i2 in range(16):
                        i = ihalf * 16 + i2
                        mm = P.op("t", lambda t, s_=slots[ihalf], i2=i2, i=i, bank=bank:
                                  t.matmul(
                            PS[bank][:, :], FWJ[s_][:, i2, :],
                            HC[:, i, :], start=(i == 0), stop=(i == 31)),
                            after=[stoks[0], stoks[1], hq_tok[0], hq_tok[1],
                                   bank_free[bank]] if i == 0 else [])
                ev = P.op("v", lambda v, j=j, t0=t0, wlo=wlo, bank=bank:
                          v.scalar_tensor_tensor(
                    XA[:, j, t0 - 1 + wlo:t0 + 511], PS[bank][:, wlo:512], sc2,
                    XA[:, j, t0 - 1 + wlo:t0 + 511], ALU.mult, ALU.add),
                    after=[mm, vt])
                vt = ev
                bank_free[bank] = ev
                wj_free[slots[0]] = mm
                wj_free[slots[1]] = mm

        ee1 = P.op("v", lambda v: v.tensor_mul(
            QZ[:, 0:32], H0[:, :, 512:513].rearrange("p c n -> p (c n)"),
            CWC[:, :, 0]), after=[vt])
        ee2 = P.op("v", lambda v: v.tensor_mul(
            QZ[:, 32:64], H0[:, :, 513:514].rearrange("p c n -> p (c n)"),
            CWC[:, :, 1]), after=[ee1])
        ee3 = P.op("v", lambda v: v.tensor_add(QZ[:, 0:32], QZ[:, 0:32], QZ[:, 32:64]),
                   after=[ee2])
        ee3b = P.op("v", lambda v: v.tensor_add(QZ[:, 0:32], QZ[:, 0:32], CBC[:]),
                    after=[ee3])
        ee4 = P.op("s", lambda s: s.activation(
            QZ[:, 0:32], QZ[:, 0:32], AF.Silu), after=[ee3b, stt])
        stt = ee4
        et1 = P.op("t", lambda t: t.transpose(PSB[5][0:32, 0:128], QZ[:, 0:32], IDB[:]),
                   after=[ee4, bank_free[5]])
        et2 = P.op("v", lambda v: v.reduce_max(
            CL[0:32, 0:1], PSB[5][0:32, 0:128], axis=AXX, apply_absolute_value=True),
            after=[et1, vt])
        et3 = P.op("t", lambda t: t.transpose(PS[6][0:1, 0:32], CL[0:32, 0:1],
                                              IDF[0:32, 0:32]), after=[et2, bank_free[6]])
        et4 = P.op("v", lambda v: v.reduce_max(
            CL[0:1, 1:2], PS[6][0:1, 0:32], axis=AXX), after=[et3])
        et5 = P.op("v", lambda v: v.tensor_scalar(
            CL[0:1, 2:3], CL[0:1, 1:2], 1e-5, None, ALU.max), after=[et4])
        et6 = P.op("v", lambda v: v.reciprocal(CL[0:1, 3:4], CL[0:1, 2:3]), after=[et5])
        et7 = P.op("v", lambda v: v.tensor_scalar(
            CL[0:1, 4:5], CL[0:1, 3:4], 127.0, None, ALU.mult), after=[et6])
        et8 = P.op("v", lambda v: v.tensor_scalar(
            CL[0:1, 5:6], CL[0:1, 2:3], 1.0 / 127, None, ALU.mult), after=[et7])
        eb1 = P.op("t", lambda t: t.matmul(PS[6][:, 0:2], ONE1[0:1, :],
                                           CL[0:1, 4:6], start=True, stop=True),
                   after=[et8, bank_free[6]])
        eb2 = P.op("v", lambda v: v.tensor_copy(CL[:, 8:10], PS[6][:, 0:2]),
                   after=[eb1])
        bank_free[6] = eb2
        bank_free[5] = et2
        f1 = P.op("v", lambda v: v.tensor_single_scalar(
            QZ[:, 0:32], QZ[:, 0:32], CL[:, 8:9], ALU.mult), after=[eb2])
        f2 = P.op("v", lambda v: v.tensor_copy(QI[:, 0:32], QZ[:, 0:32]), after=[f1])
        f3 = P.op("v", lambda v: v.tensor_copy(QT[:, 0:32], QI[:, 0:32]), after=[f2])
        f4 = P.op("v", lambda v: v.tensor_sub(QD[:, 0:32], QZ[:, 0:32], QT[:, 0:32]),
                  after=[f3])
        f5 = P.op("v", lambda v: v.scalar_tensor_tensor(
            QT[:, 0:32], QD[:, 0:32], 0.5, QT[:, 0:32], ALU.is_ge, ALU.add), after=[f4])
        f6 = P.op("v", lambda v: v.scalar_tensor_tensor(
            QT[:, 0:32], QD[:, 0:32], -0.5, QT[:, 0:32], ALU.is_gt, ALU.add), after=[f5])
        f7 = P.op("v", lambda v: v.tensor_scalar(
            QZ[:, 0:32], QT[:, 0:32], 1.0, CL[:, 9:10], ALU.subtract, ALU.mult),
            after=[f6])
        vt = f7
        stoks = [None, None]
        slots = [None, None]
        for half in range(2):
            slots[half], stoks[half] = fwload("f2T", 0, half)
        for j in range(NC):
            if j > 0:
                for half in range(2):
                    slots[half], stoks[half] = fwload("f2T", j, half)
            mm = None
            for ihalf in range(2):
                for i2 in range(16):
                    i = ihalf * 16 + i2
                    mm = P.op("t", lambda t, s_=slots[ihalf], i2=i2, i=i, j=j: t.matmul(
                        PS[2][:, 0:1], FWJ[s_][:, i2, :], QZ[:, i:i + 1],
                        start=(i == 0), stop=(i == 31)),
                        after=[stoks[0], stoks[1], f7, bank_free[2]] if i == 0 else [])
            ev = P.op("v", lambda v, j=j: v.scalar_tensor_tensor(
                XA[:, j, 2047:2048], PS[2][:, 0:1], sc2, XA[:, j, 2047:2048],
                ALU.mult, ALU.add), after=[mm, vt])
            vt = ev
            bank_free[2] = ev
            wj_free[slots[0]] = mm
            wj_free[slots[1]] = mm
        return dict(v=P.tok("v"), t=P.tok("t"), s=P.tok("s"))


def stage_inputs(inputs):
    import ml_dtypes
    g = {k: np.asarray(v) for k, v in inputs.items()}
    sh = {}
    sh["idf"] = np.eye(128, dtype=np.float32)
    sh["idb"] = np.eye(128, dtype=np.float32).astype(ml_dtypes.bfloat16)
    m = np.zeros((128, 128), np.float32)
    m[np.triu_indices(128)] = 1.0
    sh["triu"] = m
    sh["ones"] = np.ones((128, 128), np.float32)
    sh["tmk"] = _colmajor(g["L0_attn_time_mix_k"])
    sh["tmv"] = _colmajor(g["L0_attn_time_mix_v"])
    sh["tmr"] = _colmajor(g["L0_attn_time_mix_r"])
    sh["td"] = _colmajor(g["L0_attn_time_decay"])
    sh["tf"] = _colmajor(g["L0_attn_time_first"])
    sh["wkT"] = _tr(g["L0_attn_Wk"]); sh["wvT"] = _tr(g["L0_attn_Wv"])
    sh["wrT"] = _tr(g["L0_attn_Wr"]); sh["woT"] = _tr(g["L0_attn_Wo"])
    sh["w1R"] = _jblk(g["L0_ffn_w1_w"]); sh["w2R"] = _jblk(g["L0_ffn_w2_w"])
    sh["h1R"] = _jblk(g["L0_ffn_h1_w"]); sh["h2R"] = _jblk(g["L0_ffn_h2_w"])
    sh["w3R"] = _jblk(g["L0_ffn_w3_w"])
    sh["s1"] = _colmajor(g["L0_ffn_w1_s"]); sh["s2"] = _colmajor(g["L0_ffn_w2_s"])
    sh["sh1"] = _colmajor(g["L0_ffn_h1_s"]); sh["sh2"] = _colmajor(g["L0_ffn_h2_s"])
    sh["s3"] = _colmajor(g["L0_ffn_w3_s"])
    sh["qwT"] = _tr(g["L1_attn_wq"]); sh["kwT"] = _tr(g["L1_attn_wk"])
    sh["vwT"] = _tr(g["L1_attn_wv"]); sh["gwT"] = _tr(g["L1_attn_wg"])
    sh["gk1wT"] = _tr(g["L1_attn_wgk1"]); sh["gk2wT"] = _tr(g["L1_attn_wgk2"])
    sh["owT"] = _tr(g["L1_attn_wo"])
    sh["f1T"] = _tr(g["L1_ffn_w1"]); sh["f2T"] = _tr(g["L1_ffn_w2"])
    sh["nqB"] = _bcast(g["L1_attn_nq"]); sh["nkB"] = _bcast(g["L1_attn_nk"])
    sh["nvB"] = _bcast(g["L1_attn_nv"]); sh["ngB"] = _bcast(g["L1_attn_ng"])
    sh["ngk1B"] = _bcast(g["L1_attn_ngk1"]); sh["noB"] = _bcast(g["L1_attn_no"])
    sh["ngk2c"] = np.ascontiguousarray(np.asarray(g["L1_attn_ngk2"], np.float32)[:, None])
    sh["ngk2B"] = _bcast(g["L1_attn_ngk2"])
    sh["bgk2c"] = _colmajor(g["L1_attn_bgk2"])
    sh["cw"] = np.ascontiguousarray(np.asarray(g["L1_ffn_conv_w"], np.float32)
                                    .reshape(DH, 3).reshape(32, 128, 3).transpose(1, 0, 2))
    sh["cb"] = _colmajor(g["L1_ffn_conv_b"])
    sh["cb32"] = sh["cb"]

    x = np.asarray(g["L0_attn_x"], np.float32)
    in_maps = []
    for b in range(B):
        im = dict(sh)
        im["xT"] = np.ascontiguousarray(x[b].T)
        in_maps.append(im)
    return in_maps


def kernel(**inputs):
    in_maps = stage_inputs(inputs)
    nc = build(upto="all", tap=None)
    res = run_bass_kernel_spmd(nc, in_maps, core_ids=list(range(B)))
    return np.stack([np.asarray(r["out"], np.float32) for r in res.results], axis=0)


# version 10
# speedup vs baseline: 1.0262x; 1.0262x over previous
import numpy as np
from contextlib import ExitStack

import concourse.bass as bass
import concourse.mybir as mybir
import concourse.bass_isa as bass_isa
from concourse.bass_utils import run_bass_kernel_spmd

F32 = mybir.dt.float32
BF16 = mybir.dt.bfloat16
AF = mybir.ActivationFunctionType
ALU = mybir.AluOpType
AXX = mybir.AxisListType.X
RADD = bass_isa.ReduceOp.add

B, T, C = 8, 2048, 1024
NC = 8
DH = 4096
NTT = T // 128

ENGS = ("t", "v", "s", "g", "y")
DMACH = {"g": "dG", "y": "dY", "s": "dS"}
NROT = 6


class Prog:

    def __init__(self, nc, sems):
        self.nc = nc
        self.sems = sems
        self.ops = {e: [] for e in ENGS}
        self.ncomp = {e: 0 for e in ENGS}
        self.need = {e: set() for e in ENGS}
        self.dmacnt = {}
        for q in DMACH.values():
            for i in range(NROT):
                self.dmacnt[q + str(i)] = 0
        self.dmarot = {q: 0 for q in DMACH.values()}
        self.dmagroup_open = {q: False for q in DMACH.values()}
        self.group_ends = {ch: [] for ch in self.dmacnt}
        self.pending = {e: [] for e in ENGS}
        self.bg = []

    def tok(self, e):
        return (e, self.ncomp[e])

    def _waits(self, e, after):
        w = []
        for t in list(after) + self.pending[e]:
            if t is None:
                continue
            kind, val = t
            if val <= 0:
                continue
            w.append((kind, val))
            if kind in ENGS:
                self.need[kind].add(val)
        self.pending[e] = []
        return w

    def op(self, e, fn, after=(), note=""):
        self.ops[e].append(dict(fn=fn, waits=self._waits(e, after), dma=None,
                                comp=self.ncomp[e], note=note))
        self.ncomp[e] += 1
        return (e, self.ncomp[e])

    def dma(self, e, out, in_, after=(), note="", new=False):
        q = DMACH[e]
        if new or not self.dmagroup_open[q]:
            cur = q + str(self.dmarot[q])
            if self.dmacnt[cur] > 0 and (not self.group_ends[cur]
                                         or self.group_ends[cur][-1] != self.dmacnt[cur]):
                self.group_ends[cur].append(self.dmacnt[cur])
            self.dmarot[q] = (self.dmarot[q] + 1) % NROT
            ch = q + str(self.dmarot[q])
            if self.dmacnt[ch] > 0:
                self.pending[e].append((ch, self.dmacnt[ch]))
            self.dmagroup_open[q] = True
        ch = q + str(self.dmarot[q])
        def fn(eng, out=out, in_=in_):
            return eng.dma_start(out=out, in_=in_)
        self.ops[e].append(dict(fn=fn, waits=self._waits(e, after), dma=ch,
                                comp=None, note=note))
        self.dmacnt[ch] += 16
        return (ch, self.dmacnt[ch])

    def dma_new(self, e):
        self.dmagroup_open[DMACH[e]] = False

    def barrier(self, engines=("t", "v", "s", "g", "y")):
        toks = {e: self.tok(e) for e in engines}
        for e in engines:
            for o in engines:
                if o != e:
                    self.pending[e].append(toks[o])

    def pump(self, n=1):
        for _ in range(n):
            if self.bg:
                self.bg.pop(0)()

    def pump_all(self):
        while self.bg:
            self.bg.pop(0)()

    def finalize(self, blk):
        import bisect
        for ch, tot in self.dmacnt.items():
            if tot > 0 and (not self.group_ends[ch] or self.group_ends[ch][-1] != tot):
                self.group_ends[ch].append(tot)
        for e in ENGS:
            for rec in self.ops[e]:
                w2 = []
                for kind, val in rec["waits"]:
                    if kind in self.group_ends:
                        ge = self.group_ends[kind]
                        val = ge[bisect.bisect_left(ge, val)]
                    w2.append((kind, val))
                rec["waits"] = w2
        incs = {e: {} for e in ENGS}
        for e in ENGS:
            need = sorted(self.need[e])
            if need and need[-1] > self.ncomp[e]:
                raise RuntimeError(f"wait beyond issued ops on {e}")
            full, prev = [], 0
            for p in need:
                while p - prev > 32:
                    prev += 32
                    full.append(prev)
                if p != prev:
                    full.append(p)
                    prev = p
            comp2idx = {}
            for i, rec in enumerate(self.ops[e]):
                if rec["comp"] is not None:
                    comp2idx[rec["comp"]] = i
            prev = 0
            for p in full:
                incs[e][comp2idx[p - 1]] = p - prev
                prev = p

        prog = self

        def make_body(e):
            def body(eng):
                seen = {}
                for i, rec in enumerate(prog.ops[e]):
                    for kind, val in rec["waits"]:
                        if seen.get(kind, 0) >= val:
                            continue
                        eng.wait_ge(prog.sems[kind], val)
                        seen[kind] = val
                    inst = rec["fn"](eng)
                    if rec["dma"] is not None:
                        inst.then_inc(prog.sems[rec["dma"]], 16)
                    amt = incs[e].get(i)
                    if amt:
                        inst.then_inc(prog.sems[e], amt)
            return body

        blk.tensor(make_body("t"))
        blk.vector(make_body("v"))
        blk.scalar(make_body("s"))
        blk.gpsimd(make_body("g"))
        blk.sync(make_body("y"))



def _colmajor(v):
    v = np.asarray(v, np.float32)
    n = v.size // 128
    return np.ascontiguousarray(v.reshape(n, 128).T)


def _bcast(v):
    v = np.asarray(v, np.float32)
    return np.ascontiguousarray(np.broadcast_to(v[None, :], (128, v.size)))


def _tr(w):
    return np.ascontiguousarray(np.asarray(w, np.float32).T)


def _jblk(w):
    wT = np.asarray(w, np.float32).T
    di, do = wT.shape
    return np.ascontiguousarray(
        wT.reshape(di // 128, 128, do // 128, 128).transpose(2, 1, 0, 3))


QW = [
    ("qwT", C, C), ("kwT", C, C), ("vwT", C, 2048), ("gwT", C, 2048),
    ("gk1wT", C, 16), ("gk2wT", 16, C), ("owT", 2048, C),
    ("f1T", C, DH), ("f2T", DH, C),
]
QWI = {nm: i for i, (nm, _, _) in enumerate(QW)}


def build(upto="all", tap=None):
    nc = bass.Bass("TRN2", target_bir_lowering=False)

    din = {}
    def dram_in(name, shape, dt=F32):
        din[name] = nc.dram_tensor(name, list(shape), dt, kind="ExternalInput").ap()
        return din[name]

    xT = dram_in("xT", [C, T])
    dram_in("idf", [128, 128]); dram_in("idb", [128, 128], BF16)
    dram_in("triu", [128, 128])
    dram_in("ones", [128, 128])
    for nm in ("tmk", "tmv", "tmr", "td", "tf", "s3", "bgk2c", "cb32"):
        dram_in(nm, [128, NC] if nm not in ("cb32",) else [128, 32])
    dram_in("cb", [128, 32])
    for nm in ("wkT", "wvT", "wrT", "woT"):
        dram_in(nm, [C, C])
    dram_in("w1R", [32, 128, 8, 128]); dram_in("w2R", [32, 128, 8, 128])
    dram_in("h1R", [32, 128, 32, 128]); dram_in("h2R", [32, 128, 32, 128])
    dram_in("w3R", [8, 128, 32, 128])
    for nm in ("s1", "s2", "sh1", "sh2"):
        dram_in(nm, [128, 32])
    for nm, di, do in QW:
        dram_in(nm, [di, do])
    for nm in ("nqB", "nkB", "nvB", "ngB", "ngk1B"):
        dram_in(nm, [128, C])
    dram_in("noB", [128, 2048])
    dram_in("ngk2c", [16, 1])
    dram_in("ngk2B", [128, 16])
    dram_in("cw", [128, 32, 3])

    out = nc.dram_tensor("out", [T, C], F32, kind="ExternalOutput").ap()
    tap_t = None
    if tap is not None:
        tap_t = nc.dram_tensor("tap", list(tap[0]), tap[1],
                               kind="ExternalOutput").ap()

    QSHAPE = {
        "qwT": [128, 1, 8, 1024], "kwT": [128, 1, 8, 1024],
        "vwT": [128, 2, 8, 1024], "gwT": [128, 2, 8, 1024],
        "owT": [128, 1, 16, 1024],
        "f1T": [32, 128, 8, 128], "f2T": [8, 128, 32, 128],
        "gk1wT": [C, 16], "gk2wT": [16, C],
    }
    qscr = {}
    for nm, di, do in QW:
        qscr[nm] = nc.dram_tensor("q_" + nm, QSHAPE[nm], BF16, kind="Internal").ap()

    with ExitStack() as es:
        sems = {}
        semnames = list(ENGS) + [q + str(i) for q in DMACH.values() for i in range(NROT)]
        for e in semnames:
            sems[e] = es.enter_context(nc.semaphore("sem_" + e))
        P = Prog(nc, sems)

        def sb(name, shape, dt=F32, stack=None):
            return (stack or es).enter_context(nc.sbuf_tensor(name, list(shape), dt))

        XA = sb("XA", [128, NC, T])
        IDF = sb("c_idf", [128, 128])
        IDB = sb("c_idb", [128, 128], BF16)
        TRIU = sb("c_triu", [128, 128])
        ONES = sb("c_ones", [128, 128])
        SVEC = sb("SVEC", [128, 64])
        SCL = sb("SCL", [128, 16])
        STAT = sb("STAT", [128, 64])
        PS = [es.enter_context(nc.psum_tensor(f"ps{b}", [128, 512], F32))
              for b in range(8)]
        PSB = [PS[b][:, :].bitcast(BF16) for b in range(8)]

        P.dma("y", IDF[:], din["idf"])
        d_idb = P.dma("y", IDB[:], din["idb"])
        d_triu = P.dma("y", TRIU[:], din["triu"])
        d_ones = P.dma("y", ONES[:], din["ones"])
        P.dma("y", SVEC[:, 0:8], din["tmk"])
        P.dma("y", SVEC[:, 8:16], din["tmv"])
        d_tm = P.dma("y", SVEC[:, 16:24], din["tmr"])
        d_td = P.dma("y", SVEC[:, 24:32], din["td"])
        d_tf = P.dma("y", SVEC[:, 32:40], din["tf"])
        P.dma("y", SVEC[:, 40:48], din["s3"])
        d_sv = P.dma("y", SVEC[:, 48:56], din["bgk2c"])
        d_x = None
        for c in range(NC):
            d_x = P.dma("y", XA[:, c, :], xT[c * 128:(c + 1) * 128, :],
                        new=(c == 0))
        s_ew1 = P.op("s", lambda s: s.activation(SVEC[:, 24:32], SVEC[:, 24:32], AF.Exp),
                     after=[d_td])
        s_ew = P.op("s", lambda s: s.activation(SVEC[:, 24:32], SVEC[:, 24:32], AF.Exp,
                                                scale=-1.0), after=[s_ew1])
        s_eu = P.op("s", lambda s: s.activation(SVEC[:, 32:40], SVEC[:, 32:40], AF.Exp),
                    after=[d_tf])

        pre_stack = ExitStack()
        pre = dict(
            ins=[sb(f"pre_in{i}", [128, 1024], F32, pre_stack) for i in range(3)],
            outs=[sb(f"pre_out{i}", [128, 1024], BF16, pre_stack) for i in range(2)],
            acc=sb("pre_acc", [128, 1], F32, pre_stack),
            reds=[sb(f"pre_red{i}", [128, 1], F32, pre_stack) for i in range(2)],
            rs=sb("pre_rs", [128, 1], F32, pre_stack),
            cmp=sb("pre_cmp", [128, 1024], F32, pre_stack))
        _emit_prepass(P, pre, din, qscr, SCL, ONES, PS, d_ones)

        ctx = dict(P=P, nc=nc, sb=sb, es=es, XA=XA, IDF=IDF, IDB=IDB, TRIU=TRIU,
                   SVEC=SVEC, SCL=SCL, STAT=STAT, PS=PS, PSB=PSB, din=din, qscr=qscr,
                   xready=d_x, svec_ready=(s_ew, s_eu, d_sv, d_tm))

        ctx["out"] = out
        order = ["l0attn", "ln1", "l0ffn", "ln2", "l1attn", "ln3", "l1ffn", "ln4"]
        stop = upto if upto in order or upto in ("l0ffnA", "l0ffnB") else "ln4"
        toks = stage_l0attn(ctx)
        done = upto == "l0attn"
        if not done:
            toks = stage_ln(ctx, 1, toks)
            P.pump_all()
            pre_stack.close()
            done = upto == "ln1"
        else:
            P.pump_all()
            pre_stack.close()
        if not done and upto in ("l0ffnA", "l0ffnB"):
            toksA = stage_l0ffn(ctx, toks, debug_stop=upto[-1])
            P.barrier()
            tk = [P.tok(e) for e in ("t", "v", "s", "g")]
            H0d = toksA["H0"]
            last = None
            for jj in range(32):
                last = P.dma("y", tap_t[jj * 128:(jj + 1) * 128, :],
                             H0d[:, jj * 1024:(jj + 1) * 1024], after=tk,
                             new=(jj == 0))
            P.op("y", lambda y: y.nop(), after=[last])
            done = True
        if not done:
            toks = stage_l0ffn(ctx, toks)
            done = upto == "l0ffn"
        if not done:
            toks = stage_ln(ctx, 2, toks)
            done = upto == "ln2"
        if not done:
            toks = stage_l1attn(ctx, toks)
            done = upto == "l1attn"
        if not done:
            toks = stage_ln(ctx, 3, toks)
            done = upto == "ln3"
        if not done:
            toks = stage_l1ffn(ctx, toks)
            done = upto == "l1ffn"
        if not done:
            toks = stage_ln(ctx, 4, toks)
            P.op("y", lambda y: y.nop(), after=[toks["out_dma"]])
            done = True
        elif tap_t is not None and upto not in ("l0ffnA", "l0ffnB"):
            _tap_xa(P, tap_t, XA)

        P.pump_all()
        blk = es.enter_context(nc.Block())
        P.finalize(blk)
    return nc


def _tap_xa(P, tap_t, XA):
    P.barrier()
    tk = [P.tok(e) for e in ("t", "v", "s", "g")]
    last = None
    for c in range(NC):
        last = P.dma("y", tap_t[c * 128:(c + 1) * 128, :], XA[:, c, :], after=tk,
                     new=(c == 0))
    P.op("y", lambda y: y.nop(), after=[last])


def _emit_prepass(P, pre, din, qscr, SCL, ONES, PS, d_ones):
    state = {"g": None, "v": None, "s": None, "tread": None,
             "ld": [None, None, None], "qd": [None, None, None],
             "redf": [None, None]}
    NR = 3
    for nm, di, do in QW:
        wi = QWI[nm]
        src, dst = din[nm], qscr[nm]
        tiles = []
        for r0 in range(0, di, 128):
            rr = min(128, di - r0)
            for c0 in range(0, do, 1024):
                tiles.append((r0, rr, c0, min(1024, do - c0)))
        inv_n = 1.0 / float(di * do)

        def t_zero():
            state["g"] = P.op("g", lambda g: g.memset(pre["acc"][:], 0.0),
                              after=[state["g"]])
        P.bg.append(t_zero)

        for ti, tile in enumerate(tiles):
            def t_red(tile=tile, ti=ti, src=src):
                r0, rr, c0, cc = tile
                buf = pre["ins"][ti % NR]
                red = pre["reds"][ti % 2]
                ld = P.dma("y", buf[0:rr, 0:cc], src[r0:r0 + rr, c0:c0 + cc],
                           after=[state["ld"][ti % NR]], new=True)
                s1 = P.op("s", lambda s: s.activation(
                    buf[0:rr, 0:cc], buf[0:rr, 0:cc], AF.Abs,
                    accum_out=red[0:rr, :]), after=[ld, state["s"],
                                                    state["redf"][ti % 2]])
                state["s"] = s1
                state["ld"][ti % NR] = s1
                state["g"] = P.op("g", lambda g: g.tensor_add(
                    pre["acc"][0:rr, :], pre["acc"][0:rr, :], red[0:rr, :]),
                    after=[s1, state["g"]])
                state["redf"][ti % 2] = state["g"]
            P.bg.append(t_red)

        def t_scale(wi=wi, inv_n=inv_n):
            mm = P.op("t", lambda t: t.matmul(
                PS[7][:, 0:1], ONES[:], pre["acc"][:, :], start=True, stop=True),
                after=[state["g"], d_ones, state["tread"]])
            state["tread"] = P.op("v", lambda v: v.tensor_scalar(
                SCL[:, wi:wi + 1], PS[7][:, 0:1], inv_n, 1e-5,
                ALU.mult, ALU.max), after=[mm, state["v"]])
            state["v"] = state["tread"]
            state["g"] = state["tread"]

        P.bg.append(t_scale)

        def t_recip(wi=wi):
            state["v"] = P.op("v", lambda v: v.reciprocal(
                pre["rs"][:], SCL[:, wi:wi + 1]), after=[state["g"], state["v"]])
        P.bg.append(t_recip)

        for ti, tile in enumerate(tiles):
            def t_quant(tile=tile, ti=ti, src=src, dst=dst, nm=nm):
                r0, rr, c0, cc = tile
                buf = pre["ins"][ti % NR]
                obuf = pre["outs"][ti % 2]
                ld = P.dma("y", buf[0:rr, 0:cc], src[r0:r0 + rr, c0:c0 + cc],
                           after=[state["ld"][ti % NR], state["v"]], new=True)
                v1 = P.op("v", lambda v: v.tensor_single_scalar(
                    buf[0:rr, 0:cc], buf[0:rr, 0:cc], pre["rs"][0:rr, :], ALU.mult),
                    after=[ld, state["v"]])
                v2 = P.op("v", lambda v: v.tensor_scalar(
                    pre["cmp"][0:rr, 0:cc], buf[0:rr, 0:cc], -0.5, -1.0,
                    ALU.is_le, ALU.mult), after=[v1])
                v4 = P.op("v", lambda v: v.scalar_tensor_tensor(
                    obuf[0:rr, 0:cc], buf[0:rr, 0:cc], 0.5, pre["cmp"][0:rr, 0:cc],
                    ALU.is_ge, ALU.add), after=[v2, state["qd"][ti % 2]])
                state["v"] = v4
                state["ld"][ti % NR] = v4
                if nm in ("f1T", "f2T"):
                    j0 = c0 // 128
                    nj = cc // 128
                    dview = dst[j0:j0 + nj, :, r0 // 128, :].rearrange("a p n -> p a n")
                    sview = obuf[0:rr, 0:cc].rearrange("p (a n) -> p a n", a=nj)
                    state["qd"][ti % 2] = P.dma("g", dview, sview,
                                                after=[v4], new=True)
                elif nm in ("gk1wT", "gk2wT"):
                    state["qd"][ti % 2] = P.dma("g", dst[r0:r0 + rr, c0:c0 + cc],
                                                obuf[0:rr, 0:cc], after=[v4], new=True)
                else:
                    state["qd"][ti % 2] = P.dma(
                        "g", dst[:, c0 // 1024, r0 // 128, 0:cc],
                        obuf[0:rr, 0:cc], after=[v4], new=True)
            P.bg.append(t_quant)


def stage_l0attn(ctx):
    P, sb, es = ctx["P"], ctx["sb"], ctx["es"]
    XA, SVEC, PS, din = ctx["XA"], ctx["SVEC"], ctx["PS"], ctx["din"]
    s_ew, s_eu, d_sv, d_tm = ctx["svec_ready"]
    xready = ctx["xready"]

    with ExitStack() as st_:
        WK = sb("l0_wk", [128, NC, C], BF16, st_)
        WV = sb("l0_wv", [128, NC, C], BF16, st_)
        WR = sb("l0_wr", [128, NC, C], BF16, st_)
        WO = sb("l0_wo", [128, NC, C], BF16, st_)
        XK = sb("l0_xk", [128, NC, 512], BF16, st_)
        XV = sb("l0_xv", [128, NC, 512], BF16, st_)
        XR = sb("l0_xr", [128, NC, 512], BF16, st_)
        RW = sb("l0_rw", [128, NC, 512], BF16, st_)
        TMPD = sb("l0_tmpd", [128, 512], F32, st_)
        XPREV = sb("l0_xprev", [128, NC, 1], F32, st_)
        NS = 2
        SL_EK = [sb(f"l0_ek{i}", [128, 512], F32, st_) for i in range(NS)]
        SL_EKV = [sb(f"l0_ekv{i}", [128, 512], F32, st_) for i in range(NS)]
        SL_A = [sb(f"l0_A{i}", [128, 513], F32, st_) for i in range(NS)]
        SL_B = [sb(f"l0_B{i}", [128, 513], F32, st_) for i in range(NS)]
        SL_SR = [sb(f"l0_sr{i}", [128, 512], BF16, st_) for i in range(NS)]
        CARA = sb("l0_carA", [128, NC, 1], F32, st_)
        CARB = sb("l0_carB", [128, NC, 1], F32, st_)

        wtok = None
        first = True
        for Wb, nm in ((WK, "wkT"), (WV, "wvT"), (WR, "wrT"), (WO, "woT")):
            for c in range(NC):
                wtok = P.dma("g", Wb[:, c, :], din[nm][c * 128:(c + 1) * 128, :],
                             new=first)
                first = False

        v0 = P.op("v", lambda v: v.memset(CARA[:], 0.0))
        v0 = P.op("v", lambda v: v.memset(CARB[:], 0.0), after=[v0])
        vprev = P.op("v", lambda v: v.memset(XPREV[:], 0.0), after=[v0])

        tmk, tmv, tmr = SVEC[:, 0:8], SVEC[:, 8:16], SVEC[:, 16:24]
        ew, eu = SVEC[:, 24:32], SVEC[:, 32:40]

        slot_free = [None] * NS
        bank_free = [None] * 8
        vt = vprev
        st = None

        for s in range(4):
            s0 = s * 512
            mix_start = [xready, d_tm, vprev, vt]
            for c in range(NC):
                P.pump(1)
                v1 = P.op("v", lambda v, c=c, s0=s0: v.tensor_sub(
                    TMPD[:, 0:1], XA[:, c, s0:s0 + 1], XPREV[:, c, :]),
                    after=mix_start if c == 0 else [vt])
                v2 = P.op("v", lambda v, c=c, s0=s0: v.tensor_sub(
                    TMPD[:, 1:512], XA[:, c, s0 + 1:s0 + 512], XA[:, c, s0:s0 + 511]),
                    after=[v1])
                vt = v2
                for XBUF, tm in ((XK, tmk), (XV, tmv), (XR, tmr)):
                    m1 = P.op("v", lambda v, XBUF=XBUF, tm=tm, c=c: v.scalar_tensor_tensor(
                        XBUF[:, c, 0:1], TMPD[:, 0:1], tm[:, c:c + 1], XPREV[:, c, :],
                        ALU.mult, ALU.add), after=[vt])
                    vt = P.op("v", lambda v, XBUF=XBUF, tm=tm, c=c, s0=s0: v.scalar_tensor_tensor(
                        XBUF[:, c, 1:512], TMPD[:, 1:512], tm[:, c:c + 1],
                        XA[:, c, s0:s0 + 511], ALU.mult, ALU.add), after=[m1])
            mix_all = P.tok("v")

            for c in range(NC):
                P.pump(1)
                sl = c % NS
                ek, ekv, A, Bt, sr = SL_EK[sl], SL_EKV[sl], SL_A[sl], SL_B[sl], SL_SR[sl]
                bk = c % 2
                bv = bk ^ 1

                mmK = None
                for i in range(NC):
                    mmK = P.op("t", lambda t, c=c, i=i, bk=bk: t.matmul(
                        PS[bk][:, :], WK[:, i, c * 128:(c + 1) * 128], XK[:, i, :],
                        start=(i == 0), stop=(i == NC - 1)),
                        after=[wtok, mix_all, bank_free[bk]] if i == 0 else [])
                e1 = P.op("s", lambda s_, ek=ek, bk=bk: s_.activation(
                    ek[:], PS[bk][:, :], AF.Exp), after=[mmK, slot_free[sl], st])
                st = e1
                mmV = None
                for i in range(NC):
                    mmV = P.op("t", lambda t, c=c, i=i, bv=bv: t.matmul(
                        PS[bv][:, :], WV[:, i, c * 128:(c + 1) * 128], XV[:, i, :],
                        start=(i == 0), stop=(i == NC - 1)),
                        after=[bank_free[bv]] if i == 0 else [])
                e2 = P.op("v", lambda v, ek=ek, ekv=ekv, bv=bv: v.tensor_mul(
                    ekv[:], ek[:], PS[bv][:, :]), after=[mmV, e1, vt, slot_free[sl]])
                vt = e2
                bank_free[bk] = e1
                bank_free[bv] = e2
                mmR = None
                for i in range(NC):
                    mmR = P.op("t", lambda t, c=c, i=i: t.matmul(
                        PS[2][:, :], WR[:, i, c * 128:(c + 1) * 128], XR[:, i, :],
                        start=(i == 0), stop=(i == NC - 1)),
                        after=[bank_free[2]] if i == 0 else [])
                e3 = P.op("s", lambda s_, sr=sr: s_.activation(
                    sr[:], PS[2][:, :], AF.Sigmoid), after=[mmR, st, slot_free[sl]])
                st = e3
                bank_free[2] = e3

                c1 = P.op("v", lambda v, A=A, c=c: v.tensor_copy(
                    A[:, 0:1], CARA[:, c, :]), after=[e2, vt])
                c2 = P.op("v", lambda v, Bt=Bt, c=c: v.tensor_copy(
                    Bt[:, 0:1], CARB[:, c, :]), after=[c1])
                sc1 = P.op("v", lambda v, A=A, ekv=ekv, c=c: v.tensor_tensor_scan(
                    A[:, 1:513], ew[:, c:c + 1].broadcast_to([128, 512]), ekv[:],
                    A[:, 0:1], ALU.mult, ALU.add), after=[c2, s_ew])
                sc2 = P.op("v", lambda v, Bt=Bt, ek=ek, c=c: v.tensor_tensor_scan(
                    Bt[:, 1:513], ew[:, c:c + 1].broadcast_to([128, 512]), ek[:],
                    Bt[:, 0:1], ALU.mult, ALU.add), after=[sc1, e1])
                c3 = P.op("v", lambda v, A=A, c=c: v.tensor_copy(
                    CARA[:, c, :], A[:, 512:513]), after=[sc1])
                c4 = P.op("v", lambda v, Bt=Bt, c=c: v.tensor_copy(
                    CARB[:, c, :], Bt[:, 512:513]), after=[sc2])
                n1 = P.op("v", lambda v, ekv=ekv, A=A, c=c: v.scalar_tensor_tensor(
                    ekv[:], ekv[:], eu[:, c:c + 1], A[:, 0:512], ALU.mult, ALU.add),
                    after=[c3, s_eu])
                n2 = P.op("v", lambda v, ek=ek, Bt=Bt, c=c: v.scalar_tensor_tensor(
                    ek[:], ek[:], eu[:, c:c + 1], Bt[:, 0:512], ALU.mult, ALU.add),
                    after=[c4, n1])
                n3 = P.op("v", lambda v, ek=ek: v.reciprocal(
                    ek[:], ek[:]), after=[n2])
                n3b = P.op("v", lambda v, ekv=ekv, ek=ek: v.tensor_mul(
                    ekv[:], ekv[:], ek[:]), after=[n3])
                n4 = P.op("v", lambda v, ekv=ekv, sr=sr, c=c: v.tensor_mul(
                    RW[:, c, :], ekv[:], sr[:]), after=[n3b, e3])
                vt = n4
                slot_free[sl] = n4

            vsave = P.op("v", lambda v, s0=s0: v.tensor_copy(
                XPREV[:, :, :], XA[:, :, s0 + 511:s0 + 512]), after=[vt])
            vprev = vsave
            vt = vsave
            rw_all = P.tok("v")
            for c in range(NC):
                P.pump(1)
                bo = 3 + (c % 2)
                mmO = None
                for i in range(NC):
                    mmO = P.op("t", lambda t, c=c, i=i, bo=bo: t.matmul(
                        PS[bo][:, :], WO[:, i, c * 128:(c + 1) * 128], RW[:, i, :],
                        start=(i == 0), stop=(i == NC - 1)),
                        after=[rw_all, bank_free[bo]] if i == 0 else [])
                ev = P.op("v", lambda v, c=c, s0=s0, bo=bo: v.tensor_add(
                    XA[:, c, s0:s0 + 512], PS[bo][:, :], XA[:, c, s0:s0 + 512]),
                    after=[mmO, vt])
                vt = ev
                bank_free[bo] = ev



def stage_ln(ctx, which, toks):
    P, sb, es = ctx["P"], ctx["sb"], ctx["es"]
    XA, IDF, PS, STAT = ctx["XA"], ctx["IDF"], ctx["PS"], ctx["STAT"]
    out = ctx["out"]
    final = which == 4

    with ExitStack() as st_:
        TM = [sb(f"ln{which}_tm{i}", [128, 1024], F32, st_) for i in range(2)]
        SQJ = sb(f"ln{which}_sqj", [128, 1024], F32, st_)
        COLS = sb(f"ln{which}_cols", [128, 8], F32, st_)

        P.barrier()
        vt = None
        stt = None
        ttok = None
        tm_free = [None, None]
        fwd_bank_free = [None, None]
        bwd_bank_free = [None, None]
        dma_last = None
        for tt in range(NTT):
            P.pump(1)
            sl = tt % 2
            tmb = TM[sl]
            tps = []
            for c in range(NC):
                bank = 5 + c // 4
                cc = c % 4
                a = [tm_free[sl], fwd_bank_free[c // 4]] if c in (0, 4) else []
                tps.append(P.op("t", lambda t, c=c, tt=tt, bank=bank, cc=cc: t.transpose(
                    PS[bank][:, cc * 128:(cc + 1) * 128],
                    XA[:, c, tt * 128:(tt + 1) * 128], IDF[:]), after=a))
            ttok = tps[-1]
            e1 = P.op("s", lambda s, tmb=tmb: s.activation(
                tmb[:, 0:512], PS[5][:, :], AF.Identity), after=[tps[3], stt, tm_free[sl]])
            e2 = P.op("s", lambda s, tmb=tmb: s.activation(
                tmb[:, 512:1024], PS[6][:, :], AF.Identity), after=[tps[7], e1])
            stt = e2
            fwd_bank_free[0] = e1
            fwd_bank_free[1] = e2
            v1 = P.op("v", lambda v, tmb=tmb: v.reduce_sum(
                COLS[:, 0:1], tmb[:], axis=AXX), after=[e2, vt])
            v2 = P.op("s", lambda s, tmb=tmb: s.activation(
                SQJ[:], tmb[:], AF.Square, accum_out=COLS[:, 1:2]),
                after=[v1, stt])
            stt = v2
            v3 = P.op("v", lambda v: v.tensor_scalar(
                COLS[:, 2:3], COLS[:, 0:1], 1.0 / 1024, None, ALU.mult), after=[v2])
            v4 = P.op("v", lambda v: v.tensor_mul(
                COLS[:, 3:4], COLS[:, 2:3], COLS[:, 2:3]), after=[v3])
            v5 = P.op("v", lambda v: v.scalar_tensor_tensor(
                COLS[:, 4:5], COLS[:, 1:2], 1.0 / 1024, COLS[:, 3:4],
                ALU.mult, ALU.subtract), after=[v4])
            v6 = P.op("v", lambda v: v.tensor_scalar(
                COLS[:, 4:5], COLS[:, 4:5], 1e-5, None, ALU.add), after=[v5])
            v7 = P.op("v", lambda v: v.reciprocal(COLS[:, 5:6], COLS[:, 4:5]),
                      after=[v6])
            s1 = P.op("s", lambda s: s.activation(
                COLS[:, 6:7], COLS[:, 5:6], AF.Sqrt), after=[v7, stt])
            stt = s1
            v8 = P.op("v", lambda v, tmb=tmb: v.tensor_scalar(
                tmb[:], tmb[:], COLS[:, 2:3], COLS[:, 6:7],
                ALU.subtract, ALU.mult), after=[s1, vt])
            vt = v8
            if which == 2:
                stt = P.op("s", lambda s, tmb=tmb, tt=tt: s.activation(
                    SQJ[:], tmb[:], AF.Square,
                    accum_out=STAT[:, 16 + tt:17 + tt]), after=[v8, stt])
                vt = v8
            if final:
                dma_last = P.dma("y", out[tt * 128:(tt + 1) * 128, :], tmb[:],
                                 after=[vt], new=True)
                tm_free[sl] = dma_last
            else:
                bts = []
                for c in range(NC):
                    bank = 3 + c // 4
                    cc = c % 4
                    a = [vt, bwd_bank_free[c // 4]] if c in (0, 4) else []
                    bts.append(P.op("t", lambda t, c=c, bank=bank, cc=cc, tmb=tmb: t.transpose(
                        PS[bank][:, cc * 128:(cc + 1) * 128],
                        tmb[:, c * 128:(c + 1) * 128], IDF[:]), after=a))
                ttok = bts[-1]
                ev = []
                for half in range(2):
                    bank = 3 + half
                    ev.append(P.op("s", lambda s, tt=tt, bank=bank, half=half: s.activation(
                        _xa_cols(XA, tt, half), PS[bank][:, :], AF.Identity),
                        after=[bts[3 + 4 * half], stt]))
                stt = ev[-1]
                bwd_bank_free[0] = ev[0]
                bwd_bank_free[1] = ev[1]
                tm_free[sl] = ev[-1]
        return dict(v=P.tok("v"), t=P.tok("t"), s=P.tok("s"),
                    out_dma=dma_last)


def _xa_cols(XA, tt, half):
    return XA[:, half * 4:(half + 1) * 4, tt * 128:(tt + 1) * 128]


def stage_l0ffn(ctx, toks, debug_stop=None):
    P, sb = ctx["P"], ctx["sb"]
    XA, SVEC, PS, din = ctx["XA"], ctx["SVEC"], ctx["PS"], ctx["din"]

    with ExitStack() as st_:
        H0 = sb("f_h0", [128, 32768], BF16, st_)
        H1X = sb("f_h1", [128, 32768], BF16, st_)
        WJ = [sb(f"f_wj{i}", [128, 16, 128], BF16, st_) for i in range(2)]
        G1 = sb("f_g1", [128, 1024], BF16, st_)
        SC = sb("f_sc", [128, 128], F32, st_)

        P.barrier()
        d_sc = P.dma("y", SC[:, 0:32], din["s1"], new=True)
        P.dma("y", SC[:, 32:64], din["s2"])
        P.dma("y", SC[:, 64:96], din["sh1"])
        d_sc2 = P.dma("y", SC[:, 96:128], din["sh2"])
        vt = None
        stt = None
        wj_free = [None, None]
        bank_free = [None] * 8
        s3 = SVEC[:, 40:48]
        wi = 0

        def wload(nm, j, half, after=()):
            nonlocal wi
            slot = wi % 2
            wb = WJ[slot]
            srcap = din[nm][j]
            if half is None:
                dst, s_ = wb[:, 0:8, :], srcap
            else:
                dst, s_ = wb[:, 0:16, :], srcap[:, half * 16:half * 16 + 16, :]
            tok = P.dma("g", dst, s_, after=[wj_free[slot]] + list(after), new=True)
            wi += 1
            return slot, tok

        for hb in range(2):
            t0 = hb * 1024
            XB = H1X[:, 0:8192].rearrange("p (c n) -> p c n", c=8)
            cast = None
            for c in range(NC):
                cast = P.op("v", lambda v, c=c, t0=t0: v.tensor_copy(
                    XB[:, c, :], XA[:, c, t0:t0 + 1024]),
                    after=[vt, toks.get("v"), toks.get("s")] if c == 0 else [])
            vt = cast
            xb_tok = cast

            for j in range(32):
                P.pump(1)
                slot1, dtok1 = wload("w1R", j, None)
                slot2, dtok2 = wload("w2R", j, None)
                wb1, wb2 = WJ[slot1], WJ[slot2]
                mmlast = None
                for wsel in range(2):
                    wb = (wb1, wb2)[wsel]
                    dtok = (dtok1, dtok2)[wsel]
                    for i in range(NC):
                        for s2 in range(2):
                            bank = wsel * 4 + (j % 2) * 2 + s2
                            mmlast = P.op("t", lambda t, wb=wb, i=i, bank=bank, s2=s2:
                                          t.matmul(
                                PS[bank][:, :], wb[:, i, :],
                                XB[:, i, s2 * 512:(s2 + 1) * 512],
                                start=(i == 0), stop=(i == NC - 1)),
                                after=[dtok, xb_tok, bank_free[bank]]
                                if i == 0 and s2 == 0 else
                                ([bank_free[bank]] if i == 0 else []))
                for s2 in range(2):
                    scj = j
                    bank = (j % 2) * 2 + s2
                    evs = P.op("s", lambda s, scj=scj, bank=bank, s2=s2: s.activation(
                        G1[:, s2 * 512:(s2 + 1) * 512], PS[bank][:, :], AF.Silu,
                        scale=SC[:, scj:scj + 1]), after=[mmlast, stt, vt, d_sc])
                    stt = evs
                    bank_free[bank] = evs
                for s2 in range(2):
                    scj = 32 + j
                    bank = 4 + (j % 2) * 2 + s2
                    off = s2 * 512
                    evv = P.op("v", lambda v, scj=scj, j=j, off=off, bank=bank:
                               v.scalar_tensor_tensor(
                        H0[:, j * 1024 + off:j * 1024 + off + 512],
                        PS[bank][:, :], SC[:, scj:scj + 1],
                        G1[:, off:off + 512], ALU.mult, ALU.mult),
                        after=[mmlast, stt, vt])
                    vt = evv
                    bank_free[bank] = evv
                wj_free[slot1] = mmlast
                wj_free[slot2] = mmlast
            h0_done_v = vt

            for phase, nm, scoff, SRC, DST in (
                    ("B", "h1R", 64, H0, H1X), ("C", "h2R", 96, H1X, H0)):
                for j in range(32):
                    P.pump(1)
                    stok = [None, None]
                    wbs = [None, None]
                    for half in range(2):
                        wbs[half], stok[half] = wload(nm, j, half)
                    mm = None
                    for ihalf in range(2):
                        wb = WJ[wbs[ihalf]]
                        for i2 in range(16):
                            i = ihalf * 16 + i2
                            for s2 in range(2):
                                off = s2 * 512
                                bank = 4 + (j % 2) * 2 + s2
                                mm = P.op("t", lambda t, wb=wb, i2=i2, i=i, bank=bank,
                                          off=off, SRC=SRC: t.matmul(
                                    PS[bank][:, :], wb[:, i2, :],
                                    SRC[:, i * 1024 + off:i * 1024 + off + 512],
                                    start=(i == 0), stop=(i == 31)),
                                    after=[stok[0], stok[1], bank_free[bank], h0_done_v]
                                    if i == 0 and s2 == 0 else
                                    ([bank_free[bank]] if i == 0 else []))
                    for s2 in range(2):
                        off = s2 * 512
                        bank = 4 + (j % 2) * 2 + s2
                        ev = P.op("s", lambda s, j=j, off=off, bank=bank, scoff=scoff,
                                  DST=DST: s.activation(
                            DST[:, j * 1024 + off:j * 1024 + off + 512],
                            PS[bank][:, :], AF.Gelu, scale=SC[:, scoff + j:scoff + j + 1]),
                            after=[mm, stt, d_sc2])
                        stt = ev
                        bank_free[bank] = ev
                    wj_free[wbs[0]] = mm
                    wj_free[wbs[1]] = mm
                h0_done_v = stt
            hc_done = stt

            for j in range(NC):
                P.pump(1)
                stok = [None, None]
                wbs = [None, None]
                for half in range(2):
                    wbs[half], stok[half] = wload("w3R", j, half)
                mm = None
                for ihalf in range(2):
                    wb = WJ[wbs[ihalf]]
                    for i2 in range(16):
                        i = ihalf * 16 + i2
                        for s2 in range(2):
                            off = s2 * 512
                            bank = (j % 2) * 2 + s2
                            mm = P.op("t", lambda t, wb=wb, i2=i2, i=i, bank=bank,
                                      off=off: t.matmul(
                                PS[bank][:, :], wb[:, i2, :],
                                H0[:, i * 1024 + off:i * 1024 + off + 512],
                                start=(i == 0), stop=(i == 31)),
                                after=[stok[0], stok[1], bank_free[bank], hc_done]
                                if i == 0 and s2 == 0 else
                                ([bank_free[bank]] if i == 0 else []))
                for s2 in range(2):
                    off = s2 * 512
                    bank = (j % 2) * 2 + s2
                    ev = P.op("v", lambda v, j=j, off=off, t0=t0, bank=bank:
                              v.scalar_tensor_tensor(
                        XA[:, j, t0 + off:t0 + off + 512], PS[bank][:, :],
                        s3[:, j:j + 1], XA[:, j, t0 + off:t0 + off + 512],
                        ALU.mult, ALU.add), after=[mm, vt])
                    vt = ev
                    bank_free[bank] = ev
                wj_free[wbs[0]] = mm
                wj_free[wbs[1]] = mm
        return dict(v=P.tok("v"), t=P.tok("t"), s=P.tok("s"))



def stage_l1attn(ctx, toks):
    P, sb = ctx["P"], ctx["sb"]
    XA, IDF, TRIU, PS, STAT, SCL = (ctx["XA"], ctx["IDF"], ctx["TRIU"],
                                    ctx["PS"], ctx["STAT"], ctx["SCL"])
    din, qscr, SVEC, IDB = ctx["din"], ctx["qscr"], ctx["SVEC"], ctx["IDB"]
    PSB = ctx["PSB"]

    with ExitStack() as st_:
        S = sb("a_S", [128, 8, 512], F32, st_)
        X2 = sb("a_x2", [128, 2, 1024], BF16, st_)
        QSz = sb("a_qz", [128, 1024], BF16, st_)
        QSi = sb("a_qi", [128, 1024], mybir.dt.int32, st_)
        QSt = sb("a_qt", [128, 1024], BF16, st_)
        QSd = sb("a_qd", [128, 1024], BF16, st_)
        Q2z = sb("a_q2z", [128, 1024], BF16, st_)
        Q2i = sb("a_q2i", [128, 1024], mybir.dt.int32, st_)
        Q2t = sb("a_q2t", [128, 1024], BF16, st_)
        Q2d = sb("a_q2d", [128, 1024], BF16, st_)
        CL2 = sb("a_cl2", [128, 16], F32, st_)
        XQF = sb("a_xqf", [128, 8, 256], BF16, st_)
        Q = sb("a_q", [128, 8, 256], F32, st_)
        K = sb("a_k", [128, 8, 256], BF16, st_)
        GK = sb("a_gk", [128, 8, 256], BF16, st_)
        WF = sb("a_wf", [128, 8, 256], F32, st_)
        M0 = sb("a_m0", [128, 8, 256], BF16, st_)
        V = sb("a_v", [128, 2, 2048], F32, st_)
        SG = sb("a_sg", [128, 2, 2048], BF16, st_)
        GT = sb("a_gt", [128, 8, 129], F32, st_)
        DC = sb("a_dc", [128, 8], F32, st_)
        QG = sb("a_qg", [128, 2, 128], F32, st_)
        ET = sb("a_et", [128, 2, 128], F32, st_)
        KP = sb("a_kp", [128, 2, 128], F32, st_)
        KG = sb("a_kg", [128, 2, 128], F32, st_)
        KGt = sb("a_kgt", [128, 2, 128], F32, st_)
        A2 = sb("a_a2", [128, 128], F32, st_)
        OG = sb("a_og", [128, 2048], BF16, st_)
        OQF = sb("a_oqf", [128, 16, 128], BF16, st_)
        NB = sb("a_nb", [128, 1024], BF16, st_)
        NOB = sb("a_nob", [128, 2048], BF16, st_)
        NGK2 = sb("a_ngk2", [128, 16], BF16, st_)
        G1T = sb("a_g1t", [128, 2, 16], F32, st_)
        X2Q = sb("a_x2q", [16, 256], BF16, st_)
        WGK1 = sb("a_wgk1", [128, 8, 16], BF16, st_)
        WGK2 = sb("a_wgk2", [16, 1024], BF16, st_)
        WL = sb("a_wl", [128, 8192], BF16, st_)
        CL = sb("a_cl", [128, 16], F32, st_)

        P.barrier()
        d_ngk2 = P.dma("g", NGK2[:], din["ngk2B"], new=True)
        d_nob = P.dma("g", NOB[:], din["noB"], new=True)
        d_wgk1 = P.dma("y", WGK1[:, :, :],
                       qscr["gk1wT"].rearrange("(c p) n -> p c n", p=128), new=True)
        d_wgk2 = P.dma("y", WGK2[:, :], qscr["gk2wT"], new=True)

        vz = P.op("v", lambda v: v.memset(S[:], 0.0))
        vz2 = P.op("v", lambda v: v.memset(STAT[:, 62:63], 0.0), after=[vz])
        vt = vz2
        gt = None
        nb_free2 = [None, None]
        zb_free = [None, None]
        stt = None
        bank_free = [None] * 8
        wl_free = None
        x2_free = None
        sgate = SVEC[:, 48:56]

        NPROJ = [("q", "qwT", "nqB", 1024), ("k", "kwT", "nkB", 1024),
                 ("v", "vwT", "nvB", 2048), ("g", "gwT", "ngB", 2048),
                 ("gk1", "gk1wT", "ngk1B", 16)]

        for blk_i in range(8):
            t0 = blk_i * 256
            for tt in range(2):
                tps = []
                for c in range(NC):
                    bank = 5 + c // 4
                    cc = c % 4
                    a = [x2_free, bank_free[bank]] if (c % 4 == 0) else []
                    tps.append(P.op("t", lambda t, tt=tt, c=c, t0=t0, bank=bank, cc=cc:
                                    t.transpose(PS[bank][:, cc * 128:(cc + 1) * 128],
                                    XA[:, c, t0 + tt * 128:t0 + (tt + 1) * 128], IDF[:]),
                                    after=a))
                for half in range(2):
                    bank = 5 + half
                    ev = P.op("v", lambda v, tt=tt, half=half, bank=bank: v.tensor_copy(
                        X2[:, tt, half * 512:(half + 1) * 512], PS[bank][:, :]),
                        after=[tps[half * 4 + 3], vt])
                    vt = ev
                    bank_free[bank] = ev
            x2tok = vt
            c1 = P.op("v", lambda v, blk_i=blk_i: v.tensor_scalar(
                CL[:, 0:2], STAT[:, 16 + 2 * blk_i:18 + 2 * blk_i],
                1.0 / 1024, 1e-8, ALU.mult, ALU.add), after=[vt])
            c2 = P.op("v", lambda v: v.reciprocal(CL[:, 2:4], CL[:, 0:2]), after=[c1])
            c3 = P.op("s", lambda s: s.activation(CL[:, 0:2], CL[:, 2:4], AF.Sqrt),
                      after=[c2, stt])
            stt = c3

            for pi, (pnm, wnm, nnm, dout) in enumerate(NPROJ):
                d_nb = P.dma("g", NB[:, 0:1024], din[nnm], new=True,
                             after=[nb_free2[0], nb_free2[1]])
                qtoks = []
                for tt in range(2):
                    en = "v"
                    Zb, Ib, Tb, Db = (QSz, QSi, QSt, QSd) if tt == 0 else (Q2z, Q2i, Q2t, Q2d)
                    CLx = CL if tt == 0 else CL2
                    ch = vt
                    p1 = P.op(en, lambda v, tt=tt, Zb=Zb: v.tensor_mul(
                        Zb[:], X2[:, tt, :], NB[:, 0:1024]),
                        after=[d_nb, x2tok, ch, zb_free[tt]])
                    nb_free2[tt] = p1
                    p2 = P.op("v", lambda v, Zb=Zb, CLx=CLx: v.reduce_max(
                        CLx[:, 4:5], Zb[:], axis=AXX, apply_absolute_value=True),
                        after=[p1, vt])
                    p3 = P.op("v", lambda v, tt=tt, CLx=CLx: v.tensor_scalar(
                        CLx[:, 5:6], CLx[:, 4:5], CL[:, tt:tt + 1], 1e-5,
                        ALU.mult, ALU.max), after=[p2, c3])
                    p4 = P.op("v", lambda v, CLx=CLx: v.reciprocal(
                        CLx[:, 6:7], CLx[:, 5:6]), after=[p3])
                    p5 = P.op("v", lambda v, tt=tt, CLx=CLx: v.scalar_tensor_tensor(
                        CLx[:, 7:8], CLx[:, 6:7], 127.0, CL[:, tt:tt + 1],
                        ALU.mult, ALU.mult), after=[p4])
                    p6 = P.op("v", lambda v, CLx=CLx: v.tensor_scalar(
                        CLx[:, 8:9], CLx[:, 5:6], 1.0 / 127, None, ALU.mult),
                        after=[p5])
                    vt = p6
                    q1 = P.op("s", lambda s, Zb=Zb, CLx=CLx: s.activation(
                        Zb[:], Zb[:], AF.Identity, scale=CLx[:, 7:8]),
                        after=[p6, ch, stt])
                    stt = q1
                    q6, s6 = _round_fix(P, en, Zb, Ib, Tb, Db, Zb, q1, stt)
                    stt = s6
                    q7 = P.op(en, lambda v, Zb=Zb, Tb=Tb, CLx=CLx: v.tensor_scalar(
                        Zb[:], Tb[:], 1.0, CLx[:, 8:9], ALU.subtract, ALU.mult),
                        after=[q6])
                    vt = q7
                    xtp = []
                    for c in range(NC):
                        bank = 5 + c // 4
                        cc = c % 4
                        a = [q7, bank_free[bank]] if (c % 4 == 0) else []
                        xtp.append(P.op("t", lambda t, c=c, bank=bank, cc=cc, Zb=Zb:
                                        t.transpose(
                            PSB[bank][:, cc * 128:(cc + 1) * 128],
                            Zb[:, c * 128:(c + 1) * 128], IDB[:]), after=a))
                    zb_free[tt] = xtp[-1]
                    for half in range(2):
                        bank = 5 + half
                        ev = P.op("v", lambda v, tt=tt, half=half, bank=bank: v.tensor_copy(
                            XQF[:, half * 4:(half + 1) * 4, tt * 128:(tt + 1) * 128],
                            PSB[bank][:, 0:512].rearrange("p (c n) -> p c n", c=4)),
                            after=[xtp[half * 4 + 3], vt])
                        vt = ev
                        bank_free[bank] = ev
                    qtoks.append(vt)
                xq_tok = vt

                if pnm == "gk1":
                    mms = []
                    for tt in range(2):
                        bank = 2
                        mm = None
                        for i in range(NC):
                            mm = P.op("t", lambda t, i=i, tt=tt, bank=bank: t.matmul(
                                PS[bank][:, 0:16], XQF[:, i, tt * 128:(tt + 1) * 128],
                                WGK1[:, i, :], start=(i == 0), stop=(i == NC - 1)),
                                after=[d_wgk1, xq_tok, bank_free[bank]] if i == 0 else [])
                        ev = P.op("v", lambda v, tt=tt, bank=bank: v.tensor_scalar(
                            G1T[:, tt, :], PS[bank][:, 0:16],
                            SCL[:, QWI["gk1wT"]:QWI["gk1wT"] + 1], None, ALU.mult),
                            after=[mm, vt])
                        vt = ev
                        bank_free[bank] = ev
                    continue
                wq = qscr[wnm]
                scl_c = SCL[:, QWI[wnm]:QWI[wnm] + 1]
                for half in range(dout // 1024):
                    WLv = WL[:, 0:8192].rearrange("p (c n) -> p c n", c=8)
                    d_w = P.dma("y", WLv[:, :, :], wq[:, half, :, :],
                                after=[wl_free], new=True)
                    if pnm in ("q", "k"):
                        DSTB = Q if pnm == "q" else K
                        for j in range(NC):
                            bank = j % 2
                            mm = None
                            for i in range(NC):
                                mm = P.op("t", lambda t, i=i, j=j, bank=bank: t.matmul(
                                    PS[bank][:, 0:256], WLv[:, i, j * 128:(j + 1) * 128],
                                    XQF[:, i, :],
                                    start=(i == 0), stop=(i == NC - 1)),
                                    after=[d_w, xq_tok, bank_free[bank]] if i == 0 else [])
                            ev = P.op("v", lambda v, j=j, bank=bank, DSTB=DSTB,
                                      scl_c=scl_c: v.tensor_scalar(
                                DSTB[:, j, :], PS[bank][:, 0:256], scl_c, None,
                                ALU.mult), after=[mm, vt])
                            vt = ev
                            bank_free[bank] = ev
                            wl_free = mm
                    else:
                        for tt in range(2):
                            for ds in range(2):
                                bank = ds % 2
                                mm = None
                                for i in range(NC):
                                    mm = P.op("t", lambda t, i=i, tt=tt, ds=ds, bank=bank:
                                              t.matmul(
                                        PS[bank][:, :],
                                        XQF[:, i, tt * 128:(tt + 1) * 128],
                                        WLv[:, i, ds * 512:(ds + 1) * 512],
                                        start=(i == 0), stop=(i == NC - 1)),
                                        after=[d_w, xq_tok, bank_free[bank]] if i == 0 else [])
                                off = half * 1024 + ds * 512
                                if pnm == "v":
                                    ev = P.op("v", lambda v, tt=tt, off=off, bank=bank,
                                              scl_c=scl_c: v.tensor_scalar(
                                        V[:, tt, off:off + 512], PS[bank][:, :],
                                        scl_c, None, ALU.mult), after=[mm, vt])
                                    vt = ev
                                else:
                                    ev = P.op("s", lambda s, tt=tt, off=off, bank=bank,
                                              scl_c=scl_c: s.activation(
                                        SG[:, tt, off:off + 512], PS[bank][:, :],
                                        AF.Silu, scale=scl_c), after=[mm, stt])
                                    stt = ev
                                bank_free[bank] = ev
                                wl_free = mm

            for tt in range(2):
                h1 = P.op("s", lambda s, tt=tt: s.activation(
                    WF[:, 0, 0:16], G1T[:, tt, :], AF.Square,
                    accum_out=CL[:, 9:10]), after=[vt, stt])
                stt = h1
                h2 = P.op("v", lambda v: v.tensor_scalar(
                    CL[:, 10:11], CL[:, 9:10], 1.0 / 16, 1e-8, ALU.mult, ALU.add),
                    after=[h1])
                h3 = P.op("v", lambda v: v.reciprocal(CL[:, 11:12], CL[:, 10:11]),
                          after=[h2])
                h4 = P.op("s", lambda s: s.activation(CL[:, 10:11], CL[:, 11:12],
                                                      AF.Sqrt), after=[h3, stt])
                stt = h4
                r1 = P.op("v", lambda v, tt=tt: v.tensor_mul(
                    QSz[:, 0:16], G1T[:, tt, :], NGK2[:]),
                    after=[h4, d_ngk2, vt, zb_free[0]])
                r2 = P.op("v", lambda v: v.reduce_max(
                    CL[:, 4:5], QSz[:, 0:16], axis=AXX, apply_absolute_value=True),
                    after=[r1])
                r3 = P.op("v", lambda v: v.tensor_scalar(
                    CL[:, 5:6], CL[:, 4:5], CL[:, 10:11], 1e-5, ALU.mult, ALU.max),
                    after=[r2])
                r4 = P.op("v", lambda v: v.reciprocal(CL[:, 6:7], CL[:, 5:6]),
                          after=[r3])
                r5 = P.op("v", lambda v: v.scalar_tensor_tensor(
                    CL[:, 7:8], CL[:, 6:7], 127.0, CL[:, 10:11], ALU.mult, ALU.mult),
                    after=[r4])
                r6 = P.op("v", lambda v: v.tensor_scalar(
                    CL[:, 8:9], CL[:, 5:6], 1.0 / 127, None, ALU.mult), after=[r5])
                u1 = P.op("v", lambda v: v.tensor_single_scalar(
                    QSz[:, 0:16], QSz[:, 0:16], CL[:, 7:8], ALU.mult), after=[r6])
                u2 = P.op("v", lambda v: v.tensor_copy(QSi[:, 0:16], QSz[:, 0:16]),
                          after=[u1])
                u3 = P.op("v", lambda v: v.tensor_copy(QSt[:, 0:16], QSi[:, 0:16]),
                          after=[u2])
                u4 = P.op("v", lambda v: v.tensor_sub(
                    QSd[:, 0:16], QSz[:, 0:16], QSt[:, 0:16]), after=[u3])
                u5 = P.op("v", lambda v: v.scalar_tensor_tensor(
                    QSt[:, 0:16], QSd[:, 0:16], 0.5, QSt[:, 0:16], ALU.is_ge,
                    ALU.add), after=[u4])
                u6 = P.op("v", lambda v: v.scalar_tensor_tensor(
                    QSt[:, 0:16], QSd[:, 0:16], -0.5, QSt[:, 0:16], ALU.is_gt,
                    ALU.add), after=[u5])
                u7 = P.op("v", lambda v: v.tensor_scalar(
                    QSz[:, 0:16], QSt[:, 0:16], 1.0, CL[:, 8:9], ALU.subtract,
                    ALU.mult), after=[u6])
                vt = u7
                tp = P.op("t", lambda t, tt=tt: t.transpose(
                    PSB[2][0:16, 0:128], QSz[:, 0:16], IDB[:]), after=[u7, bank_free[2]])
                zb_free[0] = tp
                ev = P.op("v", lambda v, tt=tt: v.tensor_copy(
                    X2Q[:, tt * 128:(tt + 1) * 128], PSB[2][0:16, 0:128]),
                    after=[tp, vt])
                vt = ev
                bank_free[2] = ev
            for j in range(NC):
                bank = 2 + (j % 2)
                mm = P.op("t", lambda t, j=j, bank=bank: t.matmul(
                    PS[bank][:, 0:256], WGK2[:, j * 128:(j + 1) * 128],
                    X2Q[:], start=True, stop=True),
                    after=[d_wgk2, vt, bank_free[bank]])
                ev = P.op("s", lambda s, j=j, bank=bank: s.activation(
                    WF[:, j, :], PS[bank][:, 0:256], AF.Identity,
                    scale=SCL[:, QWI["gk2wT"]:QWI["gk2wT"] + 1],
                    bias=sgate[:, j:j + 1]), after=[mm, stt])
                stt = ev
                bank_free[bank] = ev
            l1 = P.op("v", lambda v: v.tensor_scalar(
                M0[:], WF[:], 0.0, 1.0 / 16, ALU.min, ALU.mult), after=[stt, vt])
            l2 = P.op("s", lambda s: s.activation(WF[:], WF[:], AF.Abs),
                      after=[l1, stt])
            l3 = P.op("s", lambda s: s.activation(WF[:], WF[:], AF.Exp, scale=-1.0),
                      after=[l2])
            l4 = P.op("s", lambda s: s.activation(WF[:], WF[:], AF.Ln, bias=1.0),
                      after=[l3])
            l5 = P.op("v", lambda v: v.scalar_tensor_tensor(
                GK[:], WF[:], -1.0 / 16, M0[:], ALU.mult, ALU.add), after=[l4])
            vt = l5

            for cn in range(2):
                z0 = P.op("v", lambda v: v.memset(GT[:, :, 0:1], 0.0), after=[vt])
                sc = []
                for c2 in range(NC):
                    sc.append(P.op("v", lambda v, c2=c2, cn=cn: v.tensor_tensor_scan(
                        GT[:, c2, 1:129], GK[:, c2, cn * 128:(cn + 1) * 128],
                        STAT[:, 62:63].broadcast_to([128, 128]), 0.0,
                        ALU.add, ALU.add), after=[z0 if c2 == 0 else sc[-1], l5]))
                vt = sc[-1]
                dc1 = P.op("s", lambda s: s.activation(
                    DC[:, :], GT[:, :, 128:129].rearrange("p c n -> p (c n)"),
                    AF.Exp), after=[vt, stt])
                stt = dc1
                for h in range(4):
                    f1 = P.op("s", lambda s, h=h: s.activation(
                        QG[:, :, :], GT[:, 2 * h:2 * h + 2, 1:129], AF.Exp),
                        after=[vt, stt, bank_free[3]])
                    f2 = P.op("s", lambda s, h=h: s.activation(
                        ET[:, :, :], GT[:, 2 * h:2 * h + 2, 1:129], AF.Exp,
                        scale=-1.0), after=[f1])
                    stt = f2
                    f3 = P.op("v", lambda v, h=h, cn=cn: v.scalar_tensor_tensor(
                        QG[:, :, :], QG[:, :, :], 1.0 / 16,
                        Q[:, 2 * h:2 * h + 2, cn * 128:(cn + 1) * 128],
                        ALU.mult, ALU.mult), after=[f1, vt])
                    f4 = P.op("v", lambda v, h=h, cn=cn: v.tensor_mul(
                        KP[:, :, :], ET[:, :, :],
                        K[:, 2 * h:2 * h + 2, cn * 128:(cn + 1) * 128]),
                        after=[f2, f3])
                    f5 = []
                    for j in range(2):
                        f5.append(P.op("v", lambda v, h=h, j=j: v.tensor_scalar(
                            ET[:, j, :], GT[:, 2 * h + j, 1:129],
                            GT[:, 2 * h + j, 128:129], None, ALU.subtract),
                            after=[f4]))
                    f6 = P.op("s", lambda s: s.activation(
                        ET[:, :, :], ET[:, :, :], AF.Exp, scale=-1.0),
                        after=[f5[0], f5[1], stt])
                    stt = f6
                    f7 = P.op("v", lambda v, h=h, cn=cn: v.tensor_mul(
                        KG[:, :, :], ET[:, :, :],
                        K[:, 2 * h:2 * h + 2, cn * 128:(cn + 1) * 128]),
                        after=[f6])
                    vt = f7
                    amm = None
                    for j in range(2):
                        amm = P.op("t", lambda t, j=j: t.matmul(
                            PS[2][:, 0:128], KP[:, j, :], QG[:, j, :],
                            start=(j == 0), stop=(j == 1)),
                            after=[f7, bank_free[2]] if j == 0 else [])
                    aev = P.op("v", lambda v: v.tensor_mul(
                        A2[:], PS[2][:, 0:128], TRIU[:]), after=[amm, vt])
                    vt = aev
                    bank_free[2] = aev
                    ktp = []
                    for j in range(2):
                        ktp.append(P.op("t", lambda t, j=j: t.transpose(
                            PS[4][:, j * 128:(j + 1) * 128], KG[:, j, :], IDF[:]),
                            after=[f7, bank_free[4]] if j == 0 else []))
                    kev = P.op("v", lambda v: v.tensor_copy(
                        KGt[:, :, :], PS[4][:, 0:256].rearrange("p (c n) -> p c n", c=2)),
                        after=[ktp[1], vt])
                    vt = kev
                    bank_free[4] = kev
                    omm = None
                    for j in range(2):
                        omm = P.op("t", lambda t, h=h, j=j: t.matmul(
                            PS[3][:, :], QG[:, j, :], S[:, 2 * h + j, :],
                            start=(j == 0), stop=False),
                            after=[f3, bank_free[3], vz] if j == 0 else [])
                    omm = P.op("t", lambda t, h=h, cn=cn: t.matmul(
                        PS[3][:, :], A2[:], V[:, cn, h * 512:(h + 1) * 512],
                        start=False, stop=True), after=[aev])
                    dmm = []
                    for j in range(2):
                        dmm.append(P.op("t", lambda t, h=h, cn=cn, j=j: t.matmul(
                            PS[6 + j][:, :], KGt[:, j, :],
                            V[:, cn, h * 512:(h + 1) * 512],
                            start=True, stop=True),
                            after=[kev, bank_free[6 + j]]))
                    for j in range(2):
                        su = P.op("v", lambda v, h=h, j=j: v.scalar_tensor_tensor(
                            S[:, 2 * h + j, :], S[:, 2 * h + j, :],
                            DC[:, 2 * h + j:2 * h + j + 1], PS[6 + j][:, :],
                            ALU.mult, ALU.add), after=[dmm[j], omm, dc1, vt])
                        vt = su
                        bank_free[6 + j] = su
                    oev = P.op("v", lambda v, h=h, cn=cn: v.tensor_mul(
                        OG[:, h * 512:(h + 1) * 512], PS[3][:, :],
                        SG[:, cn, h * 512:(h + 1) * 512]), after=[omm, vt])
                    vt = oev
                    bank_free[3] = oev
                og_tok = vt

                WFf = WF[:, :, :].rearrange("p c n -> p (c n)")
                m1 = P.op("s", lambda s, WFf=WFf: s.activation(
                    WFf[:, 0:1024], OG[:, 0:1024], AF.Square,
                    accum_out=CL[:, 9:10]), after=[og_tok, stt])
                m2 = P.op("s", lambda s, WFf=WFf: s.activation(
                    WFf[:, 1024:2048], OG[:, 1024:2048], AF.Square,
                    accum_out=CL[:, 10:11]), after=[m1])
                stt = m2
                m3 = P.op("v", lambda v: v.tensor_add(
                    CL[:, 9:10], CL[:, 9:10], CL[:, 10:11]), after=[m2, vt])
                m4 = P.op("v", lambda v: v.tensor_scalar(
                    CL[:, 10:11], CL[:, 9:10], 1.0 / 2048, 1e-8, ALU.mult, ALU.add),
                    after=[m3])
                m5 = P.op("v", lambda v: v.reciprocal(CL[:, 11:12], CL[:, 10:11]),
                          after=[m4])
                m6 = P.op("s", lambda s: s.activation(CL[:, 10:11], CL[:, 11:12],
                                                      AF.Sqrt), after=[m5, stt])
                stt = m6
                hv = []
                for half in range(2):
                    w1 = P.op("v", lambda v, half=half: v.tensor_mul(
                        QSz[:], OG[:, half * 1024:(half + 1) * 1024],
                        NOB[:, half * 1024:(half + 1) * 1024]), after=[m6, vt, d_nob])
                    w2 = P.op("v", lambda v, half=half: v.reduce_max(
                        CL[:, 4 + half:5 + half], QSz[:], axis=AXX,
                        apply_absolute_value=True), after=[w1])
                    hv.append(w2)
                    vt = w2
                w3 = P.op("v", lambda v: v.tensor_max(
                    CL[:, 4:5], CL[:, 4:5], CL[:, 5:6]), after=[hv[0], hv[1]])
                w4 = P.op("v", lambda v: v.tensor_scalar(
                    CL[:, 5:6], CL[:, 4:5], CL[:, 10:11], 1e-5, ALU.mult, ALU.max),
                    after=[w3])
                w5 = P.op("v", lambda v: v.reciprocal(CL[:, 6:7], CL[:, 5:6]),
                          after=[w4])
                w6 = P.op("v", lambda v: v.scalar_tensor_tensor(
                    CL[:, 7:8], CL[:, 6:7], 127.0, CL[:, 10:11], ALU.mult, ALU.mult),
                    after=[w5])
                w7 = P.op("v", lambda v: v.tensor_scalar(
                    CL[:, 8:9], CL[:, 5:6], 1.0 / 127, None, ALU.mult), after=[w6])
                vt = w7
                ytoks = [None, None]
                for half in range(2):
                    en = "v"
                    Zb, Ib, Tb, Db = (QSz, QSi, QSt, QSd) if half == 0 else (Q2z, Q2i, Q2t, Q2d)
                    ch = vt
                    y1 = P.op(en, lambda v, half=half, Zb=Zb: v.tensor_mul(
                        Zb[:], OG[:, half * 1024:(half + 1) * 1024],
                        NOB[:, half * 1024:(half + 1) * 1024]),
                        after=[ch, w7, zb_free[half]])
                    y2 = P.op("s", lambda s, Zb=Zb: s.activation(
                        Zb[:], Zb[:], AF.Identity, scale=CL[:, 7:8]),
                        after=[y1, stt])
                    stt = y2
                    y7_, sy = _round_fix(P, en, Zb, Ib, Tb, Db, Zb, y2, stt)
                    stt = sy
                    y8 = P.op(en, lambda v, Zb=Zb, Tb=Tb: v.tensor_scalar(
                        Zb[:], Tb[:], 1.0, CL[:, 8:9], ALU.subtract, ALU.mult),
                        after=[y7_])
                    ytoks[half] = y8
                    vt = y8
                    otp = []
                    Zb2 = QSz if half == 0 else Q2z
                    for c in range(NC):
                        bank = 5 + c // 4
                        cc = c % 4
                        a = [ytoks[half], bank_free[bank]] if (c % 4 == 0) else []
                        otp.append(P.op("t", lambda t, c=c, bank=bank, cc=cc, Zb2=Zb2:
                                        t.transpose(
                            PSB[bank][:, cc * 128:(cc + 1) * 128],
                            Zb2[:, c * 128:(c + 1) * 128], IDB[:]), after=a))
                    zb_free[half] = otp[-1]
                    for hh in range(2):
                        bank = 5 + hh
                        ev = P.op("v", lambda v, half=half, hh=hh, bank=bank: v.tensor_copy(
                            OQF[:, half * 8 + hh * 4:half * 8 + (hh + 1) * 4, :],
                            PSB[bank][:, 0:512].rearrange("p (c n) -> p c n", c=4)),
                            after=[otp[hh * 4 + 3], vt])
                        vt = ev
                        bank_free[bank] = ev
                oq_tok = vt

                WLv = WL[:, 0:8192].rearrange("p (c n) -> p c n", c=8)
                mm = None
                for rnd in range(2):
                    d_wo = P.dma("y", WLv[:, :, :],
                                 qscr["owT"][:, 0, rnd * 8:(rnd + 1) * 8, :],
                                 after=[wl_free, mm], new=True)
                    for jj in range(NC):
                        bank = jj // 4
                        cc = jj % 4
                        for i in range(NC):
                            mm = P.op("t", lambda t, i=i, jj=jj, rnd=rnd, bank=bank,
                                      cc=cc: t.matmul(
                                PS[bank][:, cc * 128:(cc + 1) * 128],
                                WLv[:, i, jj * 128:(jj + 1) * 128],
                                OQF[:, rnd * 8 + i, :],
                                start=(rnd == 0 and i == 0 and jj % 4 == 0),
                                stop=(rnd == 1 and i == 7 and jj % 4 == 3)),
                                after=[d_wo, oq_tok, bank_free[bank]]
                                if (jj % 4 == 0 and i == 0) else [])
                    wl_free = mm
                for jj in range(NC):
                    bank = jj // 4
                    cc = jj % 4
                    ev = P.op("v", lambda v, jj=jj, cn=cn, t0=t0, bank=bank, cc=cc:
                              v.scalar_tensor_tensor(
                        XA[:, jj, t0 + cn * 128:t0 + (cn + 1) * 128],
                        PS[bank][:, cc * 128:(cc + 1) * 128],
                        SCL[:, QWI["owT"]:QWI["owT"] + 1],
                        XA[:, jj, t0 + cn * 128:t0 + (cn + 1) * 128],
                        ALU.mult, ALU.add), after=[mm, vt])
                    vt = ev
                bank_free[0] = vt
                bank_free[1] = vt
            x2_free = vt
        return dict(v=P.tok("v"), t=P.tok("t"), s=P.tok("s"))



def _round_fix(P, en, Zb, Ib, Tb, Db, M, prev, sprev=None):
    o1 = P.op("s", lambda s, Zb=Zb, Ib=Ib: s.activation(Ib[:], Zb[:], AF.Identity),
              after=[prev, sprev])
    o2 = P.op("s", lambda s, Ib=Ib, Tb=Tb: s.activation(Tb[:], Ib[:], AF.Identity),
              after=[o1])
    o3 = P.op("v", lambda v, Zb=Zb, Tb=Tb, Db=Db: v.tensor_sub(Db[:], Zb[:], Tb[:]),
              after=[o2, prev])
    o4 = P.op("v", lambda v, Tb=Tb, Db=Db: v.scalar_tensor_tensor(
        Tb[:], Db[:], 0.5, Tb[:], ALU.is_ge, ALU.add), after=[o3])
    o5 = P.op("v", lambda v, Tb=Tb, Db=Db: v.scalar_tensor_tensor(
        Tb[:], Db[:], -0.5, Tb[:], ALU.is_gt, ALU.add), after=[o4])
    return o5, o2


def _act_quant_fm(P, flat, UB, WB, QSV, QSG, vt, gt, ready=None, stt=None):
    st = [stt]
    """Feature-major act-quant of flat [128, 32*512] in place; pieces alternate
    vector/gpsimd. QSV/QSG = (QI, QT, QD) scratch per engine. Returns (vt, gt)."""
    piece = 1024
    for pc in range(16384 // piece):
        o0 = pc * piece
        en = "v"
        QI, QT, QD, QM = QSV if pc % 2 == 0 else QSG
        ch = vt
        q1 = P.op(en, lambda v, o0=o0: v.tensor_mul(
            flat[:, o0:o0 + piece], flat[:, o0:o0 + piece], UB[:, 0:piece]),
            after=[ch, ready if pc < 2 else None])
        q6, s6 = _round_fix(P, en, flat[:, o0:o0 + piece], QI[:, 0:piece],
                            QT[:, 0:piece], QD[:, 0:piece], QM[:, 0:piece], q1, st[0])
        st[0] = s6
        q7 = P.op(en, lambda v, QT=QT: v.tensor_scalar(
            QT[:, 0:piece], QT[:, 0:piece], 1.0, None, ALU.subtract), after=[q6])
        q8 = P.op(en, lambda v, o0=o0, QT=QT: v.tensor_mul(
            flat[:, o0:o0 + piece], QT[:, 0:piece], WB[:, 0:piece]), after=[q7])
        vt = q8
    return vt, gt, st[0]


def stage_l1ffn(ctx, toks):
    P, sb = ctx["P"], ctx["sb"]
    XA, IDF, IDB, PS, STAT, SCL = (ctx["XA"], ctx["IDF"], ctx["IDB"],
                                   ctx["PS"], ctx["STAT"], ctx["SCL"])
    din, qscr, PSB = ctx["din"], ctx["qscr"], ctx["PSB"]

    with ExitStack() as st_:
        XQ3 = sb("b_xq3", [128, 8, 512], BF16, st_)
        H0 = sb("b_h0", [128, 32, 514], BF16, st_)
        HC = sb("b_hc", [128, 32, 512], BF16, st_)
        TL = sb("b_tl", [128, 32, 2], BF16, st_)
        FWJ = [sb(f"b_wj{i}", [128, 16, 128], BF16, st_) for i in range(3)]
        QZ = sb("b_qz", [128, 1024], BF16, st_)
        QI = sb("b_qi", [128, 1024], mybir.dt.int32, st_)
        QT = sb("b_qt", [128, 1024], BF16, st_)
        QD = sb("b_qd", [128, 1024], BF16, st_)
        Q2z = sb("b_q2z", [128, 1024], BF16, st_)
        Q2i = sb("b_q2i", [128, 1024], mybir.dt.int32, st_)
        Q2t = sb("b_q2t", [128, 1024], BF16, st_)
        Q2d = sb("b_q2d", [128, 1024], BF16, st_)
        CL2 = sb("b_cl2", [128, 16], F32, st_)
        MSKF = [sb(f"b_msk{i}", [128, 1024], BF16, st_) for i in range(2)]
        UB = sb("b_ub", [128, 1024], BF16, st_)
        WB = sb("b_wb", [128, 1024], BF16, st_)
        CWC = sb("b_cw", [128, 32, 3], F32, st_)
        CBC = sb("b_cb", [128, 32], F32, st_)
        CL = sb("b_cl", [128, 16], F32, st_)
        MX = sb("b_mx", [128, 8], F32, st_)
        UR = sb("b_ur", [1, 512], F32, st_)
        WR = sb("b_wr", [1, 512], F32, st_)
        ONE1 = sb("b_one", [1, 128], F32, st_)

        P.barrier()
        d_cw = P.dma("y", CWC[:], din["cw"], new=True)
        d_cb = P.dma("y", CBC[:], din["cb"])
        o1 = P.op("v", lambda v: v.memset(ONE1[:], 1.0))
        vt = o1
        gt = None
        stt = None
        zb_free = [None, None]
        wj_free = [None, None, None]
        bank_free = [None] * 8
        wi = 0
        sc1 = SCL[:, QWI["f1T"]:QWI["f1T"] + 1]
        sc2 = SCL[:, QWI["f2T"]:QWI["f2T"] + 1]

        def fwload(nm, j, half, after=()):
            nonlocal wi
            slot = wi % 3
            if half is None:
                dst, s_ = FWJ[slot][:, 0:8, :], qscr[nm][j]
            else:
                dst = FWJ[slot][:, 0:16, :]
                s_ = qscr[nm][j][:, half * 16:half * 16 + 16, :]
            tok = P.dma("y", dst, s_, after=[wj_free[slot]] + list(after), new=True)
            wi += 1
            return slot, tok

        for q in range(4):
            t0 = q * 512
            if q > 0:
                tl1 = P.op("v", lambda v: v.tensor_copy(TL[:], H0[:, :, 512:514]),
                           after=[vt])
                vt = tl1
            else:
                tl1 = None
            for tt in range(4):
                en = "v"
                Zb, Ib, Tb, Db = (QZ, QI, QT, QD) if tt % 2 == 0 else (Q2z, Q2i, Q2t, Q2d)
                CLx = CL if tt % 2 == 0 else CL2
                ch = vt
                tps = []
                for c in range(NC):
                    bank = 5 + c // 4
                    cc = c % 4
                    a = [bank_free[bank]] if (c % 4 == 0) else []
                    tps.append(P.op("t", lambda t, tt=tt, c=c, t0=t0, bank=bank, cc=cc:
                                    t.transpose(PS[bank][:, cc * 128:(cc + 1) * 128],
                                    XA[:, c, t0 + tt * 128:t0 + (tt + 1) * 128],
                                    IDF[:]), after=a + ([toks.get("v"), toks.get("s")]
                                    if q == 0 and tt == 0 and c == 0 else [])))
                e1 = P.op("v", lambda v, Zb=Zb: v.tensor_copy(
                    Zb[:, 0:512], PS[5][:, :]), after=[tps[3], vt, zb_free[tt % 2]])
                e2 = P.op("v", lambda v, Zb=Zb: v.tensor_copy(
                    Zb[:, 512:1024], PS[6][:, :]), after=[tps[7], e1])
                vt = e2
                bank_free[5] = e1
                bank_free[6] = e2
                a1 = P.op("v", lambda v, Zb=Zb, CLx=CLx: v.reduce_max(
                    CLx[:, 0:1], Zb[:], axis=AXX, apply_absolute_value=True),
                    after=[e2])
                a2 = P.op("v", lambda v, CLx=CLx: v.tensor_scalar(
                    CLx[:, 1:2], CLx[:, 0:1], 1e-5, None, ALU.max), after=[a1])
                a3 = P.op("v", lambda v, CLx=CLx: v.reciprocal(CLx[:, 2:3], CLx[:, 1:2]),
                          after=[a2])
                a4 = P.op("v", lambda v, CLx=CLx: v.tensor_scalar(
                    CLx[:, 3:4], CLx[:, 2:3], 127.0, None, ALU.mult), after=[a3])
                a5 = P.op("v", lambda v, CLx=CLx: v.tensor_scalar(
                    CLx[:, 4:5], CLx[:, 1:2], 1.0 / 127, None, ALU.mult), after=[a4])
                vt = a5
                b1 = P.op("s", lambda s, Zb=Zb, CLx=CLx: s.activation(
                    Zb[:], Zb[:], AF.Identity, scale=CLx[:, 3:4]),
                    after=[a5, ch, stt])
                stt = b1
                b6, sb6 = _round_fix(P, en, Zb, Ib, Tb, Db, MSKF[tt % 2], b1, stt)
                stt = sb6
                b7 = P.op(en, lambda v, Zb=Zb, Tb=Tb, CLx=CLx: v.tensor_scalar(
                    Zb[:], Tb[:], 1.0, CLx[:, 4:5], ALU.subtract, ALU.mult), after=[b6])
                vt = b7
                xtp = []
                for c in range(NC):
                    bank = 5 + c // 4
                    cc = c % 4
                    a = [b7, bank_free[bank]] if (c % 4 == 0) else []
                    xtp.append(P.op("t", lambda t, c=c, bank=bank, cc=cc, Zb=Zb:
                                    t.transpose(
                        PSB[bank][:, cc * 128:(cc + 1) * 128],
                        Zb[:, c * 128:(c + 1) * 128], IDB[:]), after=a))
                zb_free[tt % 2] = xtp[-1]
                for half in range(2):
                    bank = 5 + half
                    ev = P.op("v", lambda v, tt=tt, half=half, bank=bank: v.tensor_copy(
                        XQ3[:, half * 4:(half + 1) * 4, tt * 128:(tt + 1) * 128],
                        PSB[bank][:, 0:512].rearrange("p (c n) -> p c n", c=4)),
                        after=[xtp[half * 4 + 3], vt])
                    vt = ev
                    bank_free[bank] = ev
            xq_tok = vt
            for j in range(32):
                P.pump(1)
                slot, dtok = fwload("f1T", j, None)
                bank = j % 2
                mm = None
                for i in range(NC):
                    mm = P.op("t", lambda t, slot=slot, i=i, j=j, bank=bank: t.matmul(
                        PS[bank][:, :], FWJ[slot][:, i, :], XQ3[:, i, :],
                        start=(i == 0), stop=(i == NC - 1)),
                        after=[dtok, xq_tok, bank_free[bank]] if i == 0 else [])
                ev = P.op("s", lambda s, j=j, bank=bank: s.activation(
                    H0[:, j, 2:514], PS[bank][:, :], AF.Identity, scale=sc1),
                    after=[mm, stt, tl1])
                stt = ev
                bank_free[bank] = ev
                wj_free[slot] = mm
            h0_tok = stt
            if q == 0:
                tz = P.op("v", lambda v: v.memset(H0[:, :, 0:2], 0.0), after=[vt, h0_tok])
            else:
                tz = P.op("v", lambda v: v.tensor_copy(H0[:, :, 0:2], TL[:]),
                          after=[vt, h0_tok, tl1])
            vt = tz

            for j in range(32):
                c1 = P.op("v", lambda v, j=j: v.tensor_single_scalar(
                    QZ[:, 0:512], H0[:, j, 0:512], CWC[:, j, 0:1], ALU.mult),
                    after=[tz, vt, stt, d_cw, h0_tok, zb_free[0]])
                c2 = P.op("v", lambda v, j=j: v.scalar_tensor_tensor(
                    QZ[:, 0:512], H0[:, j, 1:513], CWC[:, j, 1:2], QZ[:, 0:512],
                    ALU.mult, ALU.add), after=[c1])
                c3 = P.op("v", lambda v, j=j: v.scalar_tensor_tensor(
                    QZ[:, 0:512], H0[:, j, 2:514], CWC[:, j, 2:3], QZ[:, 0:512],
                    ALU.mult, ALU.add), after=[c2])
                c4 = P.op("s", lambda s, j=j: s.activation(
                    HC[:, j, :], QZ[:, 0:512], AF.Silu, bias=CBC[:, j:j + 1]),
                    after=[c3, stt, d_cb])
                stt = c4
                vt = c3
            hc_tok = stt

            wlo = 1 if q == 0 else 0
            mzero = P.op("v", lambda v: v.memset(MX[:, 0:4], 0.0), after=[vt])
            vt = mzero
            for tt in range(4):
                for grp in range(8):
                    tps = []
                    for cc in range(4):
                        c = grp * 4 + cc
                        bank = 5
                        a = [hc_tok, bank_free[5]] if cc == 0 else []
                        tps.append(P.op("t", lambda t, c=c, tt=tt, cc=cc: t.transpose(
                            PSB[5][:, cc * 128:(cc + 1) * 128], HC[:, c, tt * 128:(tt + 1) * 128],
                            IDB[:]), after=a))
                    r1 = P.op("v", lambda v: v.reduce_max(
                        CL[:, 0:1], PSB[5][:, 0:512], axis=AXX, apply_absolute_value=True),
                        after=[tps[3], vt])
                    r2 = P.op("v", lambda v, tt=tt: v.tensor_max(
                        MX[:, tt:tt + 1], MX[:, tt:tt + 1], CL[:, 0:1]), after=[r1])
                    vt = r2
                    bank_free[5] = r1
            s1_ = P.op("v", lambda v: v.tensor_scalar(
                MX[:, 4:8], MX[:, 0:4], 1e-5, None, ALU.max), after=[vt])
            s2_ = P.op("v", lambda v: v.reciprocal(MX[:, 0:4], MX[:, 4:8]), after=[s1_])
            s3_ = P.op("v", lambda v: v.tensor_scalar(
                MX[:, 0:4], MX[:, 0:4], 127.0, None, ALU.mult), after=[s2_])
            s4_ = P.op("v", lambda v: v.tensor_scalar(
                MX[:, 4:8], MX[:, 4:8], 1.0 / 127, None, ALU.mult), after=[s3_])
            vt = s4_
            for which, COLS, ROW in ((0, MX[:, 0:4], UR), (1, MX[:, 4:8], WR)):
                for tt in range(4):
                    tpu = P.op("t", lambda t, COLS=COLS, tt=tt: t.transpose(
                        PS[6][0:1, 0:128], COLS[:, tt:tt + 1], IDF[:]),
                        after=[vt, s4_, bank_free[6]])
                    evu = P.op("v", lambda v, ROW=ROW, tt=tt: v.tensor_copy(
                        ROW[:, tt * 128:(tt + 1) * 128], PS[6][0:1, 0:128]),
                        after=[tpu, vt])
                    vt = evu
                    bank_free[6] = evu
                bmm = P.op("t", lambda t, ROW=ROW: t.matmul(
                    PS[6][:, :], ONE1[:], ROW[:, 0:512],
                    start=True, stop=True), after=[vt, bank_free[6], o1])
                bev1 = P.op("v", lambda v, which=which: v.tensor_copy(
                    (UB if which == 0 else WB)[:, 0:512], PS[6][:, :]),
                    after=[bmm, vt])
                bev = P.op("v", lambda v, which=which: v.tensor_copy(
                    (UB if which == 0 else WB)[:, 512:1024], PS[6][:, :]),
                    after=[bev1])
                vt = bev
                bank_free[6] = bev
            vt, gt, stt = _act_quant_fm(P, HC[:, :, :].rearrange("p c n -> p (c n)"),
                                        UB, WB, (QI, QT, QD, MSKF[0]),
                                        (Q2i, Q2t, Q2d, MSKF[1]), vt, gt,
                                        ready=vt, stt=stt)
            hq_tok = (vt, gt)

            for j in range(NC):
                P.pump(1)
                stoks = [None, None]
                slots = [None, None]
                for half in range(2):
                    slots[half], stoks[half] = fwload("f2T", j, half)
                bank = 2 + (j % 2)
                mm = None
                for ihalf in range(2):
                    for i2 in range(16):
                        i = ihalf * 16 + i2
                        mm = P.op("t", lambda t, s_=slots[ihalf], i2=i2, i=i, bank=bank:
                                  t.matmul(
                            PS[bank][:, :], FWJ[s_][:, i2, :],
                            HC[:, i, :], start=(i == 0), stop=(i == 31)),
                            after=[stoks[0], stoks[1], hq_tok[0], hq_tok[1],
                                   bank_free[bank]] if i == 0 else [])
                ev = P.op("v", lambda v, j=j, t0=t0, wlo=wlo, bank=bank:
                          v.scalar_tensor_tensor(
                    XA[:, j, t0 - 1 + wlo:t0 + 511], PS[bank][:, wlo:512], sc2,
                    XA[:, j, t0 - 1 + wlo:t0 + 511], ALU.mult, ALU.add),
                    after=[mm, vt])
                vt = ev
                bank_free[bank] = ev
                wj_free[slots[0]] = mm
                wj_free[slots[1]] = mm

        ee1 = P.op("v", lambda v: v.tensor_mul(
            QZ[:, 0:32], H0[:, :, 512:513].rearrange("p c n -> p (c n)"),
            CWC[:, :, 0]), after=[vt])
        ee2 = P.op("v", lambda v: v.tensor_mul(
            QZ[:, 32:64], H0[:, :, 513:514].rearrange("p c n -> p (c n)"),
            CWC[:, :, 1]), after=[ee1])
        ee3 = P.op("v", lambda v: v.tensor_add(QZ[:, 0:32], QZ[:, 0:32], QZ[:, 32:64]),
                   after=[ee2])
        ee3b = P.op("v", lambda v: v.tensor_add(QZ[:, 0:32], QZ[:, 0:32], CBC[:]),
                    after=[ee3])
        ee4 = P.op("s", lambda s: s.activation(
            QZ[:, 0:32], QZ[:, 0:32], AF.Silu), after=[ee3b, stt])
        stt = ee4
        et1 = P.op("t", lambda t: t.transpose(PSB[5][0:32, 0:128], QZ[:, 0:32], IDB[:]),
                   after=[ee4, bank_free[5]])
        et2 = P.op("v", lambda v: v.reduce_max(
            CL[0:32, 0:1], PSB[5][0:32, 0:128], axis=AXX, apply_absolute_value=True),
            after=[et1, vt])
        et3 = P.op("t", lambda t: t.transpose(PS[6][0:1, 0:32], CL[0:32, 0:1],
                                              IDF[0:32, 0:32]), after=[et2, bank_free[6]])
        et4 = P.op("v", lambda v: v.reduce_max(
            CL[0:1, 1:2], PS[6][0:1, 0:32], axis=AXX), after=[et3])
        et5 = P.op("v", lambda v: v.tensor_scalar(
            CL[0:1, 2:3], CL[0:1, 1:2], 1e-5, None, ALU.max), after=[et4])
        et6 = P.op("v", lambda v: v.reciprocal(CL[0:1, 3:4], CL[0:1, 2:3]), after=[et5])
        et7 = P.op("v", lambda v: v.tensor_scalar(
            CL[0:1, 4:5], CL[0:1, 3:4], 127.0, None, ALU.mult), after=[et6])
        et8 = P.op("v", lambda v: v.tensor_scalar(
            CL[0:1, 5:6], CL[0:1, 2:3], 1.0 / 127, None, ALU.mult), after=[et7])
        eb1 = P.op("t", lambda t: t.matmul(PS[6][:, 0:2], ONE1[0:1, :],
                                           CL[0:1, 4:6], start=True, stop=True),
                   after=[et8, bank_free[6]])
        eb2 = P.op("v", lambda v: v.tensor_copy(CL[:, 8:10], PS[6][:, 0:2]),
                   after=[eb1])
        bank_free[6] = eb2
        bank_free[5] = et2
        f1 = P.op("v", lambda v: v.tensor_single_scalar(
            QZ[:, 0:32], QZ[:, 0:32], CL[:, 8:9], ALU.mult), after=[eb2])
        f2 = P.op("v", lambda v: v.tensor_copy(QI[:, 0:32], QZ[:, 0:32]), after=[f1])
        f3 = P.op("v", lambda v: v.tensor_copy(QT[:, 0:32], QI[:, 0:32]), after=[f2])
        f4 = P.op("v", lambda v: v.tensor_sub(QD[:, 0:32], QZ[:, 0:32], QT[:, 0:32]),
                  after=[f3])
        f5 = P.op("v", lambda v: v.scalar_tensor_tensor(
            QT[:, 0:32], QD[:, 0:32], 0.5, QT[:, 0:32], ALU.is_ge, ALU.add), after=[f4])
        f6 = P.op("v", lambda v: v.scalar_tensor_tensor(
            QT[:, 0:32], QD[:, 0:32], -0.5, QT[:, 0:32], ALU.is_gt, ALU.add), after=[f5])
        f7 = P.op("v", lambda v: v.tensor_scalar(
            QZ[:, 0:32], QT[:, 0:32], 1.0, CL[:, 9:10], ALU.subtract, ALU.mult),
            after=[f6])
        vt = f7
        stoks = [None, None]
        slots = [None, None]
        for half in range(2):
            slots[half], stoks[half] = fwload("f2T", 0, half)
        for j in range(NC):
            if j > 0:
                for half in range(2):
                    slots[half], stoks[half] = fwload("f2T", j, half)
            mm = None
            for ihalf in range(2):
                for i2 in range(16):
                    i = ihalf * 16 + i2
                    mm = P.op("t", lambda t, s_=slots[ihalf], i2=i2, i=i, j=j: t.matmul(
                        PS[2][:, 0:1], FWJ[s_][:, i2, :], QZ[:, i:i + 1],
                        start=(i == 0), stop=(i == 31)),
                        after=[stoks[0], stoks[1], f7, bank_free[2]] if i == 0 else [])
            ev = P.op("v", lambda v, j=j: v.scalar_tensor_tensor(
                XA[:, j, 2047:2048], PS[2][:, 0:1], sc2, XA[:, j, 2047:2048],
                ALU.mult, ALU.add), after=[mm, vt])
            vt = ev
            bank_free[2] = ev
            wj_free[slots[0]] = mm
            wj_free[slots[1]] = mm
        return dict(v=P.tok("v"), t=P.tok("t"), s=P.tok("s"))


def stage_inputs(inputs):
    import ml_dtypes
    g = {k: np.asarray(v) for k, v in inputs.items()}
    sh = {}
    sh["idf"] = np.eye(128, dtype=np.float32)
    sh["idb"] = np.eye(128, dtype=np.float32).astype(ml_dtypes.bfloat16)
    m = np.zeros((128, 128), np.float32)
    m[np.triu_indices(128)] = 1.0
    sh["triu"] = m
    sh["ones"] = np.ones((128, 128), np.float32)
    sh["tmk"] = _colmajor(g["L0_attn_time_mix_k"])
    sh["tmv"] = _colmajor(g["L0_attn_time_mix_v"])
    sh["tmr"] = _colmajor(g["L0_attn_time_mix_r"])
    sh["td"] = _colmajor(g["L0_attn_time_decay"])
    sh["tf"] = _colmajor(g["L0_attn_time_first"])
    sh["wkT"] = _tr(g["L0_attn_Wk"]); sh["wvT"] = _tr(g["L0_attn_Wv"])
    sh["wrT"] = _tr(g["L0_attn_Wr"]); sh["woT"] = _tr(g["L0_attn_Wo"])
    sh["w1R"] = _jblk(g["L0_ffn_w1_w"]); sh["w2R"] = _jblk(g["L0_ffn_w2_w"])
    sh["h1R"] = _jblk(g["L0_ffn_h1_w"]); sh["h2R"] = _jblk(g["L0_ffn_h2_w"])
    sh["w3R"] = _jblk(g["L0_ffn_w3_w"])
    sh["s1"] = _colmajor(g["L0_ffn_w1_s"]); sh["s2"] = _colmajor(g["L0_ffn_w2_s"])
    sh["sh1"] = _colmajor(g["L0_ffn_h1_s"]); sh["sh2"] = _colmajor(g["L0_ffn_h2_s"])
    sh["s3"] = _colmajor(g["L0_ffn_w3_s"])
    sh["qwT"] = _tr(g["L1_attn_wq"]); sh["kwT"] = _tr(g["L1_attn_wk"])
    sh["vwT"] = _tr(g["L1_attn_wv"]); sh["gwT"] = _tr(g["L1_attn_wg"])
    sh["gk1wT"] = _tr(g["L1_attn_wgk1"]); sh["gk2wT"] = _tr(g["L1_attn_wgk2"])
    sh["owT"] = _tr(g["L1_attn_wo"])
    sh["f1T"] = _tr(g["L1_ffn_w1"]); sh["f2T"] = _tr(g["L1_ffn_w2"])
    sh["nqB"] = _bcast(g["L1_attn_nq"]); sh["nkB"] = _bcast(g["L1_attn_nk"])
    sh["nvB"] = _bcast(g["L1_attn_nv"]); sh["ngB"] = _bcast(g["L1_attn_ng"])
    sh["ngk1B"] = _bcast(g["L1_attn_ngk1"]); sh["noB"] = _bcast(g["L1_attn_no"])
    sh["ngk2c"] = np.ascontiguousarray(np.asarray(g["L1_attn_ngk2"], np.float32)[:, None])
    sh["ngk2B"] = _bcast(g["L1_attn_ngk2"])
    sh["bgk2c"] = _colmajor(g["L1_attn_bgk2"])
    sh["cw"] = np.ascontiguousarray(np.asarray(g["L1_ffn_conv_w"], np.float32)
                                    .reshape(DH, 3).reshape(32, 128, 3).transpose(1, 0, 2))
    sh["cb"] = _colmajor(g["L1_ffn_conv_b"])
    sh["cb32"] = sh["cb"]

    x = np.asarray(g["L0_attn_x"], np.float32)
    in_maps = []
    for b in range(B):
        im = dict(sh)
        im["xT"] = np.ascontiguousarray(x[b].T)
        in_maps.append(im)
    return in_maps


def kernel(**inputs):
    in_maps = stage_inputs(inputs)
    nc = build(upto="all", tap=None)
    res = run_bass_kernel_spmd(nc, in_maps, core_ids=list(range(B)))
    return np.stack([np.asarray(r["out"], np.float32) for r in res.results], axis=0)


# version 11
# speedup vs baseline: 1.0704x; 1.0430x over previous
import numpy as np
from contextlib import ExitStack

import concourse.bass as bass
import concourse.mybir as mybir
import concourse.bass_isa as bass_isa
from concourse.bass_utils import run_bass_kernel_spmd

F32 = mybir.dt.float32
BF16 = mybir.dt.bfloat16
AF = mybir.ActivationFunctionType
ALU = mybir.AluOpType
AXX = mybir.AxisListType.X
RADD = bass_isa.ReduceOp.add

B, T, C = 8, 2048, 1024
NC = 8
DH = 4096
NTT = T // 128

ENGS = ("t", "v", "s", "g", "y")
DMACH = {"g": "dG", "y": "dY", "s": "dS"}
NROT = 6


class Prog:

    def __init__(self, nc, sems):
        self.nc = nc
        self.sems = sems
        self.ops = {e: [] for e in ENGS}
        self.ncomp = {e: 0 for e in ENGS}
        self.need = {e: set() for e in ENGS}
        self.dmacnt = {}
        for q in DMACH.values():
            for i in range(NROT):
                self.dmacnt[q + str(i)] = 0
        self.dmarot = {q: 0 for q in DMACH.values()}
        self.dmagroup_open = {q: False for q in DMACH.values()}
        self.group_ends = {ch: [] for ch in self.dmacnt}
        self.pending = {e: [] for e in ENGS}
        self.bg = []

    def tok(self, e):
        return (e, self.ncomp[e])

    def _waits(self, e, after):
        w = []
        for t in list(after) + self.pending[e]:
            if t is None:
                continue
            kind, val = t
            if val <= 0:
                continue
            w.append((kind, val))
            if kind in ENGS:
                self.need[kind].add(val)
        self.pending[e] = []
        return w

    def op(self, e, fn, after=(), note=""):
        self.ops[e].append(dict(fn=fn, waits=self._waits(e, after), dma=None,
                                comp=self.ncomp[e], note=note))
        self.ncomp[e] += 1
        return (e, self.ncomp[e])

    def dma(self, e, out, in_, after=(), note="", new=False):
        q = DMACH[e]
        if new or not self.dmagroup_open[q]:
            cur = q + str(self.dmarot[q])
            if self.dmacnt[cur] > 0 and (not self.group_ends[cur]
                                         or self.group_ends[cur][-1] != self.dmacnt[cur]):
                self.group_ends[cur].append(self.dmacnt[cur])
            self.dmarot[q] = (self.dmarot[q] + 1) % NROT
            ch = q + str(self.dmarot[q])
            if self.dmacnt[ch] > 0:
                self.pending[e].append((ch, self.dmacnt[ch]))
            self.dmagroup_open[q] = True
        ch = q + str(self.dmarot[q])
        def fn(eng, out=out, in_=in_):
            return eng.dma_start(out=out, in_=in_)
        self.ops[e].append(dict(fn=fn, waits=self._waits(e, after), dma=ch,
                                comp=None, note=note))
        self.dmacnt[ch] += 16
        return (ch, self.dmacnt[ch])

    def dma_new(self, e):
        self.dmagroup_open[DMACH[e]] = False

    def barrier(self, engines=("t", "v", "s", "g", "y")):
        toks = {e: self.tok(e) for e in engines}
        for e in engines:
            for o in engines:
                if o != e:
                    self.pending[e].append(toks[o])

    def pump(self, n=1):
        for _ in range(n):
            if self.bg:
                self.bg.pop(0)()

    def pump_all(self):
        while self.bg:
            self.bg.pop(0)()

    def finalize(self, blk):
        import bisect
        for ch, tot in self.dmacnt.items():
            if tot > 0 and (not self.group_ends[ch] or self.group_ends[ch][-1] != tot):
                self.group_ends[ch].append(tot)
        for e in ENGS:
            for rec in self.ops[e]:
                w2 = []
                for kind, val in rec["waits"]:
                    if kind in self.group_ends:
                        ge = self.group_ends[kind]
                        val = ge[bisect.bisect_left(ge, val)]
                    w2.append((kind, val))
                rec["waits"] = w2
        incs = {e: {} for e in ENGS}
        for e in ENGS:
            need = sorted(self.need[e])
            if need and need[-1] > self.ncomp[e]:
                raise RuntimeError(f"wait beyond issued ops on {e}")
            full, prev = [], 0
            for p in need:
                while p - prev > 32:
                    prev += 32
                    full.append(prev)
                if p != prev:
                    full.append(p)
                    prev = p
            comp2idx = {}
            for i, rec in enumerate(self.ops[e]):
                if rec["comp"] is not None:
                    comp2idx[rec["comp"]] = i
            prev = 0
            for p in full:
                incs[e][comp2idx[p - 1]] = p - prev
                prev = p

        prog = self

        def make_body(e):
            def body(eng):
                seen = {}
                for i, rec in enumerate(prog.ops[e]):
                    for kind, val in rec["waits"]:
                        if seen.get(kind, 0) >= val:
                            continue
                        eng.wait_ge(prog.sems[kind], val)
                        seen[kind] = val
                    inst = rec["fn"](eng)
                    if rec["dma"] is not None:
                        inst.then_inc(prog.sems[rec["dma"]], 16)
                    amt = incs[e].get(i)
                    if amt:
                        inst.then_inc(prog.sems[e], amt)
            return body

        blk.tensor(make_body("t"))
        blk.vector(make_body("v"))
        blk.scalar(make_body("s"))
        blk.gpsimd(make_body("g"))
        blk.sync(make_body("y"))



def _colmajor(v):
    v = np.asarray(v, np.float32)
    n = v.size // 128
    return np.ascontiguousarray(v.reshape(n, 128).T)


def _bcast(v):
    v = np.asarray(v, np.float32)
    return np.ascontiguousarray(np.broadcast_to(v[None, :], (128, v.size)))


def _tr(w):
    return np.ascontiguousarray(np.asarray(w, np.float32).T)


def _jblk(w):
    wT = np.asarray(w, np.float32).T
    di, do = wT.shape
    return np.ascontiguousarray(
        wT.reshape(di // 128, 128, do // 128, 128).transpose(2, 1, 0, 3))


QW = [
    ("qwT", C, C), ("kwT", C, C), ("vwT", C, 2048), ("gwT", C, 2048),
    ("gk1wT", C, 16), ("gk2wT", 16, C), ("owT", 2048, C),
    ("f1T", C, DH), ("f2T", DH, C),
]
QWI = {nm: i for i, (nm, _, _) in enumerate(QW)}


def build(upto="all", tap=None):
    nc = bass.Bass("TRN2", target_bir_lowering=False)

    din = {}
    def dram_in(name, shape, dt=F32):
        din[name] = nc.dram_tensor(name, list(shape), dt, kind="ExternalInput").ap()
        return din[name]

    xT = dram_in("xT", [C, T])
    dram_in("idf", [128, 128]); dram_in("idb", [128, 128], BF16)
    dram_in("triu", [128, 128])
    dram_in("ones", [128, 128])
    for nm in ("tmk", "tmv", "tmr", "td", "tf", "s3", "bgk2c", "cb32"):
        dram_in(nm, [128, NC] if nm not in ("cb32",) else [128, 32])
    dram_in("cb", [128, 32])
    for nm in ("wkT", "wvT", "wrT", "woT"):
        dram_in(nm, [C, C])
    dram_in("w1R", [32, 128, 8, 128]); dram_in("w2R", [32, 128, 8, 128])
    dram_in("h1R", [32, 128, 32, 128]); dram_in("h2R", [32, 128, 32, 128])
    dram_in("w3R", [8, 128, 32, 128])
    for nm in ("s1", "s2", "sh1", "sh2"):
        dram_in(nm, [128, 32])
    for nm, di, do in QW:
        dram_in(nm, [di, do])
    for nm in ("nqB", "nkB", "nvB", "ngB", "ngk1B"):
        dram_in(nm, [128, C])
    dram_in("noB", [128, 2048])
    dram_in("ngk2c", [16, 1])
    dram_in("ngk2B", [128, 16])
    dram_in("cw", [128, 32, 3])

    out = nc.dram_tensor("out", [T, C], F32, kind="ExternalOutput").ap()
    tap_t = None
    if tap is not None:
        tap_t = nc.dram_tensor("tap", list(tap[0]), tap[1],
                               kind="ExternalOutput").ap()

    QSHAPE = {
        "qwT": [128, 1, 8, 1024], "kwT": [128, 1, 8, 1024],
        "vwT": [128, 2, 8, 1024], "gwT": [128, 2, 8, 1024],
        "owT": [128, 1, 16, 1024],
        "f1T": [32, 128, 8, 128], "f2T": [8, 128, 32, 128],
        "gk1wT": [C, 16], "gk2wT": [16, C],
    }
    qscr = {}
    for nm, di, do in QW:
        qscr[nm] = nc.dram_tensor("q_" + nm, QSHAPE[nm], BF16, kind="Internal").ap()

    with ExitStack() as es:
        sems = {}
        semnames = list(ENGS) + [q + str(i) for q in DMACH.values() for i in range(NROT)]
        for e in semnames:
            sems[e] = es.enter_context(nc.semaphore("sem_" + e))
        P = Prog(nc, sems)

        def sb(name, shape, dt=F32, stack=None):
            return (stack or es).enter_context(nc.sbuf_tensor(name, list(shape), dt))

        XA = sb("XA", [128, NC, T])
        IDF = sb("c_idf", [128, 128])
        IDB = sb("c_idb", [128, 128], BF16)
        TRIU = sb("c_triu", [128, 128])
        ONES = sb("c_ones", [128, 128])
        SVEC = sb("SVEC", [128, 64])
        SCL = sb("SCL", [128, 16])
        STAT = sb("STAT", [128, 64])
        PS = [es.enter_context(nc.psum_tensor(f"ps{b}", [128, 512], F32))
              for b in range(8)]
        PSB = [PS[b][:, :].bitcast(BF16) for b in range(8)]

        P.dma("y", IDF[:], din["idf"])
        d_idb = P.dma("y", IDB[:], din["idb"])
        d_triu = P.dma("y", TRIU[:], din["triu"])
        d_ones = P.dma("y", ONES[:], din["ones"])
        P.dma("y", SVEC[:, 0:8], din["tmk"])
        P.dma("y", SVEC[:, 8:16], din["tmv"])
        d_tm = P.dma("y", SVEC[:, 16:24], din["tmr"])
        d_td = P.dma("y", SVEC[:, 24:32], din["td"])
        d_tf = P.dma("y", SVEC[:, 32:40], din["tf"])
        P.dma("y", SVEC[:, 40:48], din["s3"])
        d_sv = P.dma("y", SVEC[:, 48:56], din["bgk2c"])
        d_x = None
        for c in range(NC):
            d_x = P.dma("y", XA[:, c, :], xT[c * 128:(c + 1) * 128, :],
                        new=(c == 0))
        s_ew1 = P.op("s", lambda s: s.activation(SVEC[:, 24:32], SVEC[:, 24:32], AF.Exp),
                     after=[d_td])
        s_ew = P.op("s", lambda s: s.activation(SVEC[:, 24:32], SVEC[:, 24:32], AF.Exp,
                                                scale=-1.0), after=[s_ew1])
        s_eu = P.op("s", lambda s: s.activation(SVEC[:, 32:40], SVEC[:, 32:40], AF.Exp),
                    after=[d_tf])

        pre_stack = ExitStack()
        pre = dict(
            ins=[sb(f"pre_in{i}", [128, 1024], F32, pre_stack) for i in range(3)],
            outs=[sb(f"pre_out{i}", [128, 1024], BF16, pre_stack) for i in range(2)],
            acc=sb("pre_acc", [128, 1], F32, pre_stack),
            reds=[sb(f"pre_red{i}", [128, 1], F32, pre_stack) for i in range(2)],
            rs=sb("pre_rs", [128, 1], F32, pre_stack),
            cmp=sb("pre_cmp", [128, 1024], F32, pre_stack))
        _emit_prepass(P, pre, din, qscr, SCL, ONES, PS, d_ones)

        ctx = dict(P=P, nc=nc, sb=sb, es=es, XA=XA, IDF=IDF, IDB=IDB, TRIU=TRIU,
                   SVEC=SVEC, SCL=SCL, STAT=STAT, PS=PS, PSB=PSB, din=din, qscr=qscr,
                   xready=d_x, svec_ready=(s_ew, s_eu, d_sv, d_tm))

        ctx["out"] = out
        order = ["l0attn", "ln1", "l0ffn", "ln2", "l1attn", "ln3", "l1ffn", "ln4"]
        stop = upto if upto in order or upto in ("l0ffnA", "l0ffnB") else "ln4"
        toks = stage_l0attn(ctx)
        done = upto == "l0attn"
        if not done:
            toks = stage_ln(ctx, 1, toks)
            P.pump_all()
            pre_stack.close()
            done = upto == "ln1"
        else:
            P.pump_all()
            pre_stack.close()
        if not done and upto in ("l0ffnA", "l0ffnB"):
            toksA = stage_l0ffn(ctx, toks, debug_stop=upto[-1])
            P.barrier()
            tk = [P.tok(e) for e in ("t", "v", "s", "g")]
            H0d = toksA["H0"]
            last = None
            for jj in range(32):
                last = P.dma("y", tap_t[jj * 128:(jj + 1) * 128, :],
                             H0d[:, jj * 1024:(jj + 1) * 1024], after=tk,
                             new=(jj == 0))
            P.op("y", lambda y: y.nop(), after=[last])
            done = True
        if not done:
            toks = stage_l0ffn(ctx, toks)
            done = upto == "l0ffn"
        if not done:
            toks = stage_ln(ctx, 2, toks)
            done = upto == "ln2"
        if not done:
            toks = stage_l1attn(ctx, toks)
            done = upto == "l1attn"
        if not done:
            toks = stage_ln(ctx, 3, toks)
            done = upto == "ln3"
        if not done:
            toks = stage_l1ffn(ctx, toks)
            done = upto == "l1ffn"
        if not done:
            toks = stage_ln(ctx, 4, toks)
            P.op("y", lambda y: y.nop(), after=[toks["out_dma"]])
            done = True
        elif tap_t is not None and upto not in ("l0ffnA", "l0ffnB"):
            _tap_xa(P, tap_t, XA)

        P.pump_all()
        blk = es.enter_context(nc.Block())
        P.finalize(blk)
    return nc


def _tap_xa(P, tap_t, XA):
    P.barrier()
    tk = [P.tok(e) for e in ("t", "v", "s", "g")]
    last = None
    for c in range(NC):
        last = P.dma("y", tap_t[c * 128:(c + 1) * 128, :], XA[:, c, :], after=tk,
                     new=(c == 0))
    P.op("y", lambda y: y.nop(), after=[last])


def _emit_prepass(P, pre, din, qscr, SCL, ONES, PS, d_ones):
    state = {"g": None, "v": None, "s": None, "tread": None,
             "ld": [None, None, None], "qd": [None, None, None],
             "redf": [None, None]}
    NR = 3
    for nm, di, do in QW:
        wi = QWI[nm]
        src, dst = din[nm], qscr[nm]
        tiles = []
        for r0 in range(0, di, 128):
            rr = min(128, di - r0)
            for c0 in range(0, do, 1024):
                tiles.append((r0, rr, c0, min(1024, do - c0)))
        inv_n = 1.0 / float(di * do)

        def t_zero():
            state["g"] = P.op("g", lambda g: g.memset(pre["acc"][:], 0.0),
                              after=[state["g"]])
        P.bg.append(t_zero)

        for ti, tile in enumerate(tiles):
            def t_red(tile=tile, ti=ti, src=src):
                r0, rr, c0, cc = tile
                buf = pre["ins"][ti % NR]
                red = pre["reds"][ti % 2]
                ld = P.dma("y", buf[0:rr, 0:cc], src[r0:r0 + rr, c0:c0 + cc],
                           after=[state["ld"][ti % NR]], new=True)
                s1 = P.op("s", lambda s: s.activation(
                    buf[0:rr, 0:cc], buf[0:rr, 0:cc], AF.Abs,
                    accum_out=red[0:rr, :]), after=[ld, state["s"],
                                                    state["redf"][ti % 2]])
                state["s"] = s1
                state["ld"][ti % NR] = s1
                state["g"] = P.op("g", lambda g: g.tensor_add(
                    pre["acc"][0:rr, :], pre["acc"][0:rr, :], red[0:rr, :]),
                    after=[s1, state["g"]])
                state["redf"][ti % 2] = state["g"]
            P.bg.append(t_red)

        def t_scale(wi=wi, inv_n=inv_n):
            mm = P.op("t", lambda t: t.matmul(
                PS[7][:, 0:1], ONES[:], pre["acc"][:, :], start=True, stop=True),
                after=[state["g"], d_ones, state["tread"]])
            state["tread"] = P.op("v", lambda v: v.tensor_scalar(
                SCL[:, wi:wi + 1], PS[7][:, 0:1], inv_n, 1e-5,
                ALU.mult, ALU.max), after=[mm, state["v"]])
            state["v"] = state["tread"]
            state["g"] = state["tread"]

        P.bg.append(t_scale)

        def t_recip(wi=wi):
            state["v"] = P.op("v", lambda v: v.reciprocal(
                pre["rs"][:], SCL[:, wi:wi + 1]), after=[state["g"], state["v"]])
        P.bg.append(t_recip)

        for ti, tile in enumerate(tiles):
            def t_quant(tile=tile, ti=ti, src=src, dst=dst, nm=nm):
                r0, rr, c0, cc = tile
                buf = pre["ins"][ti % NR]
                obuf = pre["outs"][ti % 2]
                ld = P.dma("y", buf[0:rr, 0:cc], src[r0:r0 + rr, c0:c0 + cc],
                           after=[state["ld"][ti % NR], state["v"]], new=True)
                v1 = P.op("v", lambda v: v.tensor_single_scalar(
                    buf[0:rr, 0:cc], buf[0:rr, 0:cc], pre["rs"][0:rr, :], ALU.mult),
                    after=[ld, state["v"]])
                v2 = P.op("v", lambda v: v.tensor_scalar(
                    pre["cmp"][0:rr, 0:cc], buf[0:rr, 0:cc], -0.5, -1.0,
                    ALU.is_le, ALU.mult), after=[v1])
                v4 = P.op("v", lambda v: v.scalar_tensor_tensor(
                    obuf[0:rr, 0:cc], buf[0:rr, 0:cc], 0.5, pre["cmp"][0:rr, 0:cc],
                    ALU.is_ge, ALU.add), after=[v2, state["qd"][ti % 2]])
                state["v"] = v4
                state["ld"][ti % NR] = v4
                if nm in ("f1T", "f2T"):
                    j0 = c0 // 128
                    nj = cc // 128
                    dview = dst[j0:j0 + nj, :, r0 // 128, :].rearrange("a p n -> p a n")
                    sview = obuf[0:rr, 0:cc].rearrange("p (a n) -> p a n", a=nj)
                    state["qd"][ti % 2] = P.dma("g", dview, sview,
                                                after=[v4], new=True)
                elif nm in ("gk1wT", "gk2wT"):
                    state["qd"][ti % 2] = P.dma("g", dst[r0:r0 + rr, c0:c0 + cc],
                                                obuf[0:rr, 0:cc], after=[v4], new=True)
                else:
                    state["qd"][ti % 2] = P.dma(
                        "g", dst[:, c0 // 1024, r0 // 128, 0:cc],
                        obuf[0:rr, 0:cc], after=[v4], new=True)
            P.bg.append(t_quant)


def stage_l0attn(ctx):
    P, sb, es = ctx["P"], ctx["sb"], ctx["es"]
    XA, SVEC, PS, din = ctx["XA"], ctx["SVEC"], ctx["PS"], ctx["din"]
    s_ew, s_eu, d_sv, d_tm = ctx["svec_ready"]
    xready = ctx["xready"]

    with ExitStack() as st_:
        WK = sb("l0_wk", [128, NC, C], BF16, st_)
        WV = sb("l0_wv", [128, NC, C], BF16, st_)
        WR = sb("l0_wr", [128, NC, C], BF16, st_)
        WO = sb("l0_wo", [128, NC, C], BF16, st_)
        XK = sb("l0_xk", [128, NC, 512], BF16, st_)
        XV = sb("l0_xv", [128, NC, 512], BF16, st_)
        XR = sb("l0_xr", [128, NC, 512], BF16, st_)
        RW = sb("l0_rw", [128, NC, 512], BF16, st_)
        TMPD = sb("l0_tmpd", [128, 512], F32, st_)
        XPREV = sb("l0_xprev", [128, NC, 1], F32, st_)
        NS = 2
        SL_EK = [sb(f"l0_ek{i}", [128, 512], F32, st_) for i in range(NS)]
        SL_EKV = [sb(f"l0_ekv{i}", [128, 512], F32, st_) for i in range(NS)]
        SL_A = [sb(f"l0_A{i}", [128, 513], F32, st_) for i in range(NS)]
        SL_B = [sb(f"l0_B{i}", [128, 513], F32, st_) for i in range(NS)]
        SL_SR = [sb(f"l0_sr{i}", [128, 512], BF16, st_) for i in range(NS)]
        CARA = sb("l0_carA", [128, NC, 1], F32, st_)
        CARB = sb("l0_carB", [128, NC, 1], F32, st_)

        wtok = None
        first = True
        for Wb, nm in ((WK, "wkT"), (WV, "wvT"), (WR, "wrT"), (WO, "woT")):
            for c in range(NC):
                wtok = P.dma("g", Wb[:, c, :], din[nm][c * 128:(c + 1) * 128, :],
                             new=first)
                first = False

        v0 = P.op("v", lambda v: v.memset(CARA[:], 0.0))
        v0 = P.op("v", lambda v: v.memset(CARB[:], 0.0), after=[v0])
        vprev = P.op("v", lambda v: v.memset(XPREV[:], 0.0), after=[v0])

        tmk, tmv, tmr = SVEC[:, 0:8], SVEC[:, 8:16], SVEC[:, 16:24]
        ew, eu = SVEC[:, 24:32], SVEC[:, 32:40]

        slot_free = [None] * NS
        bank_free = [None] * 8
        vt = vprev
        st = None

        for s in range(4):
            s0 = s * 512
            mix_start = [xready, d_tm, vprev, vt]
            for c in range(NC):
                P.pump(1)
                v1 = P.op("v", lambda v, c=c, s0=s0: v.tensor_sub(
                    TMPD[:, 0:1], XA[:, c, s0:s0 + 1], XPREV[:, c, :]),
                    after=mix_start if c == 0 else [vt])
                v2 = P.op("v", lambda v, c=c, s0=s0: v.tensor_sub(
                    TMPD[:, 1:512], XA[:, c, s0 + 1:s0 + 512], XA[:, c, s0:s0 + 511]),
                    after=[v1])
                vt = v2
                for XBUF, tm in ((XK, tmk), (XV, tmv), (XR, tmr)):
                    m1 = P.op("v", lambda v, XBUF=XBUF, tm=tm, c=c: v.scalar_tensor_tensor(
                        XBUF[:, c, 0:1], TMPD[:, 0:1], tm[:, c:c + 1], XPREV[:, c, :],
                        ALU.mult, ALU.add), after=[vt])
                    vt = P.op("v", lambda v, XBUF=XBUF, tm=tm, c=c, s0=s0: v.scalar_tensor_tensor(
                        XBUF[:, c, 1:512], TMPD[:, 1:512], tm[:, c:c + 1],
                        XA[:, c, s0:s0 + 511], ALU.mult, ALU.add), after=[m1])
            mix_all = P.tok("v")

            for c in range(NC):
                P.pump(1)
                sl = c % NS
                ek, ekv, A, Bt, sr = SL_EK[sl], SL_EKV[sl], SL_A[sl], SL_B[sl], SL_SR[sl]
                bk = c % 2
                bv = bk ^ 1

                mmK = None
                for i in range(NC):
                    mmK = P.op("t", lambda t, c=c, i=i, bk=bk: t.matmul(
                        PS[bk][:, :], WK[:, i, c * 128:(c + 1) * 128], XK[:, i, :],
                        start=(i == 0), stop=(i == NC - 1)),
                        after=[wtok, mix_all, bank_free[bk]] if i == 0 else [])
                e1 = P.op("s", lambda s_, ek=ek, bk=bk: s_.activation(
                    ek[:], PS[bk][:, :], AF.Exp), after=[mmK, slot_free[sl], st])
                st = e1
                mmV = None
                for i in range(NC):
                    mmV = P.op("t", lambda t, c=c, i=i, bv=bv: t.matmul(
                        PS[bv][:, :], WV[:, i, c * 128:(c + 1) * 128], XV[:, i, :],
                        start=(i == 0), stop=(i == NC - 1)),
                        after=[bank_free[bv]] if i == 0 else [])
                e2 = P.op("v", lambda v, ek=ek, ekv=ekv, bv=bv: v.tensor_mul(
                    ekv[:], ek[:], PS[bv][:, :]), after=[mmV, e1, vt, slot_free[sl]])
                vt = e2
                bank_free[bk] = e1
                bank_free[bv] = e2
                mmR = None
                for i in range(NC):
                    mmR = P.op("t", lambda t, c=c, i=i: t.matmul(
                        PS[2][:, :], WR[:, i, c * 128:(c + 1) * 128], XR[:, i, :],
                        start=(i == 0), stop=(i == NC - 1)),
                        after=[bank_free[2]] if i == 0 else [])
                e3 = P.op("s", lambda s_, sr=sr: s_.activation(
                    sr[:], PS[2][:, :], AF.Sigmoid), after=[mmR, st, slot_free[sl]])
                st = e3
                bank_free[2] = e3

                c1 = P.op("v", lambda v, A=A, c=c: v.tensor_copy(
                    A[:, 0:1], CARA[:, c, :]), after=[e2, vt])
                c2 = P.op("v", lambda v, Bt=Bt, c=c: v.tensor_copy(
                    Bt[:, 0:1], CARB[:, c, :]), after=[c1])
                sc1 = P.op("v", lambda v, A=A, ekv=ekv, c=c: v.tensor_tensor_scan(
                    A[:, 1:513], ew[:, c:c + 1].broadcast_to([128, 512]), ekv[:],
                    A[:, 0:1], ALU.mult, ALU.add), after=[c2, s_ew])
                sc2 = P.op("v", lambda v, Bt=Bt, ek=ek, c=c: v.tensor_tensor_scan(
                    Bt[:, 1:513], ew[:, c:c + 1].broadcast_to([128, 512]), ek[:],
                    Bt[:, 0:1], ALU.mult, ALU.add), after=[sc1, e1])
                c3 = P.op("v", lambda v, A=A, c=c: v.tensor_copy(
                    CARA[:, c, :], A[:, 512:513]), after=[sc1])
                c4 = P.op("v", lambda v, Bt=Bt, c=c: v.tensor_copy(
                    CARB[:, c, :], Bt[:, 512:513]), after=[sc2])
                n1 = P.op("v", lambda v, ekv=ekv, A=A, c=c: v.scalar_tensor_tensor(
                    ekv[:], ekv[:], eu[:, c:c + 1], A[:, 0:512], ALU.mult, ALU.add),
                    after=[c3, s_eu])
                n2 = P.op("v", lambda v, ek=ek, Bt=Bt, c=c: v.scalar_tensor_tensor(
                    ek[:], ek[:], eu[:, c:c + 1], Bt[:, 0:512], ALU.mult, ALU.add),
                    after=[c4, n1])
                n3 = P.op("v", lambda v, ek=ek: v.reciprocal(
                    ek[:], ek[:]), after=[n2])
                n3b = P.op("v", lambda v, ekv=ekv, ek=ek: v.tensor_mul(
                    ekv[:], ekv[:], ek[:]), after=[n3])
                n4 = P.op("v", lambda v, ekv=ekv, sr=sr, c=c: v.tensor_mul(
                    RW[:, c, :], ekv[:], sr[:]), after=[n3b, e3])
                vt = n4
                slot_free[sl] = n4

            vsave = P.op("v", lambda v, s0=s0: v.tensor_copy(
                XPREV[:, :, :], XA[:, :, s0 + 511:s0 + 512]), after=[vt])
            vprev = vsave
            vt = vsave
            rw_all = P.tok("v")
            for c in range(NC):
                P.pump(1)
                bo = 3 + (c % 2)
                mmO = None
                for i in range(NC):
                    mmO = P.op("t", lambda t, c=c, i=i, bo=bo: t.matmul(
                        PS[bo][:, :], WO[:, i, c * 128:(c + 1) * 128], RW[:, i, :],
                        start=(i == 0), stop=(i == NC - 1)),
                        after=[rw_all, bank_free[bo]] if i == 0 else [])
                ev = P.op("v", lambda v, c=c, s0=s0, bo=bo: v.tensor_add(
                    XA[:, c, s0:s0 + 512], PS[bo][:, :], XA[:, c, s0:s0 + 512]),
                    after=[mmO, vt])
                vt = ev
                bank_free[bo] = ev



def stage_ln(ctx, which, toks):
    P, sb, es = ctx["P"], ctx["sb"], ctx["es"]
    XA, IDF, PS, STAT = ctx["XA"], ctx["IDF"], ctx["PS"], ctx["STAT"]
    out = ctx["out"]
    final = which == 4

    with ExitStack() as st_:
        TM = [sb(f"ln{which}_tm{i}", [128, 1024], F32, st_) for i in range(2)]
        SQJ = sb(f"ln{which}_sqj", [128, 1024], F32, st_)
        COLS = sb(f"ln{which}_cols", [128, 8], F32, st_)

        P.barrier()
        vt = None
        stt = None
        ttok = None
        tm_free = [None, None]
        fwd_bank_free = [None, None]
        bwd_bank_free = [None, None]
        dma_last = None
        for tt in range(NTT):
            P.pump(1)
            sl = tt % 2
            tmb = TM[sl]
            tps = []
            for c in range(NC):
                bank = 5 + c // 4
                cc = c % 4
                a = [tm_free[sl], fwd_bank_free[c // 4]] if c in (0, 4) else []
                tps.append(P.op("t", lambda t, c=c, tt=tt, bank=bank, cc=cc: t.transpose(
                    PS[bank][:, cc * 128:(cc + 1) * 128],
                    XA[:, c, tt * 128:(tt + 1) * 128], IDF[:]), after=a))
            ttok = tps[-1]
            e1 = P.op("s", lambda s, tmb=tmb: s.activation(
                tmb[:, 0:512], PS[5][:, :], AF.Identity), after=[tps[3], stt, tm_free[sl]])
            e2 = P.op("s", lambda s, tmb=tmb: s.activation(
                tmb[:, 512:1024], PS[6][:, :], AF.Identity), after=[tps[7], e1])
            stt = e2
            fwd_bank_free[0] = e1
            fwd_bank_free[1] = e2
            v1 = P.op("v", lambda v, tmb=tmb: v.reduce_sum(
                COLS[:, 0:1], tmb[:], axis=AXX), after=[e2, vt])
            v2 = P.op("s", lambda s, tmb=tmb: s.activation(
                SQJ[:], tmb[:], AF.Square, accum_out=COLS[:, 1:2]),
                after=[v1, stt])
            stt = v2
            v3 = P.op("v", lambda v: v.tensor_scalar(
                COLS[:, 2:3], COLS[:, 0:1], 1.0 / 1024, None, ALU.mult), after=[v2])
            v4 = P.op("v", lambda v: v.tensor_mul(
                COLS[:, 3:4], COLS[:, 2:3], COLS[:, 2:3]), after=[v3])
            v5 = P.op("v", lambda v: v.scalar_tensor_tensor(
                COLS[:, 4:5], COLS[:, 1:2], 1.0 / 1024, COLS[:, 3:4],
                ALU.mult, ALU.subtract), after=[v4])
            v6 = P.op("v", lambda v: v.tensor_scalar(
                COLS[:, 4:5], COLS[:, 4:5], 1e-5, None, ALU.add), after=[v5])
            v7 = P.op("v", lambda v: v.reciprocal(COLS[:, 5:6], COLS[:, 4:5]),
                      after=[v6])
            s1 = P.op("s", lambda s: s.activation(
                COLS[:, 6:7], COLS[:, 5:6], AF.Sqrt), after=[v7, stt])
            stt = s1
            v8 = P.op("v", lambda v, tmb=tmb: v.tensor_scalar(
                tmb[:], tmb[:], COLS[:, 2:3], COLS[:, 6:7],
                ALU.subtract, ALU.mult), after=[s1, vt])
            vt = v8
            if which == 2:
                stt = P.op("s", lambda s, tmb=tmb, tt=tt: s.activation(
                    SQJ[:], tmb[:], AF.Square,
                    accum_out=STAT[:, 16 + tt:17 + tt]), after=[v8, stt])
                vt = v8
            if final:
                dma_last = P.dma("y", out[tt * 128:(tt + 1) * 128, :], tmb[:],
                                 after=[vt], new=True)
                tm_free[sl] = dma_last
            else:
                bts = []
                for c in range(NC):
                    bank = 3 + c // 4
                    cc = c % 4
                    a = [vt, bwd_bank_free[c // 4]] if c in (0, 4) else []
                    bts.append(P.op("t", lambda t, c=c, bank=bank, cc=cc, tmb=tmb: t.transpose(
                        PS[bank][:, cc * 128:(cc + 1) * 128],
                        tmb[:, c * 128:(c + 1) * 128], IDF[:]), after=a))
                ttok = bts[-1]
                ev = []
                for half in range(2):
                    bank = 3 + half
                    ev.append(P.op("s", lambda s, tt=tt, bank=bank, half=half: s.activation(
                        _xa_cols(XA, tt, half), PS[bank][:, :], AF.Identity),
                        after=[bts[3 + 4 * half], stt]))
                stt = ev[-1]
                bwd_bank_free[0] = ev[0]
                bwd_bank_free[1] = ev[1]
                tm_free[sl] = ev[-1]
        return dict(v=P.tok("v"), t=P.tok("t"), s=P.tok("s"),
                    out_dma=dma_last)


def _xa_cols(XA, tt, half):
    return XA[:, half * 4:(half + 1) * 4, tt * 128:(tt + 1) * 128]


def stage_l0ffn(ctx, toks, debug_stop=None):
    P, sb = ctx["P"], ctx["sb"]
    XA, SVEC, PS, din = ctx["XA"], ctx["SVEC"], ctx["PS"], ctx["din"]

    with ExitStack() as st_:
        H0 = sb("f_h0", [128, 32768], BF16, st_)
        H1X = sb("f_h1", [128, 32768], BF16, st_)
        WJ = [sb(f"f_wj{i}", [128, 16, 128], BF16, st_) for i in range(2)]
        G1 = sb("f_g1", [128, 1024], BF16, st_)
        SC = sb("f_sc", [128, 128], F32, st_)

        P.barrier()
        d_sc = P.dma("y", SC[:, 0:32], din["s1"], new=True)
        P.dma("y", SC[:, 32:64], din["s2"])
        P.dma("y", SC[:, 64:96], din["sh1"])
        d_sc2 = P.dma("y", SC[:, 96:128], din["sh2"])
        vt = None
        stt = None
        wj_free = [None, None]
        bank_free = [None] * 8
        s3 = SVEC[:, 40:48]
        wi = 0

        def wload(nm, j, half, after=()):
            nonlocal wi
            slot = wi % 2
            wb = WJ[slot]
            srcap = din[nm][j]
            if half is None:
                dst, s_ = wb[:, 0:8, :], srcap
            else:
                dst, s_ = wb[:, 0:16, :], srcap[:, half * 16:half * 16 + 16, :]
            tok = P.dma("g", dst, s_, after=[wj_free[slot]] + list(after), new=True)
            wi += 1
            return slot, tok

        for hb in range(2):
            t0 = hb * 1024
            XB = H1X[:, 0:8192].rearrange("p (c n) -> p c n", c=8)
            cast = None
            for c in range(NC):
                cast = P.op("v", lambda v, c=c, t0=t0: v.tensor_copy(
                    XB[:, c, :], XA[:, c, t0:t0 + 1024]),
                    after=[vt, toks.get("v"), toks.get("s")] if c == 0 else [])
            vt = cast
            xb_tok = cast

            for j in range(32):
                P.pump(1)
                slot1, dtok1 = wload("w1R", j, None)
                slot2, dtok2 = wload("w2R", j, None)
                wb1, wb2 = WJ[slot1], WJ[slot2]
                mmlast = None
                for wsel in range(2):
                    wb = (wb1, wb2)[wsel]
                    dtok = (dtok1, dtok2)[wsel]
                    for i in range(NC):
                        for s2 in range(2):
                            bank = wsel * 4 + (j % 2) * 2 + s2
                            mmlast = P.op("t", lambda t, wb=wb, i=i, bank=bank, s2=s2:
                                          t.matmul(
                                PS[bank][:, :], wb[:, i, :],
                                XB[:, i, s2 * 512:(s2 + 1) * 512],
                                start=(i == 0), stop=(i == NC - 1)),
                                after=[dtok, xb_tok, bank_free[bank]]
                                if i == 0 and s2 == 0 else
                                ([bank_free[bank]] if i == 0 else []))
                for s2 in range(2):
                    scj = j
                    bank = (j % 2) * 2 + s2
                    evs = P.op("s", lambda s, scj=scj, bank=bank, s2=s2: s.activation(
                        G1[:, s2 * 512:(s2 + 1) * 512], PS[bank][:, :], AF.Silu,
                        scale=SC[:, scj:scj + 1]), after=[mmlast, stt, vt, d_sc])
                    stt = evs
                    bank_free[bank] = evs
                for s2 in range(2):
                    scj = 32 + j
                    bank = 4 + (j % 2) * 2 + s2
                    off = s2 * 512
                    evv = P.op("v", lambda v, scj=scj, j=j, off=off, bank=bank:
                               v.scalar_tensor_tensor(
                        H0[:, j * 1024 + off:j * 1024 + off + 512],
                        PS[bank][:, :], SC[:, scj:scj + 1],
                        G1[:, off:off + 512], ALU.mult, ALU.mult),
                        after=[mmlast, stt, vt])
                    vt = evv
                    bank_free[bank] = evv
                wj_free[slot1] = mmlast
                wj_free[slot2] = mmlast
            h0_done_v = vt

            for phase, nm, scoff, SRC, DST in (
                    ("B", "h1R", 64, H0, H1X), ("C", "h2R", 96, H1X, H0)):
                for j in range(32):
                    P.pump(1)
                    stok = [None, None]
                    wbs = [None, None]
                    for half in range(2):
                        wbs[half], stok[half] = wload(nm, j, half)
                    mm = None
                    for ihalf in range(2):
                        wb = WJ[wbs[ihalf]]
                        for i2 in range(16):
                            i = ihalf * 16 + i2
                            for s2 in range(2):
                                off = s2 * 512
                                bank = 4 + (j % 2) * 2 + s2
                                mm = P.op("t", lambda t, wb=wb, i2=i2, i=i, bank=bank,
                                          off=off, SRC=SRC: t.matmul(
                                    PS[bank][:, :], wb[:, i2, :],
                                    SRC[:, i * 1024 + off:i * 1024 + off + 512],
                                    start=(i == 0), stop=(i == 31)),
                                    after=[stok[0], stok[1], bank_free[bank], h0_done_v]
                                    if i == 0 and s2 == 0 else
                                    ([bank_free[bank]] if i == 0 else []))
                    for s2 in range(2):
                        off = s2 * 512
                        bank = 4 + (j % 2) * 2 + s2
                        ev = P.op("s", lambda s, j=j, off=off, bank=bank, scoff=scoff,
                                  DST=DST: s.activation(
                            DST[:, j * 1024 + off:j * 1024 + off + 512],
                            PS[bank][:, :], AF.Gelu, scale=SC[:, scoff + j:scoff + j + 1]),
                            after=[mm, stt, d_sc2])
                        stt = ev
                        bank_free[bank] = ev
                    wj_free[wbs[0]] = mm
                    wj_free[wbs[1]] = mm
                h0_done_v = stt
            hc_done = stt

            for j in range(NC):
                P.pump(1)
                stok = [None, None]
                wbs = [None, None]
                for half in range(2):
                    wbs[half], stok[half] = wload("w3R", j, half)
                mm = None
                for ihalf in range(2):
                    wb = WJ[wbs[ihalf]]
                    for i2 in range(16):
                        i = ihalf * 16 + i2
                        for s2 in range(2):
                            off = s2 * 512
                            bank = (j % 2) * 2 + s2
                            mm = P.op("t", lambda t, wb=wb, i2=i2, i=i, bank=bank,
                                      off=off: t.matmul(
                                PS[bank][:, :], wb[:, i2, :],
                                H0[:, i * 1024 + off:i * 1024 + off + 512],
                                start=(i == 0), stop=(i == 31)),
                                after=[stok[0], stok[1], bank_free[bank], hc_done]
                                if i == 0 and s2 == 0 else
                                ([bank_free[bank]] if i == 0 else []))
                for s2 in range(2):
                    off = s2 * 512
                    bank = (j % 2) * 2 + s2
                    ev = P.op("v", lambda v, j=j, off=off, t0=t0, bank=bank:
                              v.scalar_tensor_tensor(
                        XA[:, j, t0 + off:t0 + off + 512], PS[bank][:, :],
                        s3[:, j:j + 1], XA[:, j, t0 + off:t0 + off + 512],
                        ALU.mult, ALU.add), after=[mm, vt])
                    vt = ev
                    bank_free[bank] = ev
                wj_free[wbs[0]] = mm
                wj_free[wbs[1]] = mm
        return dict(v=P.tok("v"), t=P.tok("t"), s=P.tok("s"))



def stage_l1attn(ctx, toks):
    P, sb = ctx["P"], ctx["sb"]
    XA, IDF, TRIU, PS, STAT, SCL = (ctx["XA"], ctx["IDF"], ctx["TRIU"],
                                    ctx["PS"], ctx["STAT"], ctx["SCL"])
    din, qscr, SVEC, IDB = ctx["din"], ctx["qscr"], ctx["SVEC"], ctx["IDB"]
    PSB = ctx["PSB"]

    with ExitStack() as st_:
        S = sb("a_S", [128, 8, 512], F32, st_)
        X2 = sb("a_x2", [128, 2, 1024], BF16, st_)
        QSz = sb("a_qz", [128, 1024], BF16, st_)
        QSi = sb("a_qi", [128, 1024], mybir.dt.int32, st_)
        QSt = sb("a_qt", [128, 1024], BF16, st_)
        QSd = sb("a_qd", [128, 1024], BF16, st_)
        Q2z = sb("a_q2z", [128, 1024], BF16, st_)
        Q2i = sb("a_q2i", [128, 1024], mybir.dt.int32, st_)
        Q2t = sb("a_q2t", [128, 1024], BF16, st_)
        Q2d = sb("a_q2d", [128, 1024], BF16, st_)
        CL2 = sb("a_cl2", [128, 16], F32, st_)
        XQF = sb("a_xqf", [128, 8, 256], BF16, st_)
        Q = sb("a_q", [128, 8, 256], F32, st_)
        K = sb("a_k", [128, 8, 256], BF16, st_)
        GK = sb("a_gk", [128, 8, 256], BF16, st_)
        WF = sb("a_wf", [128, 8, 256], F32, st_)
        M0 = sb("a_m0", [128, 8, 256], BF16, st_)
        V = sb("a_v", [128, 2, 2048], F32, st_)
        SG = sb("a_sg", [128, 2, 2048], BF16, st_)
        GT = sb("a_gt", [128, 8, 129], F32, st_)
        DC = sb("a_dc", [128, 8], F32, st_)
        QG = sb("a_qg", [128, 2, 128], F32, st_)
        ET = sb("a_et", [128, 2, 128], F32, st_)
        KP = sb("a_kp", [128, 2, 128], F32, st_)
        KG = sb("a_kg", [128, 2, 128], F32, st_)
        KGt = sb("a_kgt", [128, 2, 128], F32, st_)
        A2 = sb("a_a2", [128, 128], F32, st_)
        OG = sb("a_og", [128, 2048], BF16, st_)
        OQF = sb("a_oqf", [128, 16, 128], BF16, st_)
        NB = sb("a_nb", [128, 1024], BF16, st_)
        NOB = sb("a_nob", [128, 2048], BF16, st_)
        NGK2 = sb("a_ngk2", [128, 16], BF16, st_)
        G1T = sb("a_g1t", [128, 2, 16], F32, st_)
        X2Q = sb("a_x2q", [16, 256], BF16, st_)
        WGK1 = sb("a_wgk1", [128, 8, 16], BF16, st_)
        WGK2 = sb("a_wgk2", [16, 1024], BF16, st_)
        WL = sb("a_wl", [128, 8192], BF16, st_)
        CL = sb("a_cl", [128, 16], F32, st_)

        P.barrier()
        d_ngk2 = P.dma("g", NGK2[:], din["ngk2B"], new=True)
        d_nob = P.dma("g", NOB[:], din["noB"], new=True)
        d_wgk1 = P.dma("y", WGK1[:, :, :],
                       qscr["gk1wT"].rearrange("(c p) n -> p c n", p=128), new=True)
        d_wgk2 = P.dma("y", WGK2[:, :], qscr["gk2wT"], new=True)

        vz = P.op("v", lambda v: v.memset(S[:], 0.0))
        vz2 = P.op("v", lambda v: v.memset(STAT[:, 62:63], 0.0), after=[vz])
        vt = vz2
        gt = None
        nb_free2 = [None, None]
        zb_free = [None, None]
        stt = None
        bank_free = [None] * 8
        wl_free = None
        x2_free = None
        sgate = SVEC[:, 48:56]

        NPROJ = [("q", "qwT", "nqB", 1024), ("k", "kwT", "nkB", 1024),
                 ("v", "vwT", "nvB", 2048), ("g", "gwT", "ngB", 2048),
                 ("gk1", "gk1wT", "ngk1B", 16)]

        for blk_i in range(8):
            t0 = blk_i * 256
            for tt in range(2):
                tps = []
                for c in range(NC):
                    bank = 5 + c // 4
                    cc = c % 4
                    a = [x2_free, bank_free[bank]] if (c % 4 == 0) else []
                    tps.append(P.op("t", lambda t, tt=tt, c=c, t0=t0, bank=bank, cc=cc:
                                    t.transpose(PS[bank][:, cc * 128:(cc + 1) * 128],
                                    XA[:, c, t0 + tt * 128:t0 + (tt + 1) * 128], IDF[:]),
                                    after=a))
                for half in range(2):
                    bank = 5 + half
                    ev = P.op("v", lambda v, tt=tt, half=half, bank=bank: v.tensor_copy(
                        X2[:, tt, half * 512:(half + 1) * 512], PS[bank][:, :]),
                        after=[tps[half * 4 + 3], vt])
                    vt = ev
                    bank_free[bank] = ev
            x2tok = vt
            c1 = P.op("v", lambda v, blk_i=blk_i: v.tensor_scalar(
                CL[:, 0:2], STAT[:, 16 + 2 * blk_i:18 + 2 * blk_i],
                1.0 / 1024, 1e-8, ALU.mult, ALU.add), after=[vt])
            c2 = P.op("v", lambda v: v.reciprocal(CL[:, 2:4], CL[:, 0:2]), after=[c1])
            c3 = P.op("s", lambda s: s.activation(CL[:, 0:2], CL[:, 2:4], AF.Sqrt),
                      after=[c2, stt])
            stt = c3

            for pi, (pnm, wnm, nnm, dout) in enumerate(NPROJ):
                d_nb = P.dma("g", NB[:, 0:1024], din[nnm], new=True,
                             after=[nb_free2[0], nb_free2[1]])
                qtoks = []
                half_toks = [None, None]
                for tt in range(2):
                    en = "v"
                    Zb, Ib, Tb, Db = (QSz, QSi, QSt, QSd) if tt == 0 else (Q2z, Q2i, Q2t, Q2d)
                    CLx = CL if tt == 0 else CL2
                    p1 = P.op(en, lambda v, tt=tt, Zb=Zb: v.tensor_mul(
                        Zb[:], X2[:, tt, :], NB[:, 0:1024]),
                        after=[d_nb, x2tok, vt, zb_free[tt]])
                    nb_free2[tt] = p1
                    p2 = P.op("v", lambda v, Zb=Zb, CLx=CLx: v.reduce_max(
                        CLx[:, 4:5], Zb[:], axis=AXX, apply_absolute_value=True),
                        after=[p1])
                    p3 = P.op("v", lambda v, tt=tt, CLx=CLx: v.tensor_scalar(
                        CLx[:, 5:6], CLx[:, 4:5], CL[:, tt:tt + 1], 1e-5,
                        ALU.mult, ALU.max), after=[p2, c3])
                    p4 = P.op("v", lambda v, CLx=CLx: v.reciprocal(
                        CLx[:, 6:7], CLx[:, 5:6]), after=[p3])
                    p5 = P.op("v", lambda v, tt=tt, CLx=CLx: v.scalar_tensor_tensor(
                        CLx[:, 7:8], CLx[:, 6:7], 127.0, CL[:, tt:tt + 1],
                        ALU.mult, ALU.mult), after=[p4])
                    p6 = P.op("v", lambda v, CLx=CLx: v.tensor_scalar(
                        CLx[:, 8:9], CLx[:, 5:6], 1.0 / 127, None, ALU.mult),
                        after=[p5])
                    vt = p6
                    q1 = P.op("s", lambda s, Zb=Zb, CLx=CLx: s.activation(
                        Zb[:], Zb[:], AF.Identity, scale=CLx[:, 7:8]),
                        after=[p6, stt])
                    o1 = P.op("s", lambda s, Zb=Zb, Ib=Ib: s.activation(
                        Ib[:], Zb[:], AF.Identity), after=[q1])
                    o2 = P.op("s", lambda s, Ib=Ib, Tb=Tb: s.activation(
                        Tb[:], Ib[:], AF.Identity), after=[o1])
                    stt = o2
                    half_toks[tt] = o2
                for tt in range(2):
                    Zb, Ib, Tb, Db = (QSz, QSi, QSt, QSd) if tt == 0 else (Q2z, Q2i, Q2t, Q2d)
                    CLx = CL if tt == 0 else CL2
                    o3 = P.op("v", lambda v, Zb=Zb, Tb=Tb, Db=Db: v.tensor_sub(
                        Db[:], Zb[:], Tb[:]), after=[half_toks[tt], vt])
                    o4 = P.op("v", lambda v, Tb=Tb, Db=Db: v.scalar_tensor_tensor(
                        Tb[:], Db[:], 0.5, Tb[:], ALU.is_ge, ALU.add), after=[o3])
                    o5 = P.op("v", lambda v, Tb=Tb, Db=Db: v.scalar_tensor_tensor(
                        Tb[:], Db[:], -0.5, Tb[:], ALU.is_gt, ALU.add), after=[o4])
                    q7 = P.op("v", lambda v, Zb=Zb, Tb=Tb, CLx=CLx: v.tensor_scalar(
                        Zb[:], Tb[:], 1.0, CLx[:, 8:9], ALU.subtract, ALU.mult),
                        after=[o5])
                    vt = q7
                    xtp = []
                    for c in range(NC):
                        bank = 5 + c // 4
                        cc = c % 4
                        a = [q7, bank_free[bank]] if (c % 4 == 0) else []
                        xtp.append(P.op("t", lambda t, c=c, bank=bank, cc=cc, Zb=Zb:
                                        t.transpose(
                            PSB[bank][:, cc * 128:(cc + 1) * 128],
                            Zb[:, c * 128:(c + 1) * 128], IDB[:]), after=a))
                    zb_free[tt] = xtp[-1]
                    for half in range(2):
                        bank = 5 + half
                        ev = P.op("v", lambda v, tt=tt, half=half, bank=bank: v.tensor_copy(
                            XQF[:, half * 4:(half + 1) * 4, tt * 128:(tt + 1) * 128],
                            PSB[bank][:, 0:512].rearrange("p (c n) -> p c n", c=4)),
                            after=[xtp[half * 4 + 3], vt])
                        vt = ev
                        bank_free[bank] = ev
                    qtoks.append(vt)
                xq_tok = vt

                if pnm == "gk1":
                    mms = []
                    for tt in range(2):
                        bank = 2
                        mm = None
                        for i in range(NC):
                            mm = P.op("t", lambda t, i=i, tt=tt, bank=bank: t.matmul(
                                PS[bank][:, 0:16], XQF[:, i, tt * 128:(tt + 1) * 128],
                                WGK1[:, i, :], start=(i == 0), stop=(i == NC - 1)),
                                after=[d_wgk1, xq_tok, bank_free[bank]] if i == 0 else [])
                        ev = P.op("v", lambda v, tt=tt, bank=bank: v.tensor_scalar(
                            G1T[:, tt, :], PS[bank][:, 0:16],
                            SCL[:, QWI["gk1wT"]:QWI["gk1wT"] + 1], None, ALU.mult),
                            after=[mm, vt])
                        vt = ev
                        bank_free[bank] = ev
                    continue
                wq = qscr[wnm]
                scl_c = SCL[:, QWI[wnm]:QWI[wnm] + 1]
                for half in range(dout // 1024):
                    WLv = WL[:, 0:8192].rearrange("p (c n) -> p c n", c=8)
                    d_w = P.dma("y", WLv[:, :, :], wq[:, half, :, :],
                                after=[wl_free], new=True)
                    if pnm in ("q", "k"):
                        DSTB = Q if pnm == "q" else K
                        for j in range(NC):
                            bank = j % 2
                            mm = None
                            for i in range(NC):
                                mm = P.op("t", lambda t, i=i, j=j, bank=bank: t.matmul(
                                    PS[bank][:, 0:256], WLv[:, i, j * 128:(j + 1) * 128],
                                    XQF[:, i, :],
                                    start=(i == 0), stop=(i == NC - 1)),
                                    after=[d_w, xq_tok, bank_free[bank]] if i == 0 else [])
                            ev = P.op("v", lambda v, j=j, bank=bank, DSTB=DSTB,
                                      scl_c=scl_c: v.tensor_scalar(
                                DSTB[:, j, :], PS[bank][:, 0:256], scl_c, None,
                                ALU.mult), after=[mm, vt])
                            vt = ev
                            bank_free[bank] = ev
                            wl_free = mm
                    else:
                        for tt in range(2):
                            for ds in range(2):
                                bank = ds % 2
                                mm = None
                                for i in range(NC):
                                    mm = P.op("t", lambda t, i=i, tt=tt, ds=ds, bank=bank:
                                              t.matmul(
                                        PS[bank][:, :],
                                        XQF[:, i, tt * 128:(tt + 1) * 128],
                                        WLv[:, i, ds * 512:(ds + 1) * 512],
                                        start=(i == 0), stop=(i == NC - 1)),
                                        after=[d_w, xq_tok, bank_free[bank]] if i == 0 else [])
                                off = half * 1024 + ds * 512
                                if pnm == "v":
                                    ev = P.op("v", lambda v, tt=tt, off=off, bank=bank,
                                              scl_c=scl_c: v.tensor_scalar(
                                        V[:, tt, off:off + 512], PS[bank][:, :],
                                        scl_c, None, ALU.mult), after=[mm, vt])
                                    vt = ev
                                else:
                                    ev = P.op("s", lambda s, tt=tt, off=off, bank=bank,
                                              scl_c=scl_c: s.activation(
                                        SG[:, tt, off:off + 512], PS[bank][:, :],
                                        AF.Silu, scale=scl_c), after=[mm, stt])
                                    stt = ev
                                bank_free[bank] = ev
                                wl_free = mm

            for tt in range(2):
                h1 = P.op("s", lambda s, tt=tt: s.activation(
                    WF[:, 0, 0:16], G1T[:, tt, :], AF.Square,
                    accum_out=CL[:, 9:10]), after=[vt, stt])
                stt = h1
                h2 = P.op("v", lambda v: v.tensor_scalar(
                    CL[:, 10:11], CL[:, 9:10], 1.0 / 16, 1e-8, ALU.mult, ALU.add),
                    after=[h1])
                h3 = P.op("v", lambda v: v.reciprocal(CL[:, 11:12], CL[:, 10:11]),
                          after=[h2])
                h4 = P.op("s", lambda s: s.activation(CL[:, 10:11], CL[:, 11:12],
                                                      AF.Sqrt), after=[h3, stt])
                stt = h4
                r1 = P.op("v", lambda v, tt=tt: v.tensor_mul(
                    QSz[:, 0:16], G1T[:, tt, :], NGK2[:]),
                    after=[h4, d_ngk2, vt, zb_free[0]])
                r2 = P.op("v", lambda v: v.reduce_max(
                    CL[:, 4:5], QSz[:, 0:16], axis=AXX, apply_absolute_value=True),
                    after=[r1])
                r3 = P.op("v", lambda v: v.tensor_scalar(
                    CL[:, 5:6], CL[:, 4:5], CL[:, 10:11], 1e-5, ALU.mult, ALU.max),
                    after=[r2])
                r4 = P.op("v", lambda v: v.reciprocal(CL[:, 6:7], CL[:, 5:6]),
                          after=[r3])
                r5 = P.op("v", lambda v: v.scalar_tensor_tensor(
                    CL[:, 7:8], CL[:, 6:7], 127.0, CL[:, 10:11], ALU.mult, ALU.mult),
                    after=[r4])
                r6 = P.op("v", lambda v: v.tensor_scalar(
                    CL[:, 8:9], CL[:, 5:6], 1.0 / 127, None, ALU.mult), after=[r5])
                u1 = P.op("v", lambda v: v.tensor_single_scalar(
                    QSz[:, 0:16], QSz[:, 0:16], CL[:, 7:8], ALU.mult), after=[r6])
                u2 = P.op("v", lambda v: v.tensor_copy(QSi[:, 0:16], QSz[:, 0:16]),
                          after=[u1])
                u3 = P.op("v", lambda v: v.tensor_copy(QSt[:, 0:16], QSi[:, 0:16]),
                          after=[u2])
                u4 = P.op("v", lambda v: v.tensor_sub(
                    QSd[:, 0:16], QSz[:, 0:16], QSt[:, 0:16]), after=[u3])
                u5 = P.op("v", lambda v: v.scalar_tensor_tensor(
                    QSt[:, 0:16], QSd[:, 0:16], 0.5, QSt[:, 0:16], ALU.is_ge,
                    ALU.add), after=[u4])
                u6 = P.op("v", lambda v: v.scalar_tensor_tensor(
                    QSt[:, 0:16], QSd[:, 0:16], -0.5, QSt[:, 0:16], ALU.is_gt,
                    ALU.add), after=[u5])
                u7 = P.op("v", lambda v: v.tensor_scalar(
                    QSz[:, 0:16], QSt[:, 0:16], 1.0, CL[:, 8:9], ALU.subtract,
                    ALU.mult), after=[u6])
                vt = u7
                tp = P.op("t", lambda t, tt=tt: t.transpose(
                    PSB[2][0:16, 0:128], QSz[:, 0:16], IDB[:]), after=[u7, bank_free[2]])
                zb_free[0] = tp
                ev = P.op("v", lambda v, tt=tt: v.tensor_copy(
                    X2Q[:, tt * 128:(tt + 1) * 128], PSB[2][0:16, 0:128]),
                    after=[tp, vt])
                vt = ev
                bank_free[2] = ev
            for j in range(NC):
                bank = 2 + (j % 2)
                mm = P.op("t", lambda t, j=j, bank=bank: t.matmul(
                    PS[bank][:, 0:256], WGK2[:, j * 128:(j + 1) * 128],
                    X2Q[:], start=True, stop=True),
                    after=[d_wgk2, vt, bank_free[bank]])
                ev = P.op("s", lambda s, j=j, bank=bank: s.activation(
                    WF[:, j, :], PS[bank][:, 0:256], AF.Identity,
                    scale=SCL[:, QWI["gk2wT"]:QWI["gk2wT"] + 1],
                    bias=sgate[:, j:j + 1]), after=[mm, stt])
                stt = ev
                bank_free[bank] = ev
            l1 = P.op("v", lambda v: v.tensor_scalar(
                M0[:], WF[:], 0.0, 1.0 / 16, ALU.min, ALU.mult), after=[stt, vt])
            l2 = P.op("s", lambda s: s.activation(WF[:], WF[:], AF.Abs),
                      after=[l1, stt])
            l3 = P.op("s", lambda s: s.activation(WF[:], WF[:], AF.Exp, scale=-1.0),
                      after=[l2])
            l4 = P.op("s", lambda s: s.activation(WF[:], WF[:], AF.Ln, bias=1.0),
                      after=[l3])
            l5 = P.op("v", lambda v: v.scalar_tensor_tensor(
                GK[:], WF[:], -1.0 / 16, M0[:], ALU.mult, ALU.add), after=[l4])
            vt = l5

            for cn in range(2):
                z0 = P.op("v", lambda v: v.memset(GT[:, :, 0:1], 0.0), after=[vt])
                sc = []
                for c2 in range(NC):
                    sc.append(P.op("v", lambda v, c2=c2, cn=cn: v.tensor_tensor_scan(
                        GT[:, c2, 1:129], GK[:, c2, cn * 128:(cn + 1) * 128],
                        STAT[:, 62:63].broadcast_to([128, 128]), 0.0,
                        ALU.add, ALU.add), after=[z0 if c2 == 0 else sc[-1], l5]))
                vt = sc[-1]
                dc1 = P.op("s", lambda s: s.activation(
                    DC[:, :], GT[:, :, 128:129].rearrange("p c n -> p (c n)"),
                    AF.Exp), after=[vt, stt])
                stt = dc1
                for h in range(4):
                    f1 = P.op("s", lambda s, h=h: s.activation(
                        QG[:, :, :], GT[:, 2 * h:2 * h + 2, 1:129], AF.Exp),
                        after=[vt, stt, bank_free[3]])
                    f2 = P.op("s", lambda s, h=h: s.activation(
                        ET[:, :, :], GT[:, 2 * h:2 * h + 2, 1:129], AF.Exp,
                        scale=-1.0), after=[f1])
                    stt = f2
                    f3 = P.op("v", lambda v, h=h, cn=cn: v.scalar_tensor_tensor(
                        QG[:, :, :], QG[:, :, :], 1.0 / 16,
                        Q[:, 2 * h:2 * h + 2, cn * 128:(cn + 1) * 128],
                        ALU.mult, ALU.mult), after=[f1, vt])
                    f4 = P.op("v", lambda v, h=h, cn=cn: v.tensor_mul(
                        KP[:, :, :], ET[:, :, :],
                        K[:, 2 * h:2 * h + 2, cn * 128:(cn + 1) * 128]),
                        after=[f2, f3])
                    f5 = []
                    for j in range(2):
                        f5.append(P.op("v", lambda v, h=h, j=j: v.tensor_scalar(
                            ET[:, j, :], GT[:, 2 * h + j, 1:129],
                            GT[:, 2 * h + j, 128:129], None, ALU.subtract),
                            after=[f4]))
                    f6 = P.op("s", lambda s: s.activation(
                        ET[:, :, :], ET[:, :, :], AF.Exp, scale=-1.0),
                        after=[f5[0], f5[1], stt])
                    stt = f6
                    f7 = P.op("v", lambda v, h=h, cn=cn: v.tensor_mul(
                        KG[:, :, :], ET[:, :, :],
                        K[:, 2 * h:2 * h + 2, cn * 128:(cn + 1) * 128]),
                        after=[f6])
                    vt = f7
                    amm = None
                    for j in range(2):
                        amm = P.op("t", lambda t, j=j: t.matmul(
                            PS[2][:, 0:128], KP[:, j, :], QG[:, j, :],
                            start=(j == 0), stop=(j == 1)),
                            after=[f7, bank_free[2]] if j == 0 else [])
                    aev = P.op("v", lambda v: v.tensor_mul(
                        A2[:], PS[2][:, 0:128], TRIU[:]), after=[amm, vt])
                    vt = aev
                    bank_free[2] = aev
                    ktp = []
                    for j in range(2):
                        ktp.append(P.op("t", lambda t, j=j: t.transpose(
                            PS[4][:, j * 128:(j + 1) * 128], KG[:, j, :], IDF[:]),
                            after=[f7, bank_free[4]] if j == 0 else []))
                    kev = P.op("v", lambda v: v.tensor_copy(
                        KGt[:, :, :], PS[4][:, 0:256].rearrange("p (c n) -> p c n", c=2)),
                        after=[ktp[1], vt])
                    vt = kev
                    bank_free[4] = kev
                    omm = None
                    for j in range(2):
                        omm = P.op("t", lambda t, h=h, j=j: t.matmul(
                            PS[3][:, :], QG[:, j, :], S[:, 2 * h + j, :],
                            start=(j == 0), stop=False),
                            after=[f3, bank_free[3], vz] if j == 0 else [])
                    omm = P.op("t", lambda t, h=h, cn=cn: t.matmul(
                        PS[3][:, :], A2[:], V[:, cn, h * 512:(h + 1) * 512],
                        start=False, stop=True), after=[aev])
                    dmm = []
                    for j in range(2):
                        dmm.append(P.op("t", lambda t, h=h, cn=cn, j=j: t.matmul(
                            PS[6 + j][:, :], KGt[:, j, :],
                            V[:, cn, h * 512:(h + 1) * 512],
                            start=True, stop=True),
                            after=[kev, bank_free[6 + j]]))
                    for j in range(2):
                        su = P.op("v", lambda v, h=h, j=j: v.scalar_tensor_tensor(
                            S[:, 2 * h + j, :], S[:, 2 * h + j, :],
                            DC[:, 2 * h + j:2 * h + j + 1], PS[6 + j][:, :],
                            ALU.mult, ALU.add), after=[dmm[j], omm, dc1, vt])
                        vt = su
                        bank_free[6 + j] = su
                    oev = P.op("v", lambda v, h=h, cn=cn: v.tensor_mul(
                        OG[:, h * 512:(h + 1) * 512], PS[3][:, :],
                        SG[:, cn, h * 512:(h + 1) * 512]), after=[omm, vt])
                    vt = oev
                    bank_free[3] = oev
                og_tok = vt

                WFf = WF[:, :, :].rearrange("p c n -> p (c n)")
                m1 = P.op("s", lambda s, WFf=WFf: s.activation(
                    WFf[:, 0:1024], OG[:, 0:1024], AF.Square,
                    accum_out=CL[:, 9:10]), after=[og_tok, stt])
                m2 = P.op("s", lambda s, WFf=WFf: s.activation(
                    WFf[:, 1024:2048], OG[:, 1024:2048], AF.Square,
                    accum_out=CL[:, 10:11]), after=[m1])
                stt = m2
                m3 = P.op("v", lambda v: v.tensor_add(
                    CL[:, 9:10], CL[:, 9:10], CL[:, 10:11]), after=[m2, vt])
                m4 = P.op("v", lambda v: v.tensor_scalar(
                    CL[:, 10:11], CL[:, 9:10], 1.0 / 2048, 1e-8, ALU.mult, ALU.add),
                    after=[m3])
                m5 = P.op("v", lambda v: v.reciprocal(CL[:, 11:12], CL[:, 10:11]),
                          after=[m4])
                m6 = P.op("s", lambda s: s.activation(CL[:, 10:11], CL[:, 11:12],
                                                      AF.Sqrt), after=[m5, stt])
                stt = m6
                hv = []
                for half in range(2):
                    w1 = P.op("v", lambda v, half=half: v.tensor_mul(
                        QSz[:], OG[:, half * 1024:(half + 1) * 1024],
                        NOB[:, half * 1024:(half + 1) * 1024]), after=[m6, vt, d_nob])
                    w2 = P.op("v", lambda v, half=half: v.reduce_max(
                        CL[:, 4 + half:5 + half], QSz[:], axis=AXX,
                        apply_absolute_value=True), after=[w1])
                    hv.append(w2)
                    vt = w2
                w3 = P.op("v", lambda v: v.tensor_max(
                    CL[:, 4:5], CL[:, 4:5], CL[:, 5:6]), after=[hv[0], hv[1]])
                w4 = P.op("v", lambda v: v.tensor_scalar(
                    CL[:, 5:6], CL[:, 4:5], CL[:, 10:11], 1e-5, ALU.mult, ALU.max),
                    after=[w3])
                w5 = P.op("v", lambda v: v.reciprocal(CL[:, 6:7], CL[:, 5:6]),
                          after=[w4])
                w6 = P.op("v", lambda v: v.scalar_tensor_tensor(
                    CL[:, 7:8], CL[:, 6:7], 127.0, CL[:, 10:11], ALU.mult, ALU.mult),
                    after=[w5])
                w7 = P.op("v", lambda v: v.tensor_scalar(
                    CL[:, 8:9], CL[:, 5:6], 1.0 / 127, None, ALU.mult), after=[w6])
                vt = w7
                ytoks = [None, None]
                for half in range(2):
                    Zb, Ib, Tb, Db = (QSz, QSi, QSt, QSd) if half == 0 else (Q2z, Q2i, Q2t, Q2d)
                    y1 = P.op("v", lambda v, half=half, Zb=Zb: v.tensor_mul(
                        Zb[:], OG[:, half * 1024:(half + 1) * 1024],
                        NOB[:, half * 1024:(half + 1) * 1024]),
                        after=[vt, w7, zb_free[half]])
                    vt = y1
                    y2 = P.op("s", lambda s, Zb=Zb: s.activation(
                        Zb[:], Zb[:], AF.Identity, scale=CL[:, 7:8]),
                        after=[y1, stt])
                    o1 = P.op("s", lambda s, Zb=Zb, Ib=Ib: s.activation(
                        Ib[:], Zb[:], AF.Identity), after=[y2])
                    o2 = P.op("s", lambda s, Ib=Ib, Tb=Tb: s.activation(
                        Tb[:], Ib[:], AF.Identity), after=[o1])
                    stt = o2
                    ytoks[half] = o2
                for half in range(2):
                    Zb, Ib, Tb, Db = (QSz, QSi, QSt, QSd) if half == 0 else (Q2z, Q2i, Q2t, Q2d)
                    o3 = P.op("v", lambda v, Zb=Zb, Tb=Tb, Db=Db: v.tensor_sub(
                        Db[:], Zb[:], Tb[:]), after=[ytoks[half], vt])
                    o4 = P.op("v", lambda v, Tb=Tb, Db=Db: v.scalar_tensor_tensor(
                        Tb[:], Db[:], 0.5, Tb[:], ALU.is_ge, ALU.add), after=[o3])
                    o5 = P.op("v", lambda v, Tb=Tb, Db=Db: v.scalar_tensor_tensor(
                        Tb[:], Db[:], -0.5, Tb[:], ALU.is_gt, ALU.add), after=[o4])
                    y8 = P.op("v", lambda v, Zb=Zb, Tb=Tb: v.tensor_scalar(
                        Zb[:], Tb[:], 1.0, CL[:, 8:9], ALU.subtract, ALU.mult),
                        after=[o5])
                    vt = y8
                    otp = []
                    Zb2 = QSz if half == 0 else Q2z
                    for c in range(NC):
                        bank = 5 + c // 4
                        cc = c % 4
                        a = [y8, bank_free[bank]] if (c % 4 == 0) else []
                        otp.append(P.op("t", lambda t, c=c, bank=bank, cc=cc, Zb2=Zb2:
                                        t.transpose(
                            PSB[bank][:, cc * 128:(cc + 1) * 128],
                            Zb2[:, c * 128:(c + 1) * 128], IDB[:]), after=a))
                    zb_free[half] = otp[-1]
                    for hh in range(2):
                        bank = 5 + hh
                        ev = P.op("v", lambda v, half=half, hh=hh, bank=bank: v.tensor_copy(
                            OQF[:, half * 8 + hh * 4:half * 8 + (hh + 1) * 4, :],
                            PSB[bank][:, 0:512].rearrange("p (c n) -> p c n", c=4)),
                            after=[otp[hh * 4 + 3], vt])
                        vt = ev
                        bank_free[bank] = ev
                oq_tok = vt

                WLv = WL[:, 0:8192].rearrange("p (c n) -> p c n", c=8)
                mm = None
                for rnd in range(2):
                    d_wo = P.dma("y", WLv[:, :, :],
                                 qscr["owT"][:, 0, rnd * 8:(rnd + 1) * 8, :],
                                 after=[wl_free, mm], new=True)
                    for jj in range(NC):
                        bank = jj // 4
                        cc = jj % 4
                        for i in range(NC):
                            mm = P.op("t", lambda t, i=i, jj=jj, rnd=rnd, bank=bank,
                                      cc=cc: t.matmul(
                                PS[bank][:, cc * 128:(cc + 1) * 128],
                                WLv[:, i, jj * 128:(jj + 1) * 128],
                                OQF[:, rnd * 8 + i, :],
                                start=(rnd == 0 and i == 0 and jj % 4 == 0),
                                stop=(rnd == 1 and i == 7 and jj % 4 == 3)),
                                after=[d_wo, oq_tok, bank_free[bank]]
                                if (jj % 4 == 0 and i == 0) else [])
                    wl_free = mm
                for jj in range(NC):
                    bank = jj // 4
                    cc = jj % 4
                    ev = P.op("v", lambda v, jj=jj, cn=cn, t0=t0, bank=bank, cc=cc:
                              v.scalar_tensor_tensor(
                        XA[:, jj, t0 + cn * 128:t0 + (cn + 1) * 128],
                        PS[bank][:, cc * 128:(cc + 1) * 128],
                        SCL[:, QWI["owT"]:QWI["owT"] + 1],
                        XA[:, jj, t0 + cn * 128:t0 + (cn + 1) * 128],
                        ALU.mult, ALU.add), after=[mm, vt])
                    vt = ev
                bank_free[0] = vt
                bank_free[1] = vt
            x2_free = vt
        return dict(v=P.tok("v"), t=P.tok("t"), s=P.tok("s"))



def _round_fix(P, en, Zb, Ib, Tb, Db, M, prev, sprev=None):
    o1 = P.op("s", lambda s, Zb=Zb, Ib=Ib: s.activation(Ib[:], Zb[:], AF.Identity),
              after=[prev, sprev])
    o2 = P.op("s", lambda s, Ib=Ib, Tb=Tb: s.activation(Tb[:], Ib[:], AF.Identity),
              after=[o1])
    o3 = P.op("v", lambda v, Zb=Zb, Tb=Tb, Db=Db: v.tensor_sub(Db[:], Zb[:], Tb[:]),
              after=[o2, prev])
    o4 = P.op("v", lambda v, Tb=Tb, Db=Db: v.scalar_tensor_tensor(
        Tb[:], Db[:], 0.5, Tb[:], ALU.is_ge, ALU.add), after=[o3])
    o5 = P.op("v", lambda v, Tb=Tb, Db=Db: v.scalar_tensor_tensor(
        Tb[:], Db[:], -0.5, Tb[:], ALU.is_gt, ALU.add), after=[o4])
    return o5, o2


def _act_quant_fm(P, flat, UB, WB, QSV, QSG, vt, gt, ready=None, stt=None):
    piece = 1024
    NPC = 16384 // piece
    s1tok = [None] * NPC
    sets = [QSV, QSG]
    emitted2 = 0

    def stage1(pc, vt, stt):
        o0 = pc * piece
        QI, QT, QD, QM = sets[pc % 2]
        q1 = P.op("v", lambda v, o0=o0: v.tensor_mul(
            flat[:, o0:o0 + piece], flat[:, o0:o0 + piece], UB[:, 0:piece]),
            after=[vt, ready if pc < 2 else None])
        c1 = P.op("s", lambda s, o0=o0, QI=QI: s.activation(
            QI[:, 0:piece], flat[:, o0:o0 + piece], AF.Identity), after=[q1, stt])
        c2 = P.op("s", lambda s, QI=QI, QT=QT: s.activation(
            QT[:, 0:piece], QI[:, 0:piece], AF.Identity), after=[c1])
        s1tok[pc] = c2
        return q1, c2

    def stage2(pc, vt):
        o0 = pc * piece
        QI, QT, QD, QM = sets[pc % 2]
        o3 = P.op("v", lambda v, o0=o0, QT=QT, QD=QD: v.tensor_sub(
            QD[:, 0:piece], flat[:, o0:o0 + piece], QT[:, 0:piece]),
            after=[s1tok[pc], vt])
        o4 = P.op("v", lambda v, QT=QT, QD=QD: v.scalar_tensor_tensor(
            QT[:, 0:piece], QD[:, 0:piece], 0.5, QT[:, 0:piece], ALU.is_ge,
            ALU.add), after=[o3])
        o5 = P.op("v", lambda v, QT=QT, QD=QD: v.scalar_tensor_tensor(
            QT[:, 0:piece], QD[:, 0:piece], -0.5, QT[:, 0:piece], ALU.is_gt,
            ALU.add), after=[o4])
        q7 = P.op("v", lambda v, QT=QT: v.tensor_scalar(
            QT[:, 0:piece], QT[:, 0:piece], 1.0, None, ALU.subtract), after=[o5])
        q8 = P.op("v", lambda v, o0=o0, QT=QT: v.tensor_mul(
            flat[:, o0:o0 + piece], QT[:, 0:piece], WB[:, 0:piece]), after=[q7])
        return q8

    vtc, sttc = vt, stt
    for pc in range(NPC):
        _, sttc = stage1(pc, vtc, sttc)
        if pc >= 1:
            vtc = stage2(pc - 1, vtc)
    vtc = stage2(NPC - 1, vtc)
    return vtc, gt, sttc


def stage_l1ffn(ctx, toks):
    P, sb = ctx["P"], ctx["sb"]
    XA, IDF, IDB, PS, STAT, SCL = (ctx["XA"], ctx["IDF"], ctx["IDB"],
                                   ctx["PS"], ctx["STAT"], ctx["SCL"])
    din, qscr, PSB = ctx["din"], ctx["qscr"], ctx["PSB"]

    with ExitStack() as st_:
        XQ3 = sb("b_xq3", [128, 8, 512], BF16, st_)
        H0 = sb("b_h0", [128, 32, 514], BF16, st_)
        HC = sb("b_hc", [128, 32, 512], BF16, st_)
        TL = sb("b_tl", [128, 32, 2], BF16, st_)
        FWJ = [sb(f"b_wj{i}", [128, 16, 128], BF16, st_) for i in range(3)]
        QZ = sb("b_qz", [128, 1024], BF16, st_)
        QI = sb("b_qi", [128, 1024], mybir.dt.int32, st_)
        QT = sb("b_qt", [128, 1024], BF16, st_)
        QD = sb("b_qd", [128, 1024], BF16, st_)
        Q2z = sb("b_q2z", [128, 1024], BF16, st_)
        Q2i = sb("b_q2i", [128, 1024], mybir.dt.int32, st_)
        Q2t = sb("b_q2t", [128, 1024], BF16, st_)
        Q2d = sb("b_q2d", [128, 1024], BF16, st_)
        CL2 = sb("b_cl2", [128, 16], F32, st_)
        MSKF = [sb(f"b_msk{i}", [128, 1024], BF16, st_) for i in range(2)]
        UB = sb("b_ub", [128, 1024], BF16, st_)
        WB = sb("b_wb", [128, 1024], BF16, st_)
        CWC = sb("b_cw", [128, 32, 3], F32, st_)
        CBC = sb("b_cb", [128, 32], F32, st_)
        CL = sb("b_cl", [128, 16], F32, st_)
        MX = sb("b_mx", [128, 8], F32, st_)
        UR = sb("b_ur", [1, 512], F32, st_)
        WR = sb("b_wr", [1, 512], F32, st_)
        ONE1 = sb("b_one", [1, 128], F32, st_)

        P.barrier()
        d_cw = P.dma("y", CWC[:], din["cw"], new=True)
        d_cb = P.dma("y", CBC[:], din["cb"])
        o1 = P.op("v", lambda v: v.memset(ONE1[:], 1.0))
        vt = o1
        gt = None
        stt = None
        zb_free = [None, None]
        wj_free = [None, None, None]
        bank_free = [None] * 8
        wi = 0
        sc1 = SCL[:, QWI["f1T"]:QWI["f1T"] + 1]
        sc2 = SCL[:, QWI["f2T"]:QWI["f2T"] + 1]

        def fwload(nm, j, half, after=()):
            nonlocal wi
            slot = wi % 3
            if half is None:
                dst, s_ = FWJ[slot][:, 0:8, :], qscr[nm][j]
            else:
                dst = FWJ[slot][:, 0:16, :]
                s_ = qscr[nm][j][:, half * 16:half * 16 + 16, :]
            tok = P.dma("y", dst, s_, after=[wj_free[slot]] + list(after), new=True)
            wi += 1
            return slot, tok

        for q in range(4):
            t0 = q * 512
            if q > 0:
                tl1 = P.op("v", lambda v: v.tensor_copy(TL[:], H0[:, :, 512:514]),
                           after=[vt])
                vt = tl1
            else:
                tl1 = None
            for tt in range(4):
                en = "v"
                Zb, Ib, Tb, Db = (QZ, QI, QT, QD) if tt % 2 == 0 else (Q2z, Q2i, Q2t, Q2d)
                CLx = CL if tt % 2 == 0 else CL2
                ch = vt
                tps = []
                for c in range(NC):
                    bank = 5 + c // 4
                    cc = c % 4
                    a = [bank_free[bank]] if (c % 4 == 0) else []
                    tps.append(P.op("t", lambda t, tt=tt, c=c, t0=t0, bank=bank, cc=cc:
                                    t.transpose(PS[bank][:, cc * 128:(cc + 1) * 128],
                                    XA[:, c, t0 + tt * 128:t0 + (tt + 1) * 128],
                                    IDF[:]), after=a + ([toks.get("v"), toks.get("s")]
                                    if q == 0 and tt == 0 and c == 0 else [])))
                e1 = P.op("v", lambda v, Zb=Zb: v.tensor_copy(
                    Zb[:, 0:512], PS[5][:, :]), after=[tps[3], vt, zb_free[tt % 2]])
                e2 = P.op("v", lambda v, Zb=Zb: v.tensor_copy(
                    Zb[:, 512:1024], PS[6][:, :]), after=[tps[7], e1])
                vt = e2
                bank_free[5] = e1
                bank_free[6] = e2
                a1 = P.op("v", lambda v, Zb=Zb, CLx=CLx: v.reduce_max(
                    CLx[:, 0:1], Zb[:], axis=AXX, apply_absolute_value=True),
                    after=[e2])
                a2 = P.op("v", lambda v, CLx=CLx: v.tensor_scalar(
                    CLx[:, 1:2], CLx[:, 0:1], 1e-5, None, ALU.max), after=[a1])
                a3 = P.op("v", lambda v, CLx=CLx: v.reciprocal(CLx[:, 2:3], CLx[:, 1:2]),
                          after=[a2])
                a4 = P.op("v", lambda v, CLx=CLx: v.tensor_scalar(
                    CLx[:, 3:4], CLx[:, 2:3], 127.0, None, ALU.mult), after=[a3])
                a5 = P.op("v", lambda v, CLx=CLx: v.tensor_scalar(
                    CLx[:, 4:5], CLx[:, 1:2], 1.0 / 127, None, ALU.mult), after=[a4])
                vt = a5
                b1 = P.op("s", lambda s, Zb=Zb, CLx=CLx: s.activation(
                    Zb[:], Zb[:], AF.Identity, scale=CLx[:, 3:4]),
                    after=[a5, ch, stt])
                stt = b1
                b6, sb6 = _round_fix(P, en, Zb, Ib, Tb, Db, MSKF[tt % 2], b1, stt)
                stt = sb6
                b7 = P.op(en, lambda v, Zb=Zb, Tb=Tb, CLx=CLx: v.tensor_scalar(
                    Zb[:], Tb[:], 1.0, CLx[:, 4:5], ALU.subtract, ALU.mult), after=[b6])
                vt = b7
                xtp = []
                for c in range(NC):
                    bank = 5 + c // 4
                    cc = c % 4
                    a = [b7, bank_free[bank]] if (c % 4 == 0) else []
                    xtp.append(P.op("t", lambda t, c=c, bank=bank, cc=cc, Zb=Zb:
                                    t.transpose(
                        PSB[bank][:, cc * 128:(cc + 1) * 128],
                        Zb[:, c * 128:(c + 1) * 128], IDB[:]), after=a))
                zb_free[tt % 2] = xtp[-1]
                for half in range(2):
                    bank = 5 + half
                    ev = P.op("v", lambda v, tt=tt, half=half, bank=bank: v.tensor_copy(
                        XQ3[:, half * 4:(half + 1) * 4, tt * 128:(tt + 1) * 128],
                        PSB[bank][:, 0:512].rearrange("p (c n) -> p c n", c=4)),
                        after=[xtp[half * 4 + 3], vt])
                    vt = ev
                    bank_free[bank] = ev
            xq_tok = vt
            for j in range(32):
                P.pump(1)
                slot, dtok = fwload("f1T", j, None)
                bank = j % 2
                mm = None
                for i in range(NC):
                    mm = P.op("t", lambda t, slot=slot, i=i, j=j, bank=bank: t.matmul(
                        PS[bank][:, :], FWJ[slot][:, i, :], XQ3[:, i, :],
                        start=(i == 0), stop=(i == NC - 1)),
                        after=[dtok, xq_tok, bank_free[bank]] if i == 0 else [])
                ev = P.op("s", lambda s, j=j, bank=bank: s.activation(
                    H0[:, j, 2:514], PS[bank][:, :], AF.Identity, scale=sc1),
                    after=[mm, stt, tl1])
                stt = ev
                bank_free[bank] = ev
                wj_free[slot] = mm
            h0_tok = stt
            if q == 0:
                tz = P.op("v", lambda v: v.memset(H0[:, :, 0:2], 0.0), after=[vt, h0_tok])
            else:
                tz = P.op("v", lambda v: v.tensor_copy(H0[:, :, 0:2], TL[:]),
                          after=[vt, h0_tok, tl1])
            vt = tz

            for j in range(32):
                c1 = P.op("v", lambda v, j=j: v.tensor_single_scalar(
                    QZ[:, 0:512], H0[:, j, 0:512], CWC[:, j, 0:1], ALU.mult),
                    after=[tz, vt, stt, d_cw, h0_tok, zb_free[0]])
                c2 = P.op("v", lambda v, j=j: v.scalar_tensor_tensor(
                    QZ[:, 0:512], H0[:, j, 1:513], CWC[:, j, 1:2], QZ[:, 0:512],
                    ALU.mult, ALU.add), after=[c1])
                c3 = P.op("v", lambda v, j=j: v.scalar_tensor_tensor(
                    QZ[:, 0:512], H0[:, j, 2:514], CWC[:, j, 2:3], QZ[:, 0:512],
                    ALU.mult, ALU.add), after=[c2])
                c4 = P.op("s", lambda s, j=j: s.activation(
                    HC[:, j, :], QZ[:, 0:512], AF.Silu, bias=CBC[:, j:j + 1]),
                    after=[c3, stt, d_cb])
                stt = c4
                vt = c3
            hc_tok = stt

            wlo = 1 if q == 0 else 0
            mzero = P.op("v", lambda v: v.memset(MX[:, 0:4], 0.0), after=[vt])
            vt = mzero
            for tt in range(4):
                for grp in range(8):
                    tps = []
                    for cc in range(4):
                        c = grp * 4 + cc
                        bank = 5
                        a = [hc_tok, bank_free[5]] if cc == 0 else []
                        tps.append(P.op("t", lambda t, c=c, tt=tt, cc=cc: t.transpose(
                            PSB[5][:, cc * 128:(cc + 1) * 128], HC[:, c, tt * 128:(tt + 1) * 128],
                            IDB[:]), after=a))
                    r1 = P.op("v", lambda v: v.reduce_max(
                        CL[:, 0:1], PSB[5][:, 0:512], axis=AXX, apply_absolute_value=True),
                        after=[tps[3], vt])
                    r2 = P.op("v", lambda v, tt=tt: v.tensor_max(
                        MX[:, tt:tt + 1], MX[:, tt:tt + 1], CL[:, 0:1]), after=[r1])
                    vt = r2
                    bank_free[5] = r1
            s1_ = P.op("v", lambda v: v.tensor_scalar(
                MX[:, 4:8], MX[:, 0:4], 1e-5, None, ALU.max), after=[vt])
            s2_ = P.op("v", lambda v: v.reciprocal(MX[:, 0:4], MX[:, 4:8]), after=[s1_])
            s3_ = P.op("v", lambda v: v.tensor_scalar(
                MX[:, 0:4], MX[:, 0:4], 127.0, None, ALU.mult), after=[s2_])
            s4_ = P.op("v", lambda v: v.tensor_scalar(
                MX[:, 4:8], MX[:, 4:8], 1.0 / 127, None, ALU.mult), after=[s3_])
            vt = s4_
            for which, COLS, ROW in ((0, MX[:, 0:4], UR), (1, MX[:, 4:8], WR)):
                for tt in range(4):
                    tpu = P.op("t", lambda t, COLS=COLS, tt=tt: t.transpose(
                        PS[6][0:1, 0:128], COLS[:, tt:tt + 1], IDF[:]),
                        after=[vt, s4_, bank_free[6]])
                    evu = P.op("v", lambda v, ROW=ROW, tt=tt: v.tensor_copy(
                        ROW[:, tt * 128:(tt + 1) * 128], PS[6][0:1, 0:128]),
                        after=[tpu, vt])
                    vt = evu
                    bank_free[6] = evu
                bmm = P.op("t", lambda t, ROW=ROW: t.matmul(
                    PS[6][:, :], ONE1[:], ROW[:, 0:512],
                    start=True, stop=True), after=[vt, bank_free[6], o1])
                bev1 = P.op("v", lambda v, which=which: v.tensor_copy(
                    (UB if which == 0 else WB)[:, 0:512], PS[6][:, :]),
                    after=[bmm, vt])
                bev = P.op("v", lambda v, which=which: v.tensor_copy(
                    (UB if which == 0 else WB)[:, 512:1024], PS[6][:, :]),
                    after=[bev1])
                vt = bev
                bank_free[6] = bev
            vt, gt, stt = _act_quant_fm(P, HC[:, :, :].rearrange("p c n -> p (c n)"),
                                        UB, WB, (QI, QT, QD, MSKF[0]),
                                        (Q2i, Q2t, Q2d, MSKF[1]), vt, gt,
                                        ready=vt, stt=stt)
            hq_tok = (vt, gt)

            for j in range(NC):
                P.pump(1)
                stoks = [None, None]
                slots = [None, None]
                for half in range(2):
                    slots[half], stoks[half] = fwload("f2T", j, half)
                bank = 2 + (j % 2)
                mm = None
                for ihalf in range(2):
                    for i2 in range(16):
                        i = ihalf * 16 + i2
                        mm = P.op("t", lambda t, s_=slots[ihalf], i2=i2, i=i, bank=bank:
                                  t.matmul(
                            PS[bank][:, :], FWJ[s_][:, i2, :],
                            HC[:, i, :], start=(i == 0), stop=(i == 31)),
                            after=[stoks[0], stoks[1], hq_tok[0], hq_tok[1],
                                   bank_free[bank]] if i == 0 else [])
                ev = P.op("v", lambda v, j=j, t0=t0, wlo=wlo, bank=bank:
                          v.scalar_tensor_tensor(
                    XA[:, j, t0 - 1 + wlo:t0 + 511], PS[bank][:, wlo:512], sc2,
                    XA[:, j, t0 - 1 + wlo:t0 + 511], ALU.mult, ALU.add),
                    after=[mm, vt])
                vt = ev
                bank_free[bank] = ev
                wj_free[slots[0]] = mm
                wj_free[slots[1]] = mm

        ee1 = P.op("v", lambda v: v.tensor_mul(
            QZ[:, 0:32], H0[:, :, 512:513].rearrange("p c n -> p (c n)"),
            CWC[:, :, 0]), after=[vt])
        ee2 = P.op("v", lambda v: v.tensor_mul(
            QZ[:, 32:64], H0[:, :, 513:514].rearrange("p c n -> p (c n)"),
            CWC[:, :, 1]), after=[ee1])
        ee3 = P.op("v", lambda v: v.tensor_add(QZ[:, 0:32], QZ[:, 0:32], QZ[:, 32:64]),
                   after=[ee2])
        ee3b = P.op("v", lambda v: v.tensor_add(QZ[:, 0:32], QZ[:, 0:32], CBC[:]),
                    after=[ee3])
        ee4 = P.op("s", lambda s: s.activation(
            QZ[:, 0:32], QZ[:, 0:32], AF.Silu), after=[ee3b, stt])
        stt = ee4
        et1 = P.op("t", lambda t: t.transpose(PSB[5][0:32, 0:128], QZ[:, 0:32], IDB[:]),
                   after=[ee4, bank_free[5]])
        et2 = P.op("v", lambda v: v.reduce_max(
            CL[0:32, 0:1], PSB[5][0:32, 0:128], axis=AXX, apply_absolute_value=True),
            after=[et1, vt])
        et3 = P.op("t", lambda t: t.transpose(PS[6][0:1, 0:32], CL[0:32, 0:1],
                                              IDF[0:32, 0:32]), after=[et2, bank_free[6]])
        et4 = P.op("v", lambda v: v.reduce_max(
            CL[0:1, 1:2], PS[6][0:1, 0:32], axis=AXX), after=[et3])
        et5 = P.op("v", lambda v: v.tensor_scalar(
            CL[0:1, 2:3], CL[0:1, 1:2], 1e-5, None, ALU.max), after=[et4])
        et6 = P.op("v", lambda v: v.reciprocal(CL[0:1, 3:4], CL[0:1, 2:3]), after=[et5])
        et7 = P.op("v", lambda v: v.tensor_scalar(
            CL[0:1, 4:5], CL[0:1, 3:4], 127.0, None, ALU.mult), after=[et6])
        et8 = P.op("v", lambda v: v.tensor_scalar(
            CL[0:1, 5:6], CL[0:1, 2:3], 1.0 / 127, None, ALU.mult), after=[et7])
        eb1 = P.op("t", lambda t: t.matmul(PS[6][:, 0:2], ONE1[0:1, :],
                                           CL[0:1, 4:6], start=True, stop=True),
                   after=[et8, bank_free[6]])
        eb2 = P.op("v", lambda v: v.tensor_copy(CL[:, 8:10], PS[6][:, 0:2]),
                   after=[eb1])
        bank_free[6] = eb2
        bank_free[5] = et2
        f1 = P.op("v", lambda v: v.tensor_single_scalar(
            QZ[:, 0:32], QZ[:, 0:32], CL[:, 8:9], ALU.mult), after=[eb2])
        f2 = P.op("v", lambda v: v.tensor_copy(QI[:, 0:32], QZ[:, 0:32]), after=[f1])
        f3 = P.op("v", lambda v: v.tensor_copy(QT[:, 0:32], QI[:, 0:32]), after=[f2])
        f4 = P.op("v", lambda v: v.tensor_sub(QD[:, 0:32], QZ[:, 0:32], QT[:, 0:32]),
                  after=[f3])
        f5 = P.op("v", lambda v: v.scalar_tensor_tensor(
            QT[:, 0:32], QD[:, 0:32], 0.5, QT[:, 0:32], ALU.is_ge, ALU.add), after=[f4])
        f6 = P.op("v", lambda v: v.scalar_tensor_tensor(
            QT[:, 0:32], QD[:, 0:32], -0.5, QT[:, 0:32], ALU.is_gt, ALU.add), after=[f5])
        f7 = P.op("v", lambda v: v.tensor_scalar(
            QZ[:, 0:32], QT[:, 0:32], 1.0, CL[:, 9:10], ALU.subtract, ALU.mult),
            after=[f6])
        vt = f7
        stoks = [None, None]
        slots = [None, None]
        for half in range(2):
            slots[half], stoks[half] = fwload("f2T", 0, half)
        for j in range(NC):
            if j > 0:
                for half in range(2):
                    slots[half], stoks[half] = fwload("f2T", j, half)
            mm = None
            for ihalf in range(2):
                for i2 in range(16):
                    i = ihalf * 16 + i2
                    mm = P.op("t", lambda t, s_=slots[ihalf], i2=i2, i=i, j=j: t.matmul(
                        PS[2][:, 0:1], FWJ[s_][:, i2, :], QZ[:, i:i + 1],
                        start=(i == 0), stop=(i == 31)),
                        after=[stoks[0], stoks[1], f7, bank_free[2]] if i == 0 else [])
            ev = P.op("v", lambda v, j=j: v.scalar_tensor_tensor(
                XA[:, j, 2047:2048], PS[2][:, 0:1], sc2, XA[:, j, 2047:2048],
                ALU.mult, ALU.add), after=[mm, vt])
            vt = ev
            bank_free[2] = ev
            wj_free[slots[0]] = mm
            wj_free[slots[1]] = mm
        return dict(v=P.tok("v"), t=P.tok("t"), s=P.tok("s"))


def stage_inputs(inputs):
    import ml_dtypes
    g = {k: np.asarray(v) for k, v in inputs.items()}
    sh = {}
    sh["idf"] = np.eye(128, dtype=np.float32)
    sh["idb"] = np.eye(128, dtype=np.float32).astype(ml_dtypes.bfloat16)
    m = np.zeros((128, 128), np.float32)
    m[np.triu_indices(128)] = 1.0
    sh["triu"] = m
    sh["ones"] = np.ones((128, 128), np.float32)
    sh["tmk"] = _colmajor(g["L0_attn_time_mix_k"])
    sh["tmv"] = _colmajor(g["L0_attn_time_mix_v"])
    sh["tmr"] = _colmajor(g["L0_attn_time_mix_r"])
    sh["td"] = _colmajor(g["L0_attn_time_decay"])
    sh["tf"] = _colmajor(g["L0_attn_time_first"])
    sh["wkT"] = _tr(g["L0_attn_Wk"]); sh["wvT"] = _tr(g["L0_attn_Wv"])
    sh["wrT"] = _tr(g["L0_attn_Wr"]); sh["woT"] = _tr(g["L0_attn_Wo"])
    sh["w1R"] = _jblk(g["L0_ffn_w1_w"]); sh["w2R"] = _jblk(g["L0_ffn_w2_w"])
    sh["h1R"] = _jblk(g["L0_ffn_h1_w"]); sh["h2R"] = _jblk(g["L0_ffn_h2_w"])
    sh["w3R"] = _jblk(g["L0_ffn_w3_w"])
    sh["s1"] = _colmajor(g["L0_ffn_w1_s"]); sh["s2"] = _colmajor(g["L0_ffn_w2_s"])
    sh["sh1"] = _colmajor(g["L0_ffn_h1_s"]); sh["sh2"] = _colmajor(g["L0_ffn_h2_s"])
    sh["s3"] = _colmajor(g["L0_ffn_w3_s"])
    sh["qwT"] = _tr(g["L1_attn_wq"]); sh["kwT"] = _tr(g["L1_attn_wk"])
    sh["vwT"] = _tr(g["L1_attn_wv"]); sh["gwT"] = _tr(g["L1_attn_wg"])
    sh["gk1wT"] = _tr(g["L1_attn_wgk1"]); sh["gk2wT"] = _tr(g["L1_attn_wgk2"])
    sh["owT"] = _tr(g["L1_attn_wo"])
    sh["f1T"] = _tr(g["L1_ffn_w1"]); sh["f2T"] = _tr(g["L1_ffn_w2"])
    sh["nqB"] = _bcast(g["L1_attn_nq"]); sh["nkB"] = _bcast(g["L1_attn_nk"])
    sh["nvB"] = _bcast(g["L1_attn_nv"]); sh["ngB"] = _bcast(g["L1_attn_ng"])
    sh["ngk1B"] = _bcast(g["L1_attn_ngk1"]); sh["noB"] = _bcast(g["L1_attn_no"])
    sh["ngk2c"] = np.ascontiguousarray(np.asarray(g["L1_attn_ngk2"], np.float32)[:, None])
    sh["ngk2B"] = _bcast(g["L1_attn_ngk2"])
    sh["bgk2c"] = _colmajor(g["L1_attn_bgk2"])
    sh["cw"] = np.ascontiguousarray(np.asarray(g["L1_ffn_conv_w"], np.float32)
                                    .reshape(DH, 3).reshape(32, 128, 3).transpose(1, 0, 2))
    sh["cb"] = _colmajor(g["L1_ffn_conv_b"])
    sh["cb32"] = sh["cb"]

    x = np.asarray(g["L0_attn_x"], np.float32)
    in_maps = []
    for b in range(B):
        im = dict(sh)
        im["xT"] = np.ascontiguousarray(x[b].T)
        in_maps.append(im)
    return in_maps


def kernel(**inputs):
    in_maps = stage_inputs(inputs)
    nc = build(upto="all", tap=None)
    res = run_bass_kernel_spmd(nc, in_maps, core_ids=list(range(B)))
    return np.stack([np.asarray(r["out"], np.float32) for r in res.results], axis=0)


# version 12
# speedup vs baseline: 1.0764x; 1.0056x over previous
import numpy as np
from contextlib import ExitStack

import concourse.bass as bass
import concourse.mybir as mybir
import concourse.bass_isa as bass_isa
from concourse.bass_utils import run_bass_kernel_spmd

F32 = mybir.dt.float32
BF16 = mybir.dt.bfloat16
AF = mybir.ActivationFunctionType
ALU = mybir.AluOpType
AXX = mybir.AxisListType.X
RADD = bass_isa.ReduceOp.add

B, T, C = 8, 2048, 1024
NC = 8
DH = 4096
NTT = T // 128

ENGS = ("t", "v", "s", "g", "y")
DMACH = {"g": "dG", "y": "dY", "s": "dS"}
NROT = 6


class Prog:

    def __init__(self, nc, sems):
        self.nc = nc
        self.sems = sems
        self.ops = {e: [] for e in ENGS}
        self.ncomp = {e: 0 for e in ENGS}
        self.need = {e: set() for e in ENGS}
        self.dmacnt = {}
        for q in DMACH.values():
            for i in range(NROT):
                self.dmacnt[q + str(i)] = 0
        self.dmarot = {q: 0 for q in DMACH.values()}
        self.dmagroup_open = {q: False for q in DMACH.values()}
        self.group_ends = {ch: [] for ch in self.dmacnt}
        self.pending = {e: [] for e in ENGS}
        self.bg = []

    def tok(self, e):
        return (e, self.ncomp[e])

    def _waits(self, e, after):
        w = []
        for t in list(after) + self.pending[e]:
            if t is None:
                continue
            kind, val = t
            if val <= 0:
                continue
            w.append((kind, val))
            if kind in ENGS:
                self.need[kind].add(val)
        self.pending[e] = []
        return w

    def op(self, e, fn, after=(), note=""):
        self.ops[e].append(dict(fn=fn, waits=self._waits(e, after), dma=None,
                                comp=self.ncomp[e], note=note))
        self.ncomp[e] += 1
        return (e, self.ncomp[e])

    def dma(self, e, out, in_, after=(), note="", new=False):
        q = DMACH[e]
        if new or not self.dmagroup_open[q]:
            cur = q + str(self.dmarot[q])
            if self.dmacnt[cur] > 0 and (not self.group_ends[cur]
                                         or self.group_ends[cur][-1] != self.dmacnt[cur]):
                self.group_ends[cur].append(self.dmacnt[cur])
            self.dmarot[q] = (self.dmarot[q] + 1) % NROT
            ch = q + str(self.dmarot[q])
            if self.dmacnt[ch] > 0:
                self.pending[e].append((ch, self.dmacnt[ch]))
            self.dmagroup_open[q] = True
        ch = q + str(self.dmarot[q])
        def fn(eng, out=out, in_=in_):
            return eng.dma_start(out=out, in_=in_)
        self.ops[e].append(dict(fn=fn, waits=self._waits(e, after), dma=ch,
                                comp=None, note=note))
        self.dmacnt[ch] += 16
        return (ch, self.dmacnt[ch])

    def dma_new(self, e):
        self.dmagroup_open[DMACH[e]] = False

    def barrier(self, engines=("t", "v", "s", "g", "y")):
        toks = {e: self.tok(e) for e in engines}
        for e in engines:
            for o in engines:
                if o != e:
                    self.pending[e].append(toks[o])

    def pump(self, n=1):
        for _ in range(n):
            if self.bg:
                self.bg.pop(0)()

    def pump_all(self):
        while self.bg:
            self.bg.pop(0)()

    def finalize(self, blk):
        import bisect
        for ch, tot in self.dmacnt.items():
            if tot > 0 and (not self.group_ends[ch] or self.group_ends[ch][-1] != tot):
                self.group_ends[ch].append(tot)
        for e in ENGS:
            for rec in self.ops[e]:
                w2 = []
                for kind, val in rec["waits"]:
                    if kind in self.group_ends:
                        ge = self.group_ends[kind]
                        val = ge[bisect.bisect_left(ge, val)]
                    w2.append((kind, val))
                rec["waits"] = w2
        incs = {e: {} for e in ENGS}
        for e in ENGS:
            need = sorted(self.need[e])
            if need and need[-1] > self.ncomp[e]:
                raise RuntimeError(f"wait beyond issued ops on {e}")
            full, prev = [], 0
            for p in need:
                while p - prev > 32:
                    prev += 32
                    full.append(prev)
                if p != prev:
                    full.append(p)
                    prev = p
            comp2idx = {}
            for i, rec in enumerate(self.ops[e]):
                if rec["comp"] is not None:
                    comp2idx[rec["comp"]] = i
            prev = 0
            for p in full:
                incs[e][comp2idx[p - 1]] = p - prev
                prev = p

        prog = self

        def make_body(e):
            def body(eng):
                seen = {}
                for i, rec in enumerate(prog.ops[e]):
                    for kind, val in rec["waits"]:
                        if seen.get(kind, 0) >= val:
                            continue
                        eng.wait_ge(prog.sems[kind], val)
                        seen[kind] = val
                    inst = rec["fn"](eng)
                    if rec["dma"] is not None:
                        inst.then_inc(prog.sems[rec["dma"]], 16)
                    amt = incs[e].get(i)
                    if amt:
                        inst.then_inc(prog.sems[e], amt)
            return body

        blk.tensor(make_body("t"))
        blk.vector(make_body("v"))
        blk.scalar(make_body("s"))
        blk.gpsimd(make_body("g"))
        blk.sync(make_body("y"))



def _colmajor(v):
    v = np.asarray(v, np.float32)
    n = v.size // 128
    return np.ascontiguousarray(v.reshape(n, 128).T)


def _bcast(v):
    v = np.asarray(v, np.float32)
    return np.ascontiguousarray(np.broadcast_to(v[None, :], (128, v.size)))


def _tr(w):
    return np.ascontiguousarray(np.asarray(w, np.float32).T)


def _jblk(w):
    wT = np.asarray(w, np.float32).T
    di, do = wT.shape
    return np.ascontiguousarray(
        wT.reshape(di // 128, 128, do // 128, 128).transpose(2, 1, 0, 3))


QW = [
    ("qwT", C, C), ("kwT", C, C), ("vwT", C, 2048), ("gwT", C, 2048),
    ("gk1wT", C, 16), ("gk2wT", 16, C), ("owT", 2048, C),
    ("f1T", C, DH), ("f2T", DH, C),
]
QWI = {nm: i for i, (nm, _, _) in enumerate(QW)}


def build(upto="all", tap=None):
    nc = bass.Bass("TRN2", target_bir_lowering=False)

    din = {}
    def dram_in(name, shape, dt=F32):
        din[name] = nc.dram_tensor(name, list(shape), dt, kind="ExternalInput").ap()
        return din[name]

    xT = dram_in("xT", [C, T])
    dram_in("idf", [128, 128]); dram_in("idb", [128, 128], BF16)
    dram_in("triu", [128, 128])
    dram_in("ones", [128, 128])
    for nm in ("tmk", "tmv", "tmr", "td", "tf", "s3", "bgk2c", "cb32"):
        dram_in(nm, [128, NC] if nm not in ("cb32",) else [128, 32])
    dram_in("cb", [128, 32])
    for nm in ("wkT", "wvT", "wrT", "woT"):
        dram_in(nm, [C, C])
    dram_in("w1R", [32, 128, 8, 128]); dram_in("w2R", [32, 128, 8, 128])
    dram_in("h1R", [32, 128, 32, 128]); dram_in("h2R", [32, 128, 32, 128])
    dram_in("w3R", [8, 128, 32, 128])
    for nm in ("s1", "s2", "sh1", "sh2"):
        dram_in(nm, [128, 32])
    for nm, di, do in QW:
        dram_in(nm, [di, do])
    for nm in ("nqB", "nkB", "nvB", "ngB", "ngk1B"):
        dram_in(nm, [128, C])
    dram_in("noB", [128, 2048])
    dram_in("ngk2c", [16, 1])
    dram_in("ngk2B", [128, 16])
    dram_in("cw", [128, 32, 3])

    out = nc.dram_tensor("out", [T, C], F32, kind="ExternalOutput").ap()
    tap_t = None
    if tap is not None:
        tap_t = nc.dram_tensor("tap", list(tap[0]), tap[1],
                               kind="ExternalOutput").ap()

    QSHAPE = {
        "qwT": [128, 1, 8, 1024], "kwT": [128, 1, 8, 1024],
        "vwT": [128, 2, 8, 1024], "gwT": [128, 2, 8, 1024],
        "owT": [128, 1, 16, 1024],
        "f1T": [32, 128, 8, 128], "f2T": [8, 128, 32, 128],
        "gk1wT": [C, 16], "gk2wT": [16, C],
    }
    qscr = {}
    for nm, di, do in QW:
        qscr[nm] = nc.dram_tensor("q_" + nm, QSHAPE[nm], BF16, kind="Internal").ap()

    with ExitStack() as es:
        sems = {}
        semnames = list(ENGS) + [q + str(i) for q in DMACH.values() for i in range(NROT)]
        for e in semnames:
            sems[e] = es.enter_context(nc.semaphore("sem_" + e))
        P = Prog(nc, sems)

        def sb(name, shape, dt=F32, stack=None):
            return (stack or es).enter_context(nc.sbuf_tensor(name, list(shape), dt))

        XA = sb("XA", [128, NC, T])
        IDF = sb("c_idf", [128, 128])
        IDB = sb("c_idb", [128, 128], BF16)
        TRIU = sb("c_triu", [128, 128])
        ONES = sb("c_ones", [128, 128])
        SVEC = sb("SVEC", [128, 64])
        SCL = sb("SCL", [128, 16])
        STAT = sb("STAT", [128, 64])
        PS = [es.enter_context(nc.psum_tensor(f"ps{b}", [128, 512], F32))
              for b in range(8)]
        PSB = [PS[b][:, :].bitcast(BF16) for b in range(8)]

        P.dma("y", IDF[:], din["idf"])
        d_idb = P.dma("y", IDB[:], din["idb"])
        d_triu = P.dma("y", TRIU[:], din["triu"])
        d_ones = P.dma("y", ONES[:], din["ones"])
        P.dma("y", SVEC[:, 0:8], din["tmk"])
        P.dma("y", SVEC[:, 8:16], din["tmv"])
        d_tm = P.dma("y", SVEC[:, 16:24], din["tmr"])
        d_td = P.dma("y", SVEC[:, 24:32], din["td"])
        d_tf = P.dma("y", SVEC[:, 32:40], din["tf"])
        P.dma("y", SVEC[:, 40:48], din["s3"])
        d_sv = P.dma("y", SVEC[:, 48:56], din["bgk2c"])
        d_x = None
        for c in range(NC):
            d_x = P.dma("y", XA[:, c, :], xT[c * 128:(c + 1) * 128, :],
                        new=(c == 0))
        s_ew1 = P.op("s", lambda s: s.activation(SVEC[:, 24:32], SVEC[:, 24:32], AF.Exp),
                     after=[d_td])
        s_ew = P.op("s", lambda s: s.activation(SVEC[:, 24:32], SVEC[:, 24:32], AF.Exp,
                                                scale=-1.0), after=[s_ew1])
        s_eu = P.op("s", lambda s: s.activation(SVEC[:, 32:40], SVEC[:, 32:40], AF.Exp),
                    after=[d_tf])

        pre_stack = ExitStack()
        pre = dict(
            ins=[sb(f"pre_in{i}", [128, 1024], F32, pre_stack) for i in range(3)],
            outs=[sb(f"pre_out{i}", [128, 1024], BF16, pre_stack) for i in range(2)],
            acc=sb("pre_acc", [128, 1], F32, pre_stack),
            reds=[sb(f"pre_red{i}", [128, 1], F32, pre_stack) for i in range(2)],
            rs=sb("pre_rs", [128, 1], F32, pre_stack),
            cmp=sb("pre_cmp", [128, 1024], F32, pre_stack))
        _emit_prepass(P, pre, din, qscr, SCL, ONES, PS, d_ones)

        ctx = dict(P=P, nc=nc, sb=sb, es=es, XA=XA, IDF=IDF, IDB=IDB, TRIU=TRIU,
                   SVEC=SVEC, SCL=SCL, STAT=STAT, PS=PS, PSB=PSB, din=din, qscr=qscr,
                   xready=d_x, svec_ready=(s_ew, s_eu, d_sv, d_tm))

        ctx["out"] = out
        order = ["l0attn", "ln1", "l0ffn", "ln2", "l1attn", "ln3", "l1ffn", "ln4"]
        stop = upto if upto in order or upto in ("l0ffnA", "l0ffnB") else "ln4"
        toks = stage_l0attn(ctx)
        done = upto == "l0attn"
        if not done:
            toks = stage_ln(ctx, 1, toks)
            P.pump_all()
            pre_stack.close()
            done = upto == "ln1"
        else:
            P.pump_all()
            pre_stack.close()
        if not done and upto in ("l0ffnA", "l0ffnB"):
            toksA = stage_l0ffn(ctx, toks, debug_stop=upto[-1])
            P.barrier()
            tk = [P.tok(e) for e in ("t", "v", "s", "g")]
            H0d = toksA["H0"]
            last = None
            for jj in range(32):
                last = P.dma("y", tap_t[jj * 128:(jj + 1) * 128, :],
                             H0d[:, jj * 1024:(jj + 1) * 1024], after=tk,
                             new=(jj == 0))
            P.op("y", lambda y: y.nop(), after=[last])
            done = True
        if not done:
            toks = stage_l0ffn(ctx, toks)
            done = upto == "l0ffn"
        if not done:
            toks = stage_ln(ctx, 2, toks)
            done = upto == "ln2"
        if not done:
            toks = stage_l1attn(ctx, toks)
            done = upto == "l1attn"
        if not done:
            toks = stage_ln(ctx, 3, toks)
            done = upto == "ln3"
        if not done:
            toks = stage_l1ffn(ctx, toks)
            done = upto == "l1ffn"
        if not done:
            toks = stage_ln(ctx, 4, toks)
            P.op("y", lambda y: y.nop(), after=[toks["out_dma"]])
            done = True
        elif tap_t is not None and upto not in ("l0ffnA", "l0ffnB"):
            _tap_xa(P, tap_t, XA)

        P.pump_all()
        blk = es.enter_context(nc.Block())
        P.finalize(blk)
    return nc


def _tap_xa(P, tap_t, XA):
    P.barrier()
    tk = [P.tok(e) for e in ("t", "v", "s", "g")]
    last = None
    for c in range(NC):
        last = P.dma("y", tap_t[c * 128:(c + 1) * 128, :], XA[:, c, :], after=tk,
                     new=(c == 0))
    P.op("y", lambda y: y.nop(), after=[last])


def _emit_prepass(P, pre, din, qscr, SCL, ONES, PS, d_ones):
    state = {"g": None, "v": None, "s": None, "tread": None,
             "ld": [None, None, None], "qd": [None, None, None],
             "redf": [None, None]}
    NR = 3
    for nm, di, do in QW:
        wi = QWI[nm]
        src, dst = din[nm], qscr[nm]
        tiles = []
        for r0 in range(0, di, 128):
            rr = min(128, di - r0)
            for c0 in range(0, do, 1024):
                tiles.append((r0, rr, c0, min(1024, do - c0)))
        inv_n = 1.0 / float(di * do)

        def t_zero():
            state["g"] = P.op("g", lambda g: g.memset(pre["acc"][:], 0.0),
                              after=[state["g"]])
        P.bg.append(t_zero)

        for ti, tile in enumerate(tiles):
            def t_red(tile=tile, ti=ti, src=src):
                r0, rr, c0, cc = tile
                buf = pre["ins"][ti % NR]
                red = pre["reds"][ti % 2]
                ld = P.dma("y", buf[0:rr, 0:cc], src[r0:r0 + rr, c0:c0 + cc],
                           after=[state["ld"][ti % NR]], new=True)
                s1 = P.op("s", lambda s: s.activation(
                    buf[0:rr, 0:cc], buf[0:rr, 0:cc], AF.Abs,
                    accum_out=red[0:rr, :]), after=[ld, state["s"],
                                                    state["redf"][ti % 2]])
                state["s"] = s1
                state["ld"][ti % NR] = s1
                state["g"] = P.op("g", lambda g: g.tensor_add(
                    pre["acc"][0:rr, :], pre["acc"][0:rr, :], red[0:rr, :]),
                    after=[s1, state["g"]])
                state["redf"][ti % 2] = state["g"]
            P.bg.append(t_red)

        def t_scale(wi=wi, inv_n=inv_n):
            mm = P.op("t", lambda t: t.matmul(
                PS[7][:, 0:1], ONES[:], pre["acc"][:, :], start=True, stop=True),
                after=[state["g"], d_ones, state["tread"]])
            state["tread"] = P.op("v", lambda v: v.tensor_scalar(
                SCL[:, wi:wi + 1], PS[7][:, 0:1], inv_n, 1e-5,
                ALU.mult, ALU.max), after=[mm, state["v"]])
            state["v"] = state["tread"]
            state["g"] = state["tread"]

        P.bg.append(t_scale)

        def t_recip(wi=wi):
            state["v"] = P.op("v", lambda v: v.reciprocal(
                pre["rs"][:], SCL[:, wi:wi + 1]), after=[state["g"], state["v"]])
        P.bg.append(t_recip)

        for ti, tile in enumerate(tiles):
            def t_quant(tile=tile, ti=ti, src=src, dst=dst, nm=nm):
                r0, rr, c0, cc = tile
                buf = pre["ins"][ti % NR]
                obuf = pre["outs"][ti % 2]
                ld = P.dma("y", buf[0:rr, 0:cc], src[r0:r0 + rr, c0:c0 + cc],
                           after=[state["ld"][ti % NR], state["v"]], new=True)
                v1 = P.op("v", lambda v: v.tensor_single_scalar(
                    buf[0:rr, 0:cc], buf[0:rr, 0:cc], pre["rs"][0:rr, :], ALU.mult),
                    after=[ld, state["v"]])
                v2 = P.op("v", lambda v: v.tensor_scalar(
                    pre["cmp"][0:rr, 0:cc], buf[0:rr, 0:cc], -0.5, -1.0,
                    ALU.is_le, ALU.mult), after=[v1])
                v4 = P.op("v", lambda v: v.scalar_tensor_tensor(
                    obuf[0:rr, 0:cc], buf[0:rr, 0:cc], 0.5, pre["cmp"][0:rr, 0:cc],
                    ALU.is_ge, ALU.add), after=[v2, state["qd"][ti % 2]])
                state["v"] = v4
                state["ld"][ti % NR] = v4
                if nm in ("f1T", "f2T"):
                    j0 = c0 // 128
                    nj = cc // 128
                    dview = dst[j0:j0 + nj, :, r0 // 128, :].rearrange("a p n -> p a n")
                    sview = obuf[0:rr, 0:cc].rearrange("p (a n) -> p a n", a=nj)
                    state["qd"][ti % 2] = P.dma("g", dview, sview,
                                                after=[v4], new=True)
                elif nm in ("gk1wT", "gk2wT"):
                    state["qd"][ti % 2] = P.dma("g", dst[r0:r0 + rr, c0:c0 + cc],
                                                obuf[0:rr, 0:cc], after=[v4], new=True)
                else:
                    state["qd"][ti % 2] = P.dma(
                        "g", dst[:, c0 // 1024, r0 // 128, 0:cc],
                        obuf[0:rr, 0:cc], after=[v4], new=True)
            P.bg.append(t_quant)


def stage_l0attn(ctx):
    P, sb, es = ctx["P"], ctx["sb"], ctx["es"]
    XA, SVEC, PS, din = ctx["XA"], ctx["SVEC"], ctx["PS"], ctx["din"]
    s_ew, s_eu, d_sv, d_tm = ctx["svec_ready"]
    xready = ctx["xready"]

    with ExitStack() as st_:
        WK = sb("l0_wk", [128, NC, C], BF16, st_)
        WV = sb("l0_wv", [128, NC, C], BF16, st_)
        WR = sb("l0_wr", [128, NC, C], BF16, st_)
        WO = sb("l0_wo", [128, NC, C], BF16, st_)
        XK = sb("l0_xk", [128, NC, 512], BF16, st_)
        XV = sb("l0_xv", [128, NC, 512], BF16, st_)
        XR = sb("l0_xr", [128, NC, 512], BF16, st_)
        RW = sb("l0_rw", [128, NC, 512], BF16, st_)
        TMPD = sb("l0_tmpd", [128, 512], F32, st_)
        XPREV = sb("l0_xprev", [128, NC, 1], F32, st_)
        NS = 2
        SL_EK = [sb(f"l0_ek{i}", [128, 512], F32, st_) for i in range(NS)]
        SL_EKV = [sb(f"l0_ekv{i}", [128, 512], F32, st_) for i in range(NS)]
        SL_A = [sb(f"l0_A{i}", [128, 513], F32, st_) for i in range(NS)]
        SL_B = [sb(f"l0_B{i}", [128, 513], F32, st_) for i in range(NS)]
        SL_SR = [sb(f"l0_sr{i}", [128, 512], BF16, st_) for i in range(NS)]
        CARA = sb("l0_carA", [128, NC, 1], F32, st_)
        CARB = sb("l0_carB", [128, NC, 1], F32, st_)

        wtok = None
        first = True
        for Wb, nm in ((WK, "wkT"), (WV, "wvT"), (WR, "wrT"), (WO, "woT")):
            for c in range(NC):
                wtok = P.dma("g", Wb[:, c, :], din[nm][c * 128:(c + 1) * 128, :],
                             new=first)
                first = False

        v0 = P.op("v", lambda v: v.memset(CARA[:], 0.0))
        v0 = P.op("v", lambda v: v.memset(CARB[:], 0.0), after=[v0])
        vprev = P.op("v", lambda v: v.memset(XPREV[:], 0.0), after=[v0])

        tmk, tmv, tmr = SVEC[:, 0:8], SVEC[:, 8:16], SVEC[:, 16:24]
        ew, eu = SVEC[:, 24:32], SVEC[:, 32:40]

        slot_free = [None] * NS
        bank_free = [None] * 8
        vt = vprev
        st = None

        for s in range(4):
            s0 = s * 512
            mix_start = [xready, d_tm, vprev, vt]
            for c in range(NC):
                P.pump(1)
                v1 = P.op("v", lambda v, c=c, s0=s0: v.tensor_sub(
                    TMPD[:, 0:1], XA[:, c, s0:s0 + 1], XPREV[:, c, :]),
                    after=mix_start if c == 0 else [vt])
                v2 = P.op("v", lambda v, c=c, s0=s0: v.tensor_sub(
                    TMPD[:, 1:512], XA[:, c, s0 + 1:s0 + 512], XA[:, c, s0:s0 + 511]),
                    after=[v1])
                vt = v2
                for XBUF, tm in ((XK, tmk), (XV, tmv), (XR, tmr)):
                    m1 = P.op("v", lambda v, XBUF=XBUF, tm=tm, c=c: v.scalar_tensor_tensor(
                        XBUF[:, c, 0:1], TMPD[:, 0:1], tm[:, c:c + 1], XPREV[:, c, :],
                        ALU.mult, ALU.add), after=[vt])
                    vt = P.op("v", lambda v, XBUF=XBUF, tm=tm, c=c, s0=s0: v.scalar_tensor_tensor(
                        XBUF[:, c, 1:512], TMPD[:, 1:512], tm[:, c:c + 1],
                        XA[:, c, s0:s0 + 511], ALU.mult, ALU.add), after=[m1])
            mix_all = P.tok("v")

            for c in range(NC):
                P.pump(1)
                sl = c % NS
                ek, ekv, A, Bt, sr = SL_EK[sl], SL_EKV[sl], SL_A[sl], SL_B[sl], SL_SR[sl]
                bk = c % 2
                bv = bk ^ 1

                mmK = None
                for i in range(NC):
                    mmK = P.op("t", lambda t, c=c, i=i, bk=bk: t.matmul(
                        PS[bk][:, :], WK[:, i, c * 128:(c + 1) * 128], XK[:, i, :],
                        start=(i == 0), stop=(i == NC - 1)),
                        after=[wtok, mix_all, bank_free[bk]] if i == 0 else [])
                e1 = P.op("s", lambda s_, ek=ek, bk=bk: s_.activation(
                    ek[:], PS[bk][:, :], AF.Exp), after=[mmK, slot_free[sl], st])
                st = e1
                mmV = None
                for i in range(NC):
                    mmV = P.op("t", lambda t, c=c, i=i, bv=bv: t.matmul(
                        PS[bv][:, :], WV[:, i, c * 128:(c + 1) * 128], XV[:, i, :],
                        start=(i == 0), stop=(i == NC - 1)),
                        after=[bank_free[bv]] if i == 0 else [])
                e2 = P.op("v", lambda v, ek=ek, ekv=ekv, bv=bv: v.tensor_mul(
                    ekv[:], ek[:], PS[bv][:, :]), after=[mmV, e1, vt, slot_free[sl]])
                vt = e2
                bank_free[bk] = e1
                bank_free[bv] = e2
                mmR = None
                for i in range(NC):
                    mmR = P.op("t", lambda t, c=c, i=i: t.matmul(
                        PS[2][:, :], WR[:, i, c * 128:(c + 1) * 128], XR[:, i, :],
                        start=(i == 0), stop=(i == NC - 1)),
                        after=[bank_free[2]] if i == 0 else [])
                e3 = P.op("s", lambda s_, sr=sr: s_.activation(
                    sr[:], PS[2][:, :], AF.Sigmoid), after=[mmR, st, slot_free[sl]])
                st = e3
                bank_free[2] = e3

                c1 = P.op("v", lambda v, A=A, c=c: v.tensor_copy(
                    A[:, 0:1], CARA[:, c, :]), after=[e2, vt])
                c2 = P.op("v", lambda v, Bt=Bt, c=c: v.tensor_copy(
                    Bt[:, 0:1], CARB[:, c, :]), after=[c1])
                sc1 = P.op("v", lambda v, A=A, ekv=ekv, c=c: v.tensor_tensor_scan(
                    A[:, 1:513], ew[:, c:c + 1].broadcast_to([128, 512]), ekv[:],
                    A[:, 0:1], ALU.mult, ALU.add), after=[c2, s_ew])
                sc2 = P.op("v", lambda v, Bt=Bt, ek=ek, c=c: v.tensor_tensor_scan(
                    Bt[:, 1:513], ew[:, c:c + 1].broadcast_to([128, 512]), ek[:],
                    Bt[:, 0:1], ALU.mult, ALU.add), after=[sc1, e1])
                c3 = P.op("v", lambda v, A=A, c=c: v.tensor_copy(
                    CARA[:, c, :], A[:, 512:513]), after=[sc1])
                c4 = P.op("v", lambda v, Bt=Bt, c=c: v.tensor_copy(
                    CARB[:, c, :], Bt[:, 512:513]), after=[sc2])
                n1 = P.op("v", lambda v, ekv=ekv, A=A, c=c: v.scalar_tensor_tensor(
                    ekv[:], ekv[:], eu[:, c:c + 1], A[:, 0:512], ALU.mult, ALU.add),
                    after=[c3, s_eu])
                n2 = P.op("v", lambda v, ek=ek, Bt=Bt, c=c: v.scalar_tensor_tensor(
                    ek[:], ek[:], eu[:, c:c + 1], Bt[:, 0:512], ALU.mult, ALU.add),
                    after=[c4, n1])
                n3 = P.op("v", lambda v, ek=ek: v.reciprocal(
                    ek[:], ek[:]), after=[n2])
                n3b = P.op("v", lambda v, ekv=ekv, ek=ek: v.tensor_mul(
                    ekv[:], ekv[:], ek[:]), after=[n3])
                n4 = P.op("v", lambda v, ekv=ekv, sr=sr, c=c: v.tensor_mul(
                    RW[:, c, :], ekv[:], sr[:]), after=[n3b, e3])
                vt = n4
                slot_free[sl] = n4

            vsave = P.op("v", lambda v, s0=s0: v.tensor_copy(
                XPREV[:, :, :], XA[:, :, s0 + 511:s0 + 512]), after=[vt])
            vprev = vsave
            vt = vsave
            rw_all = P.tok("v")
            for c in range(NC):
                P.pump(1)
                bo = 3 + (c % 2)
                mmO = None
                for i in range(NC):
                    mmO = P.op("t", lambda t, c=c, i=i, bo=bo: t.matmul(
                        PS[bo][:, :], WO[:, i, c * 128:(c + 1) * 128], RW[:, i, :],
                        start=(i == 0), stop=(i == NC - 1)),
                        after=[rw_all, bank_free[bo]] if i == 0 else [])
                ev = P.op("v", lambda v, c=c, s0=s0, bo=bo: v.tensor_add(
                    XA[:, c, s0:s0 + 512], PS[bo][:, :], XA[:, c, s0:s0 + 512]),
                    after=[mmO, vt])
                vt = ev
                bank_free[bo] = ev



def stage_ln(ctx, which, toks):
    P, sb, es = ctx["P"], ctx["sb"], ctx["es"]
    XA, IDF, PS, STAT = ctx["XA"], ctx["IDF"], ctx["PS"], ctx["STAT"]
    out = ctx["out"]
    final = which == 4

    with ExitStack() as st_:
        TM = [sb(f"ln{which}_tm{i}", [128, 1024], F32, st_) for i in range(2)]
        SQJ = sb(f"ln{which}_sqj", [128, 1024], F32, st_)
        COLS = sb(f"ln{which}_cols", [128, 8], F32, st_)

        P.barrier()
        vt = None
        stt = None
        ttok = None
        tm_free = [None, None]
        fwd_bank_free = [None, None]
        bwd_bank_free = [None, None]
        dma_last = None
        for tt in range(NTT):
            P.pump(1)
            sl = tt % 2
            tmb = TM[sl]
            tps = []
            for c in range(NC):
                bank = 5 + c // 4
                cc = c % 4
                a = [tm_free[sl], fwd_bank_free[c // 4]] if c in (0, 4) else []
                tps.append(P.op("t", lambda t, c=c, tt=tt, bank=bank, cc=cc: t.transpose(
                    PS[bank][:, cc * 128:(cc + 1) * 128],
                    XA[:, c, tt * 128:(tt + 1) * 128], IDF[:]), after=a))
            ttok = tps[-1]
            e1 = P.op("s", lambda s, tmb=tmb: s.activation(
                tmb[:, 0:512], PS[5][:, :], AF.Identity), after=[tps[3], stt, tm_free[sl]])
            e2 = P.op("s", lambda s, tmb=tmb: s.activation(
                tmb[:, 512:1024], PS[6][:, :], AF.Identity), after=[tps[7], e1])
            stt = e2
            fwd_bank_free[0] = e1
            fwd_bank_free[1] = e2
            v1 = P.op("v", lambda v, tmb=tmb: v.reduce_sum(
                COLS[:, 0:1], tmb[:], axis=AXX), after=[e2, vt])
            v2 = P.op("s", lambda s, tmb=tmb: s.activation(
                SQJ[:], tmb[:], AF.Square, accum_out=COLS[:, 1:2]),
                after=[v1, stt])
            stt = v2
            v3 = P.op("v", lambda v: v.tensor_scalar(
                COLS[:, 2:3], COLS[:, 0:1], 1.0 / 1024, None, ALU.mult), after=[v2])
            v4 = P.op("v", lambda v: v.tensor_mul(
                COLS[:, 3:4], COLS[:, 2:3], COLS[:, 2:3]), after=[v3])
            v5 = P.op("v", lambda v: v.scalar_tensor_tensor(
                COLS[:, 4:5], COLS[:, 1:2], 1.0 / 1024, COLS[:, 3:4],
                ALU.mult, ALU.subtract), after=[v4])
            v6 = P.op("v", lambda v: v.tensor_scalar(
                COLS[:, 4:5], COLS[:, 4:5], 1e-5, None, ALU.add), after=[v5])
            v7 = P.op("v", lambda v: v.reciprocal(COLS[:, 5:6], COLS[:, 4:5]),
                      after=[v6])
            s1 = P.op("s", lambda s: s.activation(
                COLS[:, 6:7], COLS[:, 5:6], AF.Sqrt), after=[v7, stt])
            stt = s1
            v8 = P.op("v", lambda v, tmb=tmb: v.tensor_scalar(
                tmb[:], tmb[:], COLS[:, 2:3], COLS[:, 6:7],
                ALU.subtract, ALU.mult), after=[s1, vt])
            vt = v8
            if which == 2:
                stt = P.op("s", lambda s, tmb=tmb, tt=tt: s.activation(
                    SQJ[:], tmb[:], AF.Square,
                    accum_out=STAT[:, 16 + tt:17 + tt]), after=[v8, stt])
                vt = v8
            if final:
                dma_last = P.dma("y", out[tt * 128:(tt + 1) * 128, :], tmb[:],
                                 after=[vt], new=True)
                tm_free[sl] = dma_last
            else:
                bts = []
                for c in range(NC):
                    bank = 3 + c // 4
                    cc = c % 4
                    a = [vt, bwd_bank_free[c // 4]] if c in (0, 4) else []
                    bts.append(P.op("t", lambda t, c=c, bank=bank, cc=cc, tmb=tmb: t.transpose(
                        PS[bank][:, cc * 128:(cc + 1) * 128],
                        tmb[:, c * 128:(c + 1) * 128], IDF[:]), after=a))
                ttok = bts[-1]
                ev = []
                for half in range(2):
                    bank = 3 + half
                    ev.append(P.op("s", lambda s, tt=tt, bank=bank, half=half: s.activation(
                        _xa_cols(XA, tt, half), PS[bank][:, :], AF.Identity),
                        after=[bts[3 + 4 * half], stt]))
                stt = ev[-1]
                bwd_bank_free[0] = ev[0]
                bwd_bank_free[1] = ev[1]
                tm_free[sl] = ev[-1]
        return dict(v=P.tok("v"), t=P.tok("t"), s=P.tok("s"),
                    out_dma=dma_last)


def _xa_cols(XA, tt, half):
    return XA[:, half * 4:(half + 1) * 4, tt * 128:(tt + 1) * 128]


def stage_l0ffn(ctx, toks, debug_stop=None):
    P, sb = ctx["P"], ctx["sb"]
    XA, SVEC, PS, din = ctx["XA"], ctx["SVEC"], ctx["PS"], ctx["din"]

    with ExitStack() as st_:
        H0 = sb("f_h0", [128, 32768], BF16, st_)
        H1X = sb("f_h1", [128, 32768], BF16, st_)
        WJ = [sb(f"f_wj{i}", [128, 16, 128], BF16, st_) for i in range(2)]
        G1 = sb("f_g1", [128, 1024], BF16, st_)
        SC = sb("f_sc", [128, 128], F32, st_)

        P.barrier()
        d_sc = P.dma("y", SC[:, 0:32], din["s1"], new=True)
        P.dma("y", SC[:, 32:64], din["s2"])
        P.dma("y", SC[:, 64:96], din["sh1"])
        d_sc2 = P.dma("y", SC[:, 96:128], din["sh2"])
        vt = None
        stt = None
        wj_free = [None, None]
        bank_free = [None] * 8
        s3 = SVEC[:, 40:48]
        wi = 0

        def wload(nm, j, half, after=()):
            nonlocal wi
            slot = wi % 2
            wb = WJ[slot]
            srcap = din[nm][j]
            if half is None:
                dst, s_ = wb[:, 0:8, :], srcap
            else:
                dst, s_ = wb[:, 0:16, :], srcap[:, half * 16:half * 16 + 16, :]
            tok = P.dma("g", dst, s_, after=[wj_free[slot]] + list(after), new=True)
            wi += 1
            return slot, tok

        for hb in range(2):
            t0 = hb * 1024
            XB = H1X[:, 0:8192].rearrange("p (c n) -> p c n", c=8)
            cast = None
            for c in range(NC):
                cast = P.op("v", lambda v, c=c, t0=t0: v.tensor_copy(
                    XB[:, c, :], XA[:, c, t0:t0 + 1024]),
                    after=[vt, toks.get("v"), toks.get("s")] if c == 0 else [])
            vt = cast
            xb_tok = cast

            for j in range(32):
                P.pump(1)
                slot1, dtok1 = wload("w1R", j, None)
                slot2, dtok2 = wload("w2R", j, None)
                wb1, wb2 = WJ[slot1], WJ[slot2]
                mmlast = None
                for wsel in range(2):
                    wb = (wb1, wb2)[wsel]
                    dtok = (dtok1, dtok2)[wsel]
                    for i in range(NC):
                        for s2 in range(2):
                            bank = wsel * 4 + (j % 2) * 2 + s2
                            mmlast = P.op("t", lambda t, wb=wb, i=i, bank=bank, s2=s2:
                                          t.matmul(
                                PS[bank][:, :], wb[:, i, :],
                                XB[:, i, s2 * 512:(s2 + 1) * 512],
                                start=(i == 0), stop=(i == NC - 1)),
                                after=[dtok, xb_tok, bank_free[bank]]
                                if i == 0 and s2 == 0 else
                                ([bank_free[bank]] if i == 0 else []))
                for s2 in range(2):
                    scj = j
                    bank = (j % 2) * 2 + s2
                    evs = P.op("s", lambda s, scj=scj, bank=bank, s2=s2: s.activation(
                        G1[:, s2 * 512:(s2 + 1) * 512], PS[bank][:, :], AF.Silu,
                        scale=SC[:, scj:scj + 1]), after=[mmlast, stt, vt, d_sc])
                    stt = evs
                    bank_free[bank] = evs
                for s2 in range(2):
                    scj = 32 + j
                    bank = 4 + (j % 2) * 2 + s2
                    off = s2 * 512
                    evv = P.op("v", lambda v, scj=scj, j=j, off=off, bank=bank:
                               v.scalar_tensor_tensor(
                        H0[:, j * 1024 + off:j * 1024 + off + 512],
                        PS[bank][:, :], SC[:, scj:scj + 1],
                        G1[:, off:off + 512], ALU.mult, ALU.mult),
                        after=[mmlast, stt, vt])
                    vt = evv
                    bank_free[bank] = evv
                wj_free[slot1] = mmlast
                wj_free[slot2] = mmlast
            h0_done_v = vt

            for phase, nm, scoff, SRC, DST in (
                    ("B", "h1R", 64, H0, H1X), ("C", "h2R", 96, H1X, H0)):
                for j in range(32):
                    P.pump(1)
                    stok = [None, None]
                    wbs = [None, None]
                    for half in range(2):
                        wbs[half], stok[half] = wload(nm, j, half)
                    mm = None
                    for ihalf in range(2):
                        wb = WJ[wbs[ihalf]]
                        for i2 in range(16):
                            i = ihalf * 16 + i2
                            for s2 in range(2):
                                off = s2 * 512
                                bank = 4 + (j % 2) * 2 + s2
                                mm = P.op("t", lambda t, wb=wb, i2=i2, i=i, bank=bank,
                                          off=off, SRC=SRC: t.matmul(
                                    PS[bank][:, :], wb[:, i2, :],
                                    SRC[:, i * 1024 + off:i * 1024 + off + 512],
                                    start=(i == 0), stop=(i == 31)),
                                    after=[stok[0], stok[1], bank_free[bank], h0_done_v]
                                    if i == 0 and s2 == 0 else
                                    ([bank_free[bank]] if i == 0 else []))
                    for s2 in range(2):
                        off = s2 * 512
                        bank = 4 + (j % 2) * 2 + s2
                        ev = P.op("s", lambda s, j=j, off=off, bank=bank, scoff=scoff,
                                  DST=DST: s.activation(
                            DST[:, j * 1024 + off:j * 1024 + off + 512],
                            PS[bank][:, :], AF.Gelu, scale=SC[:, scoff + j:scoff + j + 1]),
                            after=[mm, stt, d_sc2])
                        stt = ev
                        bank_free[bank] = ev
                    wj_free[wbs[0]] = mm
                    wj_free[wbs[1]] = mm
                h0_done_v = stt
            hc_done = stt

            for j in range(NC):
                P.pump(1)
                stok = [None, None]
                wbs = [None, None]
                for half in range(2):
                    wbs[half], stok[half] = wload("w3R", j, half)
                mm = None
                for ihalf in range(2):
                    wb = WJ[wbs[ihalf]]
                    for i2 in range(16):
                        i = ihalf * 16 + i2
                        for s2 in range(2):
                            off = s2 * 512
                            bank = (j % 2) * 2 + s2
                            mm = P.op("t", lambda t, wb=wb, i2=i2, i=i, bank=bank,
                                      off=off: t.matmul(
                                PS[bank][:, :], wb[:, i2, :],
                                H0[:, i * 1024 + off:i * 1024 + off + 512],
                                start=(i == 0), stop=(i == 31)),
                                after=[stok[0], stok[1], bank_free[bank], hc_done]
                                if i == 0 and s2 == 0 else
                                ([bank_free[bank]] if i == 0 else []))
                for s2 in range(2):
                    off = s2 * 512
                    bank = (j % 2) * 2 + s2
                    ev = P.op("v", lambda v, j=j, off=off, t0=t0, bank=bank:
                              v.scalar_tensor_tensor(
                        XA[:, j, t0 + off:t0 + off + 512], PS[bank][:, :],
                        s3[:, j:j + 1], XA[:, j, t0 + off:t0 + off + 512],
                        ALU.mult, ALU.add), after=[mm, vt])
                    vt = ev
                    bank_free[bank] = ev
                wj_free[wbs[0]] = mm
                wj_free[wbs[1]] = mm
        return dict(v=P.tok("v"), t=P.tok("t"), s=P.tok("s"))



def stage_l1attn(ctx, toks):
    P, sb = ctx["P"], ctx["sb"]
    XA, IDF, TRIU, PS, STAT, SCL = (ctx["XA"], ctx["IDF"], ctx["TRIU"],
                                    ctx["PS"], ctx["STAT"], ctx["SCL"])
    din, qscr, SVEC, IDB = ctx["din"], ctx["qscr"], ctx["SVEC"], ctx["IDB"]
    PSB = ctx["PSB"]

    with ExitStack() as st_:
        S = sb("a_S", [128, 8, 512], F32, st_)
        X2 = sb("a_x2", [128, 2, 1024], BF16, st_)
        QSz = sb("a_qz", [128, 1024], BF16, st_)
        QSi = sb("a_qi", [128, 1024], mybir.dt.int32, st_)
        QSt = sb("a_qt", [128, 1024], BF16, st_)
        QSd = sb("a_qd", [128, 1024], BF16, st_)
        Q2z = sb("a_q2z", [128, 1024], BF16, st_)
        Q2i = sb("a_q2i", [128, 1024], mybir.dt.int32, st_)
        Q2t = sb("a_q2t", [128, 1024], BF16, st_)
        Q2d = sb("a_q2d", [128, 1024], BF16, st_)
        CL2 = sb("a_cl2", [128, 16], F32, st_)
        XQF = sb("a_xqf", [128, 8, 256], BF16, st_)
        Q = sb("a_q", [128, 8, 256], F32, st_)
        K = sb("a_k", [128, 8, 256], BF16, st_)
        GK = sb("a_gk", [128, 8, 256], BF16, st_)
        WF = sb("a_wf", [128, 8, 256], F32, st_)
        M0 = sb("a_m0", [128, 8, 256], BF16, st_)
        V = sb("a_v", [128, 2, 2048], F32, st_)
        SG = sb("a_sg", [128, 2, 2048], BF16, st_)
        GT = sb("a_gt", [128, 8, 129], F32, st_)
        DC = sb("a_dc", [128, 8], F32, st_)
        QG = sb("a_qg", [128, 2, 128], F32, st_)
        ET = sb("a_et", [128, 2, 128], F32, st_)
        KP = sb("a_kp", [128, 2, 128], F32, st_)
        KG = sb("a_kg", [128, 2, 128], F32, st_)
        KGt = sb("a_kgt", [128, 2, 128], F32, st_)
        A2 = sb("a_a2", [128, 128], F32, st_)
        OG = sb("a_og", [128, 2048], BF16, st_)
        OQF = sb("a_oqf", [128, 16, 128], BF16, st_)
        NB = sb("a_nb", [128, 1024], BF16, st_)
        NOB = sb("a_nob", [128, 2048], BF16, st_)
        NGK2 = sb("a_ngk2", [128, 16], BF16, st_)
        G1T = sb("a_g1t", [128, 2, 16], F32, st_)
        X2Q = sb("a_x2q", [16, 256], BF16, st_)
        WGK1 = sb("a_wgk1", [128, 8, 16], BF16, st_)
        WGK2 = sb("a_wgk2", [16, 1024], BF16, st_)
        WL = sb("a_wl", [128, 8192], BF16, st_)
        CL = sb("a_cl", [128, 16], F32, st_)

        P.barrier()
        d_ngk2 = P.dma("g", NGK2[:], din["ngk2B"], new=True)
        d_nob = P.dma("g", NOB[:], din["noB"], new=True)
        d_wgk1 = P.dma("y", WGK1[:, :, :],
                       qscr["gk1wT"].rearrange("(c p) n -> p c n", p=128), new=True)
        d_wgk2 = P.dma("y", WGK2[:, :], qscr["gk2wT"], new=True)

        vz = P.op("v", lambda v: v.memset(S[:], 0.0))
        vz2 = P.op("v", lambda v: v.memset(STAT[:, 62:63], 0.0), after=[vz])
        vt = vz2
        gt = None
        nb_free2 = [None, None]
        zb_free = [None, None]
        stt = None
        bank_free = [None] * 8
        wl_free = None
        x2_free = None
        sgate = SVEC[:, 48:56]

        NPROJ = [("q", "qwT", "nqB", 1024), ("k", "kwT", "nkB", 1024),
                 ("v", "vwT", "nvB", 2048), ("g", "gwT", "ngB", 2048),
                 ("gk1", "gk1wT", "ngk1B", 16)]

        for blk_i in range(8):
            t0 = blk_i * 256
            for tt in range(2):
                tps = []
                for c in range(NC):
                    bank = 5 + c // 4
                    cc = c % 4
                    a = [x2_free, bank_free[bank]] if (c % 4 == 0) else []
                    tps.append(P.op("t", lambda t, tt=tt, c=c, t0=t0, bank=bank, cc=cc:
                                    t.transpose(PS[bank][:, cc * 128:(cc + 1) * 128],
                                    XA[:, c, t0 + tt * 128:t0 + (tt + 1) * 128], IDF[:]),
                                    after=a))
                for half in range(2):
                    bank = 5 + half
                    ev = P.op("v", lambda v, tt=tt, half=half, bank=bank: v.tensor_copy(
                        X2[:, tt, half * 512:(half + 1) * 512], PS[bank][:, :]),
                        after=[tps[half * 4 + 3], vt])
                    vt = ev
                    bank_free[bank] = ev
            x2tok = vt
            c1 = P.op("v", lambda v, blk_i=blk_i: v.tensor_scalar(
                CL[:, 0:2], STAT[:, 16 + 2 * blk_i:18 + 2 * blk_i],
                1.0 / 1024, 1e-8, ALU.mult, ALU.add), after=[vt])
            c2 = P.op("v", lambda v: v.reciprocal(CL[:, 2:4], CL[:, 0:2]), after=[c1])
            c3 = P.op("s", lambda s: s.activation(CL[:, 0:2], CL[:, 2:4], AF.Sqrt),
                      after=[c2, stt])
            stt = c3

            for pi, (pnm, wnm, nnm, dout) in enumerate(NPROJ):
                d_nb = P.dma("g", NB[:, 0:1024], din[nnm], new=True,
                             after=[nb_free2[0], nb_free2[1]])
                qtoks = []
                half_toks = [None, None]
                for tt in range(2):
                    en = "v"
                    Zb, Ib, Tb, Db = (QSz, QSi, QSt, QSd) if tt == 0 else (Q2z, Q2i, Q2t, Q2d)
                    CLx = CL if tt == 0 else CL2
                    p1 = P.op(en, lambda v, tt=tt, Zb=Zb: v.tensor_mul(
                        Zb[:], X2[:, tt, :], NB[:, 0:1024]),
                        after=[d_nb, x2tok, vt, zb_free[tt]])
                    nb_free2[tt] = p1
                    p2 = P.op("v", lambda v, Zb=Zb, CLx=CLx: v.reduce_max(
                        CLx[:, 4:5], Zb[:], axis=AXX, apply_absolute_value=True),
                        after=[p1])
                    p3 = P.op("v", lambda v, tt=tt, CLx=CLx: v.tensor_scalar(
                        CLx[:, 5:6], CLx[:, 4:5], CL[:, tt:tt + 1], 1e-5,
                        ALU.mult, ALU.max), after=[p2, c3])
                    p4 = P.op("v", lambda v, CLx=CLx: v.reciprocal(
                        CLx[:, 6:7], CLx[:, 5:6]), after=[p3])
                    p5 = P.op("v", lambda v, tt=tt, CLx=CLx: v.scalar_tensor_tensor(
                        CLx[:, 7:8], CLx[:, 6:7], 127.0, CL[:, tt:tt + 1],
                        ALU.mult, ALU.mult), after=[p4])
                    p6 = P.op("v", lambda v, CLx=CLx: v.tensor_scalar(
                        CLx[:, 8:9], CLx[:, 5:6], 1.0 / 127, None, ALU.mult),
                        after=[p5])
                    vt = p6
                    q1 = P.op("s", lambda s, Zb=Zb, CLx=CLx: s.activation(
                        Zb[:], Zb[:], AF.Identity, scale=CLx[:, 7:8]),
                        after=[p6, stt])
                    o1 = P.op("s", lambda s, Zb=Zb, Ib=Ib: s.activation(
                        Ib[:], Zb[:], AF.Identity), after=[q1])
                    o2 = P.op("s", lambda s, Ib=Ib, Tb=Tb: s.activation(
                        Tb[:], Ib[:], AF.Identity), after=[o1])
                    stt = o2
                    half_toks[tt] = o2
                for tt in range(2):
                    Zb, Ib, Tb, Db = (QSz, QSi, QSt, QSd) if tt == 0 else (Q2z, Q2i, Q2t, Q2d)
                    CLx = CL if tt == 0 else CL2
                    o3 = P.op("v", lambda v, Zb=Zb, Tb=Tb, Db=Db: v.tensor_sub(
                        Db[:], Zb[:], Tb[:]), after=[half_toks[tt], vt])
                    o4 = P.op("v", lambda v, Tb=Tb, Db=Db: v.scalar_tensor_tensor(
                        Tb[:], Db[:], 0.5, Tb[:], ALU.is_ge, ALU.add), after=[o3])
                    o5 = P.op("v", lambda v, Tb=Tb, Db=Db: v.scalar_tensor_tensor(
                        Tb[:], Db[:], -0.5, Tb[:], ALU.is_gt, ALU.add), after=[o4])
                    q7 = P.op("v", lambda v, Zb=Zb, Tb=Tb, CLx=CLx: v.tensor_scalar(
                        Zb[:], Tb[:], 1.0, CLx[:, 8:9], ALU.subtract, ALU.mult),
                        after=[o5])
                    vt = q7
                    xtp = []
                    for c in range(NC):
                        bank = 5 + c // 4
                        cc = c % 4
                        a = [q7, bank_free[bank]] if (c % 4 == 0) else []
                        xtp.append(P.op("t", lambda t, c=c, bank=bank, cc=cc, Zb=Zb:
                                        t.transpose(
                            PSB[bank][:, cc * 128:(cc + 1) * 128],
                            Zb[:, c * 128:(c + 1) * 128], IDB[:]), after=a))
                    zb_free[tt] = xtp[-1]
                    for half in range(2):
                        bank = 5 + half
                        ev = P.op("v", lambda v, tt=tt, half=half, bank=bank: v.tensor_copy(
                            XQF[:, half * 4:(half + 1) * 4, tt * 128:(tt + 1) * 128],
                            PSB[bank][:, 0:512].rearrange("p (c n) -> p c n", c=4)),
                            after=[xtp[half * 4 + 3], vt])
                        vt = ev
                        bank_free[bank] = ev
                    qtoks.append(vt)
                xq_tok = vt

                if pnm == "gk1":
                    mms = []
                    for tt in range(2):
                        bank = 2
                        mm = None
                        for i in range(NC):
                            mm = P.op("t", lambda t, i=i, tt=tt, bank=bank: t.matmul(
                                PS[bank][:, 0:16], XQF[:, i, tt * 128:(tt + 1) * 128],
                                WGK1[:, i, :], start=(i == 0), stop=(i == NC - 1)),
                                after=[d_wgk1, xq_tok, bank_free[bank]] if i == 0 else [])
                        ev = P.op("v", lambda v, tt=tt, bank=bank: v.tensor_scalar(
                            G1T[:, tt, :], PS[bank][:, 0:16],
                            SCL[:, QWI["gk1wT"]:QWI["gk1wT"] + 1], None, ALU.mult),
                            after=[mm, vt])
                        vt = ev
                        bank_free[bank] = ev
                    continue
                wq = qscr[wnm]
                scl_c = SCL[:, QWI[wnm]:QWI[wnm] + 1]
                for half in range(dout // 1024):
                    WLv = WL[:, 0:8192].rearrange("p (c n) -> p c n", c=8)
                    d_w = P.dma("y", WLv[:, :, :], wq[:, half, :, :],
                                after=[wl_free], new=True)
                    if pnm in ("q", "k"):
                        DSTB = Q if pnm == "q" else K
                        for j in range(NC):
                            bank = j % 2
                            mm = None
                            for i in range(NC):
                                mm = P.op("t", lambda t, i=i, j=j, bank=bank: t.matmul(
                                    PS[bank][:, 0:256], WLv[:, i, j * 128:(j + 1) * 128],
                                    XQF[:, i, :],
                                    start=(i == 0), stop=(i == NC - 1)),
                                    after=[d_w, xq_tok, bank_free[bank]] if i == 0 else [])
                            ev = P.op("v", lambda v, j=j, bank=bank, DSTB=DSTB,
                                      scl_c=scl_c: v.tensor_scalar(
                                DSTB[:, j, :], PS[bank][:, 0:256], scl_c, None,
                                ALU.mult), after=[mm, vt])
                            vt = ev
                            bank_free[bank] = ev
                            wl_free = mm
                    else:
                        for tt in range(2):
                            for ds in range(2):
                                bank = ds % 2
                                mm = None
                                for i in range(NC):
                                    mm = P.op("t", lambda t, i=i, tt=tt, ds=ds, bank=bank:
                                              t.matmul(
                                        PS[bank][:, :],
                                        XQF[:, i, tt * 128:(tt + 1) * 128],
                                        WLv[:, i, ds * 512:(ds + 1) * 512],
                                        start=(i == 0), stop=(i == NC - 1)),
                                        after=[d_w, xq_tok, bank_free[bank]] if i == 0 else [])
                                off = half * 1024 + ds * 512
                                if pnm == "v":
                                    ev = P.op("v", lambda v, tt=tt, off=off, bank=bank,
                                              scl_c=scl_c: v.tensor_scalar(
                                        V[:, tt, off:off + 512], PS[bank][:, :],
                                        scl_c, None, ALU.mult), after=[mm, vt])
                                    vt = ev
                                else:
                                    ev = P.op("s", lambda s, tt=tt, off=off, bank=bank,
                                              scl_c=scl_c: s.activation(
                                        SG[:, tt, off:off + 512], PS[bank][:, :],
                                        AF.Silu, scale=scl_c), after=[mm, stt])
                                    stt = ev
                                bank_free[bank] = ev
                                wl_free = mm

            for tt in range(2):
                h1 = P.op("s", lambda s, tt=tt: s.activation(
                    WF[:, 0, 0:16], G1T[:, tt, :], AF.Square,
                    accum_out=CL[:, 9:10]), after=[vt, stt])
                stt = h1
                h2 = P.op("v", lambda v: v.tensor_scalar(
                    CL[:, 10:11], CL[:, 9:10], 1.0 / 16, 1e-8, ALU.mult, ALU.add),
                    after=[h1])
                h3 = P.op("v", lambda v: v.reciprocal(CL[:, 11:12], CL[:, 10:11]),
                          after=[h2])
                h4 = P.op("s", lambda s: s.activation(CL[:, 10:11], CL[:, 11:12],
                                                      AF.Sqrt), after=[h3, stt])
                stt = h4
                r1 = P.op("v", lambda v, tt=tt: v.tensor_mul(
                    QSz[:, 0:16], G1T[:, tt, :], NGK2[:]),
                    after=[h4, d_ngk2, vt, zb_free[0]])
                r2 = P.op("v", lambda v: v.reduce_max(
                    CL[:, 4:5], QSz[:, 0:16], axis=AXX, apply_absolute_value=True),
                    after=[r1])
                r3 = P.op("v", lambda v: v.tensor_scalar(
                    CL[:, 5:6], CL[:, 4:5], CL[:, 10:11], 1e-5, ALU.mult, ALU.max),
                    after=[r2])
                r4 = P.op("v", lambda v: v.reciprocal(CL[:, 6:7], CL[:, 5:6]),
                          after=[r3])
                r5 = P.op("v", lambda v: v.scalar_tensor_tensor(
                    CL[:, 7:8], CL[:, 6:7], 127.0, CL[:, 10:11], ALU.mult, ALU.mult),
                    after=[r4])
                r6 = P.op("v", lambda v: v.tensor_scalar(
                    CL[:, 8:9], CL[:, 5:6], 1.0 / 127, None, ALU.mult), after=[r5])
                u1 = P.op("v", lambda v: v.tensor_single_scalar(
                    QSz[:, 0:16], QSz[:, 0:16], CL[:, 7:8], ALU.mult), after=[r6])
                u2 = P.op("v", lambda v: v.tensor_copy(QSi[:, 0:16], QSz[:, 0:16]),
                          after=[u1])
                u3 = P.op("v", lambda v: v.tensor_copy(QSt[:, 0:16], QSi[:, 0:16]),
                          after=[u2])
                u4 = P.op("v", lambda v: v.tensor_sub(
                    QSd[:, 0:16], QSz[:, 0:16], QSt[:, 0:16]), after=[u3])
                u5 = P.op("v", lambda v: v.scalar_tensor_tensor(
                    QSt[:, 0:16], QSd[:, 0:16], 0.5, QSt[:, 0:16], ALU.is_ge,
                    ALU.add), after=[u4])
                u6 = P.op("v", lambda v: v.scalar_tensor_tensor(
                    QSt[:, 0:16], QSd[:, 0:16], -0.5, QSt[:, 0:16], ALU.is_gt,
                    ALU.add), after=[u5])
                u7 = P.op("v", lambda v: v.tensor_scalar(
                    QSz[:, 0:16], QSt[:, 0:16], 1.0, CL[:, 8:9], ALU.subtract,
                    ALU.mult), after=[u6])
                vt = u7
                tp = P.op("t", lambda t, tt=tt: t.transpose(
                    PSB[2][0:16, 0:128], QSz[:, 0:16], IDB[:]), after=[u7, bank_free[2]])
                zb_free[0] = tp
                ev = P.op("v", lambda v, tt=tt: v.tensor_copy(
                    X2Q[:, tt * 128:(tt + 1) * 128], PSB[2][0:16, 0:128]),
                    after=[tp, vt])
                vt = ev
                bank_free[2] = ev
            for j in range(NC):
                bank = 2 + (j % 2)
                mm = P.op("t", lambda t, j=j, bank=bank: t.matmul(
                    PS[bank][:, 0:256], WGK2[:, j * 128:(j + 1) * 128],
                    X2Q[:], start=True, stop=True),
                    after=[d_wgk2, vt, bank_free[bank]])
                ev = P.op("s", lambda s, j=j, bank=bank: s.activation(
                    WF[:, j, :], PS[bank][:, 0:256], AF.Identity,
                    scale=SCL[:, QWI["gk2wT"]:QWI["gk2wT"] + 1],
                    bias=sgate[:, j:j + 1]), after=[mm, stt])
                stt = ev
                bank_free[bank] = ev
            l1 = P.op("v", lambda v: v.tensor_scalar(
                M0[:], WF[:], 0.0, 1.0 / 16, ALU.min, ALU.mult), after=[stt, vt])
            l2 = P.op("s", lambda s: s.activation(WF[:], WF[:], AF.Abs),
                      after=[l1, stt])
            l3 = P.op("s", lambda s: s.activation(WF[:], WF[:], AF.Exp, scale=-1.0),
                      after=[l2])
            l4 = P.op("s", lambda s: s.activation(WF[:], WF[:], AF.Ln, bias=1.0),
                      after=[l3])
            l5 = P.op("v", lambda v: v.scalar_tensor_tensor(
                GK[:], WF[:], -1.0 / 16, M0[:], ALU.mult, ALU.add), after=[l4])
            vt = l5

            for cn in range(2):
                z0 = P.op("v", lambda v: v.memset(GT[:, :, 0:1], 0.0), after=[vt])
                sc = []
                for c2 in range(NC):
                    sc.append(P.op("v", lambda v, c2=c2, cn=cn: v.tensor_tensor_scan(
                        GT[:, c2, 1:129], GK[:, c2, cn * 128:(cn + 1) * 128],
                        STAT[:, 62:63].broadcast_to([128, 128]), 0.0,
                        ALU.add, ALU.add), after=[z0 if c2 == 0 else sc[-1], l5]))
                vt = sc[-1]
                dc1 = P.op("s", lambda s: s.activation(
                    DC[:, :], GT[:, :, 128:129].rearrange("p c n -> p (c n)"),
                    AF.Exp), after=[vt, stt])
                stt = dc1
                for h in range(4):
                    f1 = P.op("s", lambda s, h=h: s.activation(
                        QG[:, :, :], GT[:, 2 * h:2 * h + 2, 1:129], AF.Exp),
                        after=[vt, stt, bank_free[3]])
                    f2 = P.op("s", lambda s, h=h: s.activation(
                        ET[:, :, :], GT[:, 2 * h:2 * h + 2, 1:129], AF.Exp,
                        scale=-1.0), after=[f1])
                    stt = f2
                    f3 = P.op("v", lambda v, h=h, cn=cn: v.scalar_tensor_tensor(
                        QG[:, :, :], QG[:, :, :], 1.0 / 16,
                        Q[:, 2 * h:2 * h + 2, cn * 128:(cn + 1) * 128],
                        ALU.mult, ALU.mult), after=[f1, vt])
                    f4 = P.op("v", lambda v, h=h, cn=cn: v.tensor_mul(
                        KP[:, :, :], ET[:, :, :],
                        K[:, 2 * h:2 * h + 2, cn * 128:(cn + 1) * 128]),
                        after=[f2, f3])
                    f5 = []
                    for j in range(2):
                        f5.append(P.op("v", lambda v, h=h, j=j: v.tensor_scalar(
                            ET[:, j, :], GT[:, 2 * h + j, 1:129],
                            GT[:, 2 * h + j, 128:129], None, ALU.subtract),
                            after=[f4]))
                    f6 = P.op("s", lambda s: s.activation(
                        ET[:, :, :], ET[:, :, :], AF.Exp, scale=-1.0),
                        after=[f5[0], f5[1], stt])
                    stt = f6
                    f7 = P.op("v", lambda v, h=h, cn=cn: v.tensor_mul(
                        KG[:, :, :], ET[:, :, :],
                        K[:, 2 * h:2 * h + 2, cn * 128:(cn + 1) * 128]),
                        after=[f6])
                    vt = f7
                    amm = None
                    for j in range(2):
                        amm = P.op("t", lambda t, j=j: t.matmul(
                            PS[2][:, 0:128], KP[:, j, :], QG[:, j, :],
                            start=(j == 0), stop=(j == 1)),
                            after=[f7, bank_free[2]] if j == 0 else [])
                    aev = P.op("v", lambda v: v.tensor_mul(
                        A2[:], PS[2][:, 0:128], TRIU[:]), after=[amm, vt])
                    vt = aev
                    bank_free[2] = aev
                    ktp = []
                    for j in range(2):
                        ktp.append(P.op("t", lambda t, j=j: t.transpose(
                            PS[4][:, j * 128:(j + 1) * 128], KG[:, j, :], IDF[:]),
                            after=[f7, bank_free[4]] if j == 0 else []))
                    kev = P.op("v", lambda v: v.tensor_copy(
                        KGt[:, :, :], PS[4][:, 0:256].rearrange("p (c n) -> p c n", c=2)),
                        after=[ktp[1], vt])
                    vt = kev
                    bank_free[4] = kev
                    omm = None
                    for j in range(2):
                        omm = P.op("t", lambda t, h=h, j=j: t.matmul(
                            PS[3][:, :], QG[:, j, :], S[:, 2 * h + j, :],
                            start=(j == 0), stop=False),
                            after=[f3, bank_free[3], vz] if j == 0 else [])
                    omm = P.op("t", lambda t, h=h, cn=cn: t.matmul(
                        PS[3][:, :], A2[:], V[:, cn, h * 512:(h + 1) * 512],
                        start=False, stop=True), after=[aev])
                    dmm = []
                    for j in range(2):
                        dmm.append(P.op("t", lambda t, h=h, cn=cn, j=j: t.matmul(
                            PS[6 + j][:, :], KGt[:, j, :],
                            V[:, cn, h * 512:(h + 1) * 512],
                            start=True, stop=True),
                            after=[kev, bank_free[6 + j]]))
                    for j in range(2):
                        su = P.op("v", lambda v, h=h, j=j: v.scalar_tensor_tensor(
                            S[:, 2 * h + j, :], S[:, 2 * h + j, :],
                            DC[:, 2 * h + j:2 * h + j + 1], PS[6 + j][:, :],
                            ALU.mult, ALU.add), after=[dmm[j], omm, dc1, vt])
                        vt = su
                        bank_free[6 + j] = su
                    oev = P.op("v", lambda v, h=h, cn=cn: v.tensor_mul(
                        OG[:, h * 512:(h + 1) * 512], PS[3][:, :],
                        SG[:, cn, h * 512:(h + 1) * 512]), after=[omm, vt])
                    vt = oev
                    bank_free[3] = oev
                og_tok = vt

                WFf = WF[:, :, :].rearrange("p c n -> p (c n)")
                m1 = P.op("s", lambda s, WFf=WFf: s.activation(
                    WFf[:, 0:1024], OG[:, 0:1024], AF.Square,
                    accum_out=CL[:, 9:10]), after=[og_tok, stt])
                m2 = P.op("s", lambda s, WFf=WFf: s.activation(
                    WFf[:, 1024:2048], OG[:, 1024:2048], AF.Square,
                    accum_out=CL[:, 10:11]), after=[m1])
                stt = m2
                m3 = P.op("v", lambda v: v.tensor_add(
                    CL[:, 9:10], CL[:, 9:10], CL[:, 10:11]), after=[m2, vt])
                m4 = P.op("v", lambda v: v.tensor_scalar(
                    CL[:, 10:11], CL[:, 9:10], 1.0 / 2048, 1e-8, ALU.mult, ALU.add),
                    after=[m3])
                m5 = P.op("v", lambda v: v.reciprocal(CL[:, 11:12], CL[:, 10:11]),
                          after=[m4])
                m6 = P.op("s", lambda s: s.activation(CL[:, 10:11], CL[:, 11:12],
                                                      AF.Sqrt), after=[m5, stt])
                stt = m6
                hv = []
                for half in range(2):
                    w1 = P.op("v", lambda v, half=half: v.tensor_mul(
                        QSz[:], OG[:, half * 1024:(half + 1) * 1024],
                        NOB[:, half * 1024:(half + 1) * 1024]), after=[m6, vt, d_nob])
                    w2 = P.op("v", lambda v, half=half: v.reduce_max(
                        CL[:, 4 + half:5 + half], QSz[:], axis=AXX,
                        apply_absolute_value=True), after=[w1])
                    hv.append(w2)
                    vt = w2
                w3 = P.op("v", lambda v: v.tensor_max(
                    CL[:, 4:5], CL[:, 4:5], CL[:, 5:6]), after=[hv[0], hv[1]])
                w4 = P.op("v", lambda v: v.tensor_scalar(
                    CL[:, 5:6], CL[:, 4:5], CL[:, 10:11], 1e-5, ALU.mult, ALU.max),
                    after=[w3])
                w5 = P.op("v", lambda v: v.reciprocal(CL[:, 6:7], CL[:, 5:6]),
                          after=[w4])
                w6 = P.op("v", lambda v: v.scalar_tensor_tensor(
                    CL[:, 7:8], CL[:, 6:7], 127.0, CL[:, 10:11], ALU.mult, ALU.mult),
                    after=[w5])
                w7 = P.op("v", lambda v: v.tensor_scalar(
                    CL[:, 8:9], CL[:, 5:6], 1.0 / 127, None, ALU.mult), after=[w6])
                vt = w7
                ytoks = [None, None]
                for half in range(2):
                    Zb, Ib, Tb, Db = (QSz, QSi, QSt, QSd) if half == 0 else (Q2z, Q2i, Q2t, Q2d)
                    y1 = P.op("v", lambda v, half=half, Zb=Zb: v.tensor_mul(
                        Zb[:], OG[:, half * 1024:(half + 1) * 1024],
                        NOB[:, half * 1024:(half + 1) * 1024]),
                        after=[vt, w7, zb_free[half]])
                    vt = y1
                    y2 = P.op("s", lambda s, Zb=Zb: s.activation(
                        Zb[:], Zb[:], AF.Identity, scale=CL[:, 7:8]),
                        after=[y1, stt])
                    o1 = P.op("s", lambda s, Zb=Zb, Ib=Ib: s.activation(
                        Ib[:], Zb[:], AF.Identity), after=[y2])
                    o2 = P.op("s", lambda s, Ib=Ib, Tb=Tb: s.activation(
                        Tb[:], Ib[:], AF.Identity), after=[o1])
                    stt = o2
                    ytoks[half] = o2
                for half in range(2):
                    Zb, Ib, Tb, Db = (QSz, QSi, QSt, QSd) if half == 0 else (Q2z, Q2i, Q2t, Q2d)
                    o3 = P.op("v", lambda v, Zb=Zb, Tb=Tb, Db=Db: v.tensor_sub(
                        Db[:], Zb[:], Tb[:]), after=[ytoks[half], vt])
                    o4 = P.op("v", lambda v, Tb=Tb, Db=Db: v.scalar_tensor_tensor(
                        Tb[:], Db[:], 0.5, Tb[:], ALU.is_ge, ALU.add), after=[o3])
                    o5 = P.op("v", lambda v, Tb=Tb, Db=Db: v.scalar_tensor_tensor(
                        Tb[:], Db[:], -0.5, Tb[:], ALU.is_gt, ALU.add), after=[o4])
                    y8 = P.op("v", lambda v, Zb=Zb, Tb=Tb: v.tensor_scalar(
                        Zb[:], Tb[:], 1.0, CL[:, 8:9], ALU.subtract, ALU.mult),
                        after=[o5])
                    vt = y8
                    otp = []
                    Zb2 = QSz if half == 0 else Q2z
                    for c in range(NC):
                        bank = 5 + c // 4
                        cc = c % 4
                        a = [y8, bank_free[bank]] if (c % 4 == 0) else []
                        otp.append(P.op("t", lambda t, c=c, bank=bank, cc=cc, Zb2=Zb2:
                                        t.transpose(
                            PSB[bank][:, cc * 128:(cc + 1) * 128],
                            Zb2[:, c * 128:(c + 1) * 128], IDB[:]), after=a))
                    zb_free[half] = otp[-1]
                    for hh in range(2):
                        bank = 5 + hh
                        ev = P.op("v", lambda v, half=half, hh=hh, bank=bank: v.tensor_copy(
                            OQF[:, half * 8 + hh * 4:half * 8 + (hh + 1) * 4, :],
                            PSB[bank][:, 0:512].rearrange("p (c n) -> p c n", c=4)),
                            after=[otp[hh * 4 + 3], vt])
                        vt = ev
                        bank_free[bank] = ev
                oq_tok = vt

                WLv = WL[:, 0:8192].rearrange("p (c n) -> p c n", c=8)
                mm = None
                for rnd in range(2):
                    d_wo = P.dma("y", WLv[:, :, :],
                                 qscr["owT"][:, 0, rnd * 8:(rnd + 1) * 8, :],
                                 after=[wl_free, mm], new=True)
                    for jj in range(NC):
                        bank = jj // 4
                        cc = jj % 4
                        for i in range(NC):
                            mm = P.op("t", lambda t, i=i, jj=jj, rnd=rnd, bank=bank,
                                      cc=cc: t.matmul(
                                PS[bank][:, cc * 128:(cc + 1) * 128],
                                WLv[:, i, jj * 128:(jj + 1) * 128],
                                OQF[:, rnd * 8 + i, :],
                                start=(rnd == 0 and i == 0 and jj % 4 == 0),
                                stop=(rnd == 1 and i == 7 and jj % 4 == 3)),
                                after=[d_wo, oq_tok, bank_free[bank]]
                                if (jj % 4 == 0 and i == 0) else [])
                    wl_free = mm
                for jj in range(NC):
                    bank = jj // 4
                    cc = jj % 4
                    ev = P.op("v", lambda v, jj=jj, cn=cn, t0=t0, bank=bank, cc=cc:
                              v.scalar_tensor_tensor(
                        XA[:, jj, t0 + cn * 128:t0 + (cn + 1) * 128],
                        PS[bank][:, cc * 128:(cc + 1) * 128],
                        SCL[:, QWI["owT"]:QWI["owT"] + 1],
                        XA[:, jj, t0 + cn * 128:t0 + (cn + 1) * 128],
                        ALU.mult, ALU.add), after=[mm, vt])
                    vt = ev
                bank_free[0] = vt
                bank_free[1] = vt
            x2_free = vt
        return dict(v=P.tok("v"), t=P.tok("t"), s=P.tok("s"))



def _round_fix(P, en, Zb, Ib, Tb, Db, M, prev, sprev=None):
    o1 = P.op("s", lambda s, Zb=Zb, Ib=Ib: s.activation(Ib[:], Zb[:], AF.Identity),
              after=[prev, sprev])
    o2 = P.op("s", lambda s, Ib=Ib, Tb=Tb: s.activation(Tb[:], Ib[:], AF.Identity),
              after=[o1])
    o3 = P.op("v", lambda v, Zb=Zb, Tb=Tb, Db=Db: v.tensor_sub(Db[:], Zb[:], Tb[:]),
              after=[o2, prev])
    o4 = P.op("v", lambda v, Tb=Tb, Db=Db: v.scalar_tensor_tensor(
        Tb[:], Db[:], 0.5, Tb[:], ALU.is_ge, ALU.add), after=[o3])
    o5 = P.op("v", lambda v, Tb=Tb, Db=Db: v.scalar_tensor_tensor(
        Tb[:], Db[:], -0.5, Tb[:], ALU.is_gt, ALU.add), after=[o4])
    return o5, o2


def _act_quant_fm(P, flat, UB, WB, QSV, QSG, vt, gt, ready=None, stt=None):
    piece = 1024
    NPC = 16384 // piece
    s1tok = [None] * NPC
    sets = [QSV, QSG]
    emitted2 = 0

    def stage1(pc, vt, stt):
        o0 = pc * piece
        QI, QT, QD, QM = sets[pc % 2]
        q1 = P.op("v", lambda v, o0=o0: v.tensor_mul(
            flat[:, o0:o0 + piece], flat[:, o0:o0 + piece], UB[:, 0:piece]),
            after=[vt, ready if pc < 2 else None])
        c1 = P.op("s", lambda s, o0=o0, QI=QI: s.activation(
            QI[:, 0:piece], flat[:, o0:o0 + piece], AF.Identity), after=[q1, stt])
        c2 = P.op("s", lambda s, QI=QI, QT=QT: s.activation(
            QT[:, 0:piece], QI[:, 0:piece], AF.Identity), after=[c1])
        s1tok[pc] = c2
        return q1, c2

    def stage2(pc, vt):
        o0 = pc * piece
        QI, QT, QD, QM = sets[pc % 2]
        o3 = P.op("v", lambda v, o0=o0, QT=QT, QD=QD: v.tensor_sub(
            QD[:, 0:piece], flat[:, o0:o0 + piece], QT[:, 0:piece]),
            after=[s1tok[pc], vt])
        o4 = P.op("v", lambda v, QT=QT, QD=QD: v.scalar_tensor_tensor(
            QT[:, 0:piece], QD[:, 0:piece], 0.5, QT[:, 0:piece], ALU.is_ge,
            ALU.add), after=[o3])
        o5 = P.op("v", lambda v, QT=QT, QD=QD: v.scalar_tensor_tensor(
            QT[:, 0:piece], QD[:, 0:piece], -0.5, QT[:, 0:piece], ALU.is_gt,
            ALU.add), after=[o4])
        q7 = P.op("v", lambda v, QT=QT: v.tensor_scalar(
            QT[:, 0:piece], QT[:, 0:piece], 1.0, None, ALU.subtract), after=[o5])
        q8 = P.op("v", lambda v, o0=o0, QT=QT: v.tensor_mul(
            flat[:, o0:o0 + piece], QT[:, 0:piece], WB[:, 0:piece]), after=[q7])
        return q8

    vtc, sttc = vt, stt
    for pc in range(NPC):
        _, sttc = stage1(pc, vtc, sttc)
        if pc >= 1:
            vtc = stage2(pc - 1, vtc)
    vtc = stage2(NPC - 1, vtc)
    return vtc, gt, sttc


def _l1f_x3_s2(P, bufs, tt, s1toks, vt, zb_free, bank_free, XQ3, PSB, IDB):
    QZ, QI, QT, QD, Q2z, Q2i, Q2t, Q2d, CL, CL2 = bufs
    Zb, Ib, Tb, Db = (QZ, QI, QT, QD) if tt % 2 == 0 else (Q2z, Q2i, Q2t, Q2d)
    CLx = CL if tt % 2 == 0 else CL2
    o3 = P.op("v", lambda v, Zb=Zb, Tb=Tb, Db=Db: v.tensor_sub(
        Db[:], Zb[:], Tb[:]), after=[s1toks[tt], vt])
    o4 = P.op("v", lambda v, Tb=Tb, Db=Db: v.scalar_tensor_tensor(
        Tb[:], Db[:], 0.5, Tb[:], ALU.is_ge, ALU.add), after=[o3])
    o5 = P.op("v", lambda v, Tb=Tb, Db=Db: v.scalar_tensor_tensor(
        Tb[:], Db[:], -0.5, Tb[:], ALU.is_gt, ALU.add), after=[o4])
    b7 = P.op("v", lambda v, Zb=Zb, Tb=Tb, CLx=CLx: v.tensor_scalar(
        Zb[:], Tb[:], 1.0, CLx[:, 4:5], ALU.subtract, ALU.mult), after=[o5])
    vt = b7
    xtp = []
    for c in range(NC):
        bank = 5 + c // 4
        cc = c % 4
        a = [b7, bank_free[bank]] if (c % 4 == 0) else []
        xtp.append(P.op("t", lambda t, c=c, bank=bank, cc=cc, Zb=Zb: t.transpose(
            PSB[bank][:, cc * 128:(cc + 1) * 128],
            Zb[:, c * 128:(c + 1) * 128], IDB[:]), after=a))
    zb_free[tt % 2] = xtp[-1]
    for half in range(2):
        bank = 5 + half
        ev = P.op("v", lambda v, tt=tt, half=half, bank=bank: v.tensor_copy(
            XQ3[:, half * 4:(half + 1) * 4, tt * 128:(tt + 1) * 128],
            PSB[bank][:, 0:512].rearrange("p (c n) -> p c n", c=4)),
            after=[xtp[half * 4 + 3], vt])
        vt = ev
        bank_free[bank] = ev
    return vt


def stage_l1ffn(ctx, toks):
    P, sb = ctx["P"], ctx["sb"]
    XA, IDF, IDB, PS, STAT, SCL = (ctx["XA"], ctx["IDF"], ctx["IDB"],
                                   ctx["PS"], ctx["STAT"], ctx["SCL"])
    din, qscr, PSB = ctx["din"], ctx["qscr"], ctx["PSB"]

    with ExitStack() as st_:
        XQ3 = sb("b_xq3", [128, 8, 512], BF16, st_)
        H0 = sb("b_h0", [128, 32, 514], BF16, st_)
        HC = sb("b_hc", [128, 32, 512], BF16, st_)
        TL = sb("b_tl", [128, 32, 2], BF16, st_)
        FWJ = [sb(f"b_wj{i}", [128, 16, 128], BF16, st_) for i in range(3)]
        QZ = sb("b_qz", [128, 1024], BF16, st_)
        QI = sb("b_qi", [128, 1024], mybir.dt.int32, st_)
        QT = sb("b_qt", [128, 1024], BF16, st_)
        QD = sb("b_qd", [128, 1024], BF16, st_)
        Q2z = sb("b_q2z", [128, 1024], BF16, st_)
        Q2i = sb("b_q2i", [128, 1024], mybir.dt.int32, st_)
        Q2t = sb("b_q2t", [128, 1024], BF16, st_)
        Q2d = sb("b_q2d", [128, 1024], BF16, st_)
        CL2 = sb("b_cl2", [128, 16], F32, st_)
        MSKF = [sb(f"b_msk{i}", [128, 1024], BF16, st_) for i in range(2)]
        UB = sb("b_ub", [128, 1024], BF16, st_)
        WB = sb("b_wb", [128, 1024], BF16, st_)
        CWC = sb("b_cw", [128, 32, 3], F32, st_)
        CBC = sb("b_cb", [128, 32], F32, st_)
        CL = sb("b_cl", [128, 16], F32, st_)
        MX = sb("b_mx", [128, 8], F32, st_)
        UR = sb("b_ur", [1, 512], F32, st_)
        WR = sb("b_wr", [1, 512], F32, st_)
        ONE1 = sb("b_one", [1, 128], F32, st_)

        ctx_l1f = (QZ, QI, QT, QD, Q2z, Q2i, Q2t, Q2d, CL, CL2)
        P.barrier()
        d_cw = P.dma("y", CWC[:], din["cw"], new=True)
        d_cb = P.dma("y", CBC[:], din["cb"])
        o1 = P.op("v", lambda v: v.memset(ONE1[:], 1.0))
        vt = o1
        gt = None
        stt = None
        zb_free = [None, None]
        wj_free = [None, None, None]
        bank_free = [None] * 8
        wi = 0
        sc1 = SCL[:, QWI["f1T"]:QWI["f1T"] + 1]
        sc2 = SCL[:, QWI["f2T"]:QWI["f2T"] + 1]

        def fwload(nm, j, half, after=()):
            nonlocal wi
            slot = wi % 3
            if half is None:
                dst, s_ = FWJ[slot][:, 0:8, :], qscr[nm][j]
            else:
                dst = FWJ[slot][:, 0:16, :]
                s_ = qscr[nm][j][:, half * 16:half * 16 + 16, :]
            tok = P.dma("y", dst, s_, after=[wj_free[slot]] + list(after), new=True)
            wi += 1
            return slot, tok

        for q in range(4):
            t0 = q * 512
            if q > 0:
                tl1 = P.op("v", lambda v: v.tensor_copy(TL[:], H0[:, :, 512:514]),
                           after=[vt])
                vt = tl1
            else:
                tl1 = None
            s1toks = [None] * 4
            for tt in range(4):
                Zb, Ib, Tb, Db = (QZ, QI, QT, QD) if tt % 2 == 0 else (Q2z, Q2i, Q2t, Q2d)
                CLx = CL if tt % 2 == 0 else CL2
                tps = []
                for c in range(NC):
                    bank = 5 + c // 4
                    cc = c % 4
                    a = [bank_free[bank]] if (c % 4 == 0) else []
                    tps.append(P.op("t", lambda t, tt=tt, c=c, t0=t0, bank=bank, cc=cc:
                                    t.transpose(PS[bank][:, cc * 128:(cc + 1) * 128],
                                    XA[:, c, t0 + tt * 128:t0 + (tt + 1) * 128],
                                    IDF[:]), after=a + ([toks.get("v"), toks.get("s")]
                                    if q == 0 and tt == 0 and c == 0 else [])))
                e1 = P.op("v", lambda v, Zb=Zb: v.tensor_copy(
                    Zb[:, 0:512], PS[5][:, :]), after=[tps[3], vt, zb_free[tt % 2]])
                e2 = P.op("v", lambda v, Zb=Zb: v.tensor_copy(
                    Zb[:, 512:1024], PS[6][:, :]), after=[tps[7], e1])
                vt = e2
                bank_free[5] = e1
                bank_free[6] = e2
                a1_ = P.op("v", lambda v, Zb=Zb, CLx=CLx: v.reduce_max(
                    CLx[:, 0:1], Zb[:], axis=AXX, apply_absolute_value=True),
                    after=[e2])
                a2 = P.op("v", lambda v, CLx=CLx: v.tensor_scalar(
                    CLx[:, 1:2], CLx[:, 0:1], 1e-5, None, ALU.max), after=[a1_])
                a3 = P.op("v", lambda v, CLx=CLx: v.reciprocal(CLx[:, 2:3], CLx[:, 1:2]),
                          after=[a2])
                a4 = P.op("v", lambda v, CLx=CLx: v.tensor_scalar(
                    CLx[:, 3:4], CLx[:, 2:3], 127.0, None, ALU.mult), after=[a3])
                a5 = P.op("v", lambda v, CLx=CLx: v.tensor_scalar(
                    CLx[:, 4:5], CLx[:, 1:2], 1.0 / 127, None, ALU.mult), after=[a4])
                vt = a5
                b1 = P.op("s", lambda s, Zb=Zb, CLx=CLx: s.activation(
                    Zb[:], Zb[:], AF.Identity, scale=CLx[:, 3:4]),
                    after=[a5, stt])
                c1_ = P.op("s", lambda s, Zb=Zb, Ib=Ib: s.activation(
                    Ib[:], Zb[:], AF.Identity), after=[b1])
                c2_ = P.op("s", lambda s, Ib=Ib, Tb=Tb: s.activation(
                    Tb[:], Ib[:], AF.Identity), after=[c1_])
                stt = c2_
                s1toks[tt] = c2_
                if tt >= 1:
                    vt = _l1f_x3_s2(P, ctx_l1f, tt - 1, s1toks, vt, zb_free,
                                    bank_free, XQ3, PSB, IDB)
            vt = _l1f_x3_s2(P, ctx_l1f, 3, s1toks, vt, zb_free, bank_free, XQ3, PSB, IDB)
            xq_tok = vt
            for j in range(32):
                P.pump(1)
                slot, dtok = fwload("f1T", j, None)
                bank = j % 2
                mm = None
                for i in range(NC):
                    mm = P.op("t", lambda t, slot=slot, i=i, j=j, bank=bank: t.matmul(
                        PS[bank][:, :], FWJ[slot][:, i, :], XQ3[:, i, :],
                        start=(i == 0), stop=(i == NC - 1)),
                        after=[dtok, xq_tok, bank_free[bank]] if i == 0 else [])
                ev = P.op("s", lambda s, j=j, bank=bank: s.activation(
                    H0[:, j, 2:514], PS[bank][:, :], AF.Identity, scale=sc1),
                    after=[mm, stt, tl1])
                stt = ev
                bank_free[bank] = ev
                wj_free[slot] = mm
            h0_tok = stt
            if q == 0:
                tz = P.op("v", lambda v: v.memset(H0[:, :, 0:2], 0.0), after=[vt, h0_tok])
            else:
                tz = P.op("v", lambda v: v.tensor_copy(H0[:, :, 0:2], TL[:]),
                          after=[vt, h0_tok, tl1])
            vt = tz

            for j in range(32):
                c1 = P.op("v", lambda v, j=j: v.tensor_single_scalar(
                    QZ[:, 0:512], H0[:, j, 0:512], CWC[:, j, 0:1], ALU.mult),
                    after=[tz, vt, stt, d_cw, h0_tok, zb_free[0]])
                c2 = P.op("v", lambda v, j=j: v.scalar_tensor_tensor(
                    QZ[:, 0:512], H0[:, j, 1:513], CWC[:, j, 1:2], QZ[:, 0:512],
                    ALU.mult, ALU.add), after=[c1])
                c3 = P.op("v", lambda v, j=j: v.scalar_tensor_tensor(
                    QZ[:, 0:512], H0[:, j, 2:514], CWC[:, j, 2:3], QZ[:, 0:512],
                    ALU.mult, ALU.add), after=[c2])
                c4 = P.op("s", lambda s, j=j: s.activation(
                    HC[:, j, :], QZ[:, 0:512], AF.Silu, bias=CBC[:, j:j + 1]),
                    after=[c3, stt, d_cb])
                stt = c4
                vt = c3
            hc_tok = stt

            wlo = 1 if q == 0 else 0
            mzero = P.op("v", lambda v: v.memset(MX[:, 0:4], 0.0), after=[vt])
            vt = mzero
            for tt in range(4):
                for grp in range(8):
                    tps = []
                    for cc in range(4):
                        c = grp * 4 + cc
                        bank = 5
                        a = [hc_tok, bank_free[5]] if cc == 0 else []
                        tps.append(P.op("t", lambda t, c=c, tt=tt, cc=cc: t.transpose(
                            PSB[5][:, cc * 128:(cc + 1) * 128], HC[:, c, tt * 128:(tt + 1) * 128],
                            IDB[:]), after=a))
                    r1 = P.op("v", lambda v: v.reduce_max(
                        CL[:, 0:1], PSB[5][:, 0:512], axis=AXX, apply_absolute_value=True),
                        after=[tps[3], vt])
                    r2 = P.op("v", lambda v, tt=tt: v.tensor_max(
                        MX[:, tt:tt + 1], MX[:, tt:tt + 1], CL[:, 0:1]), after=[r1])
                    vt = r2
                    bank_free[5] = r1
            s1_ = P.op("v", lambda v: v.tensor_scalar(
                MX[:, 4:8], MX[:, 0:4], 1e-5, None, ALU.max), after=[vt])
            s2_ = P.op("v", lambda v: v.reciprocal(MX[:, 0:4], MX[:, 4:8]), after=[s1_])
            s3_ = P.op("v", lambda v: v.tensor_scalar(
                MX[:, 0:4], MX[:, 0:4], 127.0, None, ALU.mult), after=[s2_])
            s4_ = P.op("v", lambda v: v.tensor_scalar(
                MX[:, 4:8], MX[:, 4:8], 1.0 / 127, None, ALU.mult), after=[s3_])
            vt = s4_
            for which, COLS, ROW in ((0, MX[:, 0:4], UR), (1, MX[:, 4:8], WR)):
                for tt in range(4):
                    tpu = P.op("t", lambda t, COLS=COLS, tt=tt: t.transpose(
                        PS[6][0:1, 0:128], COLS[:, tt:tt + 1], IDF[:]),
                        after=[vt, s4_, bank_free[6]])
                    evu = P.op("v", lambda v, ROW=ROW, tt=tt: v.tensor_copy(
                        ROW[:, tt * 128:(tt + 1) * 128], PS[6][0:1, 0:128]),
                        after=[tpu, vt])
                    vt = evu
                    bank_free[6] = evu
                bmm = P.op("t", lambda t, ROW=ROW: t.matmul(
                    PS[6][:, :], ONE1[:], ROW[:, 0:512],
                    start=True, stop=True), after=[vt, bank_free[6], o1])
                bev1 = P.op("v", lambda v, which=which: v.tensor_copy(
                    (UB if which == 0 else WB)[:, 0:512], PS[6][:, :]),
                    after=[bmm, vt])
                bev = P.op("v", lambda v, which=which: v.tensor_copy(
                    (UB if which == 0 else WB)[:, 512:1024], PS[6][:, :]),
                    after=[bev1])
                vt = bev
                bank_free[6] = bev
            vt, gt, stt = _act_quant_fm(P, HC[:, :, :].rearrange("p c n -> p (c n)"),
                                        UB, WB, (QI, QT, QD, MSKF[0]),
                                        (Q2i, Q2t, Q2d, MSKF[1]), vt, gt,
                                        ready=vt, stt=stt)
            hq_tok = (vt, gt)

            for j in range(NC):
                P.pump(1)
                stoks = [None, None]
                slots = [None, None]
                for half in range(2):
                    slots[half], stoks[half] = fwload("f2T", j, half)
                bank = 2 + (j % 2)
                mm = None
                for ihalf in range(2):
                    for i2 in range(16):
                        i = ihalf * 16 + i2
                        mm = P.op("t", lambda t, s_=slots[ihalf], i2=i2, i=i, bank=bank:
                                  t.matmul(
                            PS[bank][:, :], FWJ[s_][:, i2, :],
                            HC[:, i, :], start=(i == 0), stop=(i == 31)),
                            after=[stoks[0], stoks[1], hq_tok[0], hq_tok[1],
                                   bank_free[bank]] if i == 0 else [])
                ev = P.op("v", lambda v, j=j, t0=t0, wlo=wlo, bank=bank:
                          v.scalar_tensor_tensor(
                    XA[:, j, t0 - 1 + wlo:t0 + 511], PS[bank][:, wlo:512], sc2,
                    XA[:, j, t0 - 1 + wlo:t0 + 511], ALU.mult, ALU.add),
                    after=[mm, vt])
                vt = ev
                bank_free[bank] = ev
                wj_free[slots[0]] = mm
                wj_free[slots[1]] = mm

        ee1 = P.op("v", lambda v: v.tensor_mul(
            QZ[:, 0:32], H0[:, :, 512:513].rearrange("p c n -> p (c n)"),
            CWC[:, :, 0]), after=[vt])
        ee2 = P.op("v", lambda v: v.tensor_mul(
            QZ[:, 32:64], H0[:, :, 513:514].rearrange("p c n -> p (c n)"),
            CWC[:, :, 1]), after=[ee1])
        ee3 = P.op("v", lambda v: v.tensor_add(QZ[:, 0:32], QZ[:, 0:32], QZ[:, 32:64]),
                   after=[ee2])
        ee3b = P.op("v", lambda v: v.tensor_add(QZ[:, 0:32], QZ[:, 0:32], CBC[:]),
                    after=[ee3])
        ee4 = P.op("s", lambda s: s.activation(
            QZ[:, 0:32], QZ[:, 0:32], AF.Silu), after=[ee3b, stt])
        stt = ee4
        et1 = P.op("t", lambda t: t.transpose(PSB[5][0:32, 0:128], QZ[:, 0:32], IDB[:]),
                   after=[ee4, bank_free[5]])
        et2 = P.op("v", lambda v: v.reduce_max(
            CL[0:32, 0:1], PSB[5][0:32, 0:128], axis=AXX, apply_absolute_value=True),
            after=[et1, vt])
        et3 = P.op("t", lambda t: t.transpose(PS[6][0:1, 0:32], CL[0:32, 0:1],
                                              IDF[0:32, 0:32]), after=[et2, bank_free[6]])
        et4 = P.op("v", lambda v: v.reduce_max(
            CL[0:1, 1:2], PS[6][0:1, 0:32], axis=AXX), after=[et3])
        et5 = P.op("v", lambda v: v.tensor_scalar(
            CL[0:1, 2:3], CL[0:1, 1:2], 1e-5, None, ALU.max), after=[et4])
        et6 = P.op("v", lambda v: v.reciprocal(CL[0:1, 3:4], CL[0:1, 2:3]), after=[et5])
        et7 = P.op("v", lambda v: v.tensor_scalar(
            CL[0:1, 4:5], CL[0:1, 3:4], 127.0, None, ALU.mult), after=[et6])
        et8 = P.op("v", lambda v: v.tensor_scalar(
            CL[0:1, 5:6], CL[0:1, 2:3], 1.0 / 127, None, ALU.mult), after=[et7])
        eb1 = P.op("t", lambda t: t.matmul(PS[6][:, 0:2], ONE1[0:1, :],
                                           CL[0:1, 4:6], start=True, stop=True),
                   after=[et8, bank_free[6]])
        eb2 = P.op("v", lambda v: v.tensor_copy(CL[:, 8:10], PS[6][:, 0:2]),
                   after=[eb1])
        bank_free[6] = eb2
        bank_free[5] = et2
        f1 = P.op("v", lambda v: v.tensor_single_scalar(
            QZ[:, 0:32], QZ[:, 0:32], CL[:, 8:9], ALU.mult), after=[eb2])
        f2 = P.op("v", lambda v: v.tensor_copy(QI[:, 0:32], QZ[:, 0:32]), after=[f1])
        f3 = P.op("v", lambda v: v.tensor_copy(QT[:, 0:32], QI[:, 0:32]), after=[f2])
        f4 = P.op("v", lambda v: v.tensor_sub(QD[:, 0:32], QZ[:, 0:32], QT[:, 0:32]),
                  after=[f3])
        f5 = P.op("v", lambda v: v.scalar_tensor_tensor(
            QT[:, 0:32], QD[:, 0:32], 0.5, QT[:, 0:32], ALU.is_ge, ALU.add), after=[f4])
        f6 = P.op("v", lambda v: v.scalar_tensor_tensor(
            QT[:, 0:32], QD[:, 0:32], -0.5, QT[:, 0:32], ALU.is_gt, ALU.add), after=[f5])
        f7 = P.op("v", lambda v: v.tensor_scalar(
            QZ[:, 0:32], QT[:, 0:32], 1.0, CL[:, 9:10], ALU.subtract, ALU.mult),
            after=[f6])
        vt = f7
        stoks = [None, None]
        slots = [None, None]
        for half in range(2):
            slots[half], stoks[half] = fwload("f2T", 0, half)
        for j in range(NC):
            if j > 0:
                for half in range(2):
                    slots[half], stoks[half] = fwload("f2T", j, half)
            mm = None
            for ihalf in range(2):
                for i2 in range(16):
                    i = ihalf * 16 + i2
                    mm = P.op("t", lambda t, s_=slots[ihalf], i2=i2, i=i, j=j: t.matmul(
                        PS[2][:, 0:1], FWJ[s_][:, i2, :], QZ[:, i:i + 1],
                        start=(i == 0), stop=(i == 31)),
                        after=[stoks[0], stoks[1], f7, bank_free[2]] if i == 0 else [])
            ev = P.op("v", lambda v, j=j: v.scalar_tensor_tensor(
                XA[:, j, 2047:2048], PS[2][:, 0:1], sc2, XA[:, j, 2047:2048],
                ALU.mult, ALU.add), after=[mm, vt])
            vt = ev
            bank_free[2] = ev
            wj_free[slots[0]] = mm
            wj_free[slots[1]] = mm
        return dict(v=P.tok("v"), t=P.tok("t"), s=P.tok("s"))


def stage_inputs(inputs):
    import ml_dtypes
    g = {k: np.asarray(v) for k, v in inputs.items()}
    sh = {}
    sh["idf"] = np.eye(128, dtype=np.float32)
    sh["idb"] = np.eye(128, dtype=np.float32).astype(ml_dtypes.bfloat16)
    m = np.zeros((128, 128), np.float32)
    m[np.triu_indices(128)] = 1.0
    sh["triu"] = m
    sh["ones"] = np.ones((128, 128), np.float32)
    sh["tmk"] = _colmajor(g["L0_attn_time_mix_k"])
    sh["tmv"] = _colmajor(g["L0_attn_time_mix_v"])
    sh["tmr"] = _colmajor(g["L0_attn_time_mix_r"])
    sh["td"] = _colmajor(g["L0_attn_time_decay"])
    sh["tf"] = _colmajor(g["L0_attn_time_first"])
    sh["wkT"] = _tr(g["L0_attn_Wk"]); sh["wvT"] = _tr(g["L0_attn_Wv"])
    sh["wrT"] = _tr(g["L0_attn_Wr"]); sh["woT"] = _tr(g["L0_attn_Wo"])
    sh["w1R"] = _jblk(g["L0_ffn_w1_w"]); sh["w2R"] = _jblk(g["L0_ffn_w2_w"])
    sh["h1R"] = _jblk(g["L0_ffn_h1_w"]); sh["h2R"] = _jblk(g["L0_ffn_h2_w"])
    sh["w3R"] = _jblk(g["L0_ffn_w3_w"])
    sh["s1"] = _colmajor(g["L0_ffn_w1_s"]); sh["s2"] = _colmajor(g["L0_ffn_w2_s"])
    sh["sh1"] = _colmajor(g["L0_ffn_h1_s"]); sh["sh2"] = _colmajor(g["L0_ffn_h2_s"])
    sh["s3"] = _colmajor(g["L0_ffn_w3_s"])
    sh["qwT"] = _tr(g["L1_attn_wq"]); sh["kwT"] = _tr(g["L1_attn_wk"])
    sh["vwT"] = _tr(g["L1_attn_wv"]); sh["gwT"] = _tr(g["L1_attn_wg"])
    sh["gk1wT"] = _tr(g["L1_attn_wgk1"]); sh["gk2wT"] = _tr(g["L1_attn_wgk2"])
    sh["owT"] = _tr(g["L1_attn_wo"])
    sh["f1T"] = _tr(g["L1_ffn_w1"]); sh["f2T"] = _tr(g["L1_ffn_w2"])
    sh["nqB"] = _bcast(g["L1_attn_nq"]); sh["nkB"] = _bcast(g["L1_attn_nk"])
    sh["nvB"] = _bcast(g["L1_attn_nv"]); sh["ngB"] = _bcast(g["L1_attn_ng"])
    sh["ngk1B"] = _bcast(g["L1_attn_ngk1"]); sh["noB"] = _bcast(g["L1_attn_no"])
    sh["ngk2c"] = np.ascontiguousarray(np.asarray(g["L1_attn_ngk2"], np.float32)[:, None])
    sh["ngk2B"] = _bcast(g["L1_attn_ngk2"])
    sh["bgk2c"] = _colmajor(g["L1_attn_bgk2"])
    sh["cw"] = np.ascontiguousarray(np.asarray(g["L1_ffn_conv_w"], np.float32)
                                    .reshape(DH, 3).reshape(32, 128, 3).transpose(1, 0, 2))
    sh["cb"] = _colmajor(g["L1_ffn_conv_b"])
    sh["cb32"] = sh["cb"]

    x = np.asarray(g["L0_attn_x"], np.float32)
    in_maps = []
    for b in range(B):
        im = dict(sh)
        im["xT"] = np.ascontiguousarray(x[b].T)
        in_maps.append(im)
    return in_maps


def kernel(**inputs):
    in_maps = stage_inputs(inputs)
    nc = build(upto="all", tap=None)
    res = run_bass_kernel_spmd(nc, in_maps, core_ids=list(range(B)))
    return np.stack([np.asarray(r["out"], np.float32) for r in res.results], axis=0)
